# version 15
# baseline (speedup 1.0000x reference)
import numpy as np
from contextlib import ExitStack

import concourse.bass as bass
import concourse.mybir as mybir
from concourse.bass_utils import run_bass_kernel_spmd

F32 = mybir.dt.float32
BF16 = mybir.dt.bfloat16
AF = mybir.ActivationFunctionType
ALU = mybir.AluOpType
AX = mybir.AxisListType

D = 1024
DC = 8
T = 2048
FF = 4096
FC = 32
NP = 8
QS = 512
NQS = T // QS
NTC = T // 128
EPS = 1e-5


class Sem:
    def __init__(self, es, nc, name):
        self.h = es.enter_context(nc.semaphore(name))
        self.n = 0

    def inc(self, instr, k=1):
        instr.then_inc(self.h, k)
        self.n += k
        return self.n

    def wait(self, eng, n=None):
        eng.wait_ge(self.h, self.n if n is None else n)


def build_program(debug=False):
    nc = bass.Bass("TRN2", target_bir_lowering=False)

    x_t = nc.dram_tensor("x_t", [D, T], F32, kind="ExternalInput").ap()
    wqT = nc.dram_tensor("wqT", [D, D], F32, kind="ExternalInput").ap()
    wkT = nc.dram_tensor("wkT", [D, D], F32, kind="ExternalInput").ap()
    wvT = nc.dram_tensor("wvT", [D, D], F32, kind="ExternalInput").ap()
    woT = nc.dram_tensor("woT", [D, D], F32, kind="ExternalInput").ap()
    wfcb = nc.dram_tensor("wfcb", [FC, DC, 128, 128], F32, kind="ExternalInput").ap()
    wpb = nc.dram_tensor("wpb", [DC, FC, 128, 128], F32, kind="ExternalInput").ap()
    brow = nc.dram_tensor("brow", [1, 4 * D], F32, kind="ExternalInput").ap()
    cst = nc.dram_tensor("cst", [128, 1], F32, kind="ExternalInput").ap()
    out_t = nc.dram_tensor("out_t", [D, T], F32, kind="ExternalOutput").ap()
    cc_in = nc.dram_tensor("cc_in", [2048, 2048], BF16, kind="Internal").ap()
    cc_out = nc.dram_tensor("cc_out", [2, 2048, 2048], BF16, kind="Internal").ap()
    attnv_dram = nc.dram_tensor("attnv_dram", [D, T], BF16, kind="Internal").ap()
    wfc16 = nc.dram_tensor("wfc16", [FC, DC, 128, 128], BF16, kind="Internal").ap()
    wp16 = nc.dram_tensor("wp16", [DC, FC, 128, 128], BF16, kind="Internal").ap()

    es = ExitStack()

    def sb(stack, name, shape, dt):
        return stack.enter_context(nc.sbuf_tensor(name, shape, dt))

    x_sb = sb(es, "x_sb", [128, DC, T], F32)
    ln_sb = sb(es, "ln_sb", [128, DC, T], BF16)
    ones_col = sb(es, "ones_col", [128, 1], BF16)
    ones_row = sb(es, "ones_row", [1, 512], BF16)
    masks = sb(es, "masks", [128, DC, 16], BF16)
    b_sb = sb(es, "b_sb", [1, D], BF16)
    eps_sb = sb(es, "eps_sb", [128, 1], F32)
    bneg_sb = sb(es, "bneg_sb", [128, 1], F32)
    rows_sb = sb(es, "rows_sb", [1, 2, 512], F32)
    ab_bf = sb(es, "ab_bf", [1, 1024], BF16)
    ones33 = sb(es, "ones33", [33, 64], BF16)
    s16_row = sb(es, "s16_row", [1, 128], BF16)
    m44_row = sb(es, "m44_row", [1, 128], BF16)
    nmx = sb(es, "nmx", [16, 1], F32)
    nmx2 = sb(es, "nmx2", [16, 1], F32)
    nmx3 = sb(es, "nmx3", [16, 1], F32)
    xb_sb = sb(es, "xb_sb", [128, 2, 512], BF16)
    xsq_sb = sb(es, "xsq_sb", [128, 2, 512], BF16)
    tmp_sb = sb(es, "tmp_sb", [128, 512], F32)
    brow_f = sb(es, "brow_f", [1, 16], F32)
    t32a = sb(es, "t32a", [32, 32], F32)
    t32b = sb(es, "t32b", [32, 32], F32)
    brow_b = sb(es, "brow_b", [1, 16], BF16)

    pAB = es.enter_context(nc.psum_tensor("pAB", [128, 1024], F32))
    pCD = es.enter_context(nc.psum_tensor("pCD", [128, 1024], F32))
    pR1 = es.enter_context(nc.psum_tensor("pR1", [128, 512], F32))
    pR2 = es.enter_context(nc.psum_tensor("pR2", [128, 512], F32))
    pR3 = es.enter_context(nc.psum_tensor("pR3", [128, 512], F32))
    pR4 = es.enter_context(nc.psum_tensor("pR4", [128, 512], F32))

    sIN = Sem(es, nc, "sIN")
    sW = Sem(es, nc, "sW")
    sT = Sem(es, nc, "sT")
    sV = Sem(es, nc, "sV")
    sS = Sem(es, nc, "sS")
    sSP = Sem(es, nc, "sSP")
    sCC = Sem(es, nc, "sCC")
    sKV = Sem(es, nc, "sKV")
    sOUT = Sem(es, nc, "sOUT")

    blk = es.enter_context(nc.Block())

    @blk.sync
    def _(sy):
        sIN.inc(sy.dma_start(x_sb[:], x_t.rearrange("(c p) t -> p c t", p=128)), 16)
        sIN.inc(sy.dma_start(eps_sb[:], cst), 16)

    @blk.vector
    def _(v):
        v.memset(ones_col[:], 1.0)
        v.memset(ones_row[:], 1.0)
        v.memset(ones33[0:1, :], 1.0)
        v.memset(ones33[32:33, :], 1.0)
        v.memset(s16_row[:], 0.125)
        v.memset(m44_row[:], -44.0)
        v.memset(masks[:], 0.0)
        for c in range(DC):
            v.memset(masks[0:64, c, 2 * c:2 * c + 1], 1.0)
        last = None
        for c in range(DC):
            last = v.memset(masks[64:128, c, 2 * c + 1:2 * c + 2], 1.0)
        sV.inc(last)

    cast_fc = {}
    cast_wp = {}

    @blk.gpsimd
    def _(g):
        for f in range(FC):
            cast_fc[f] = sW.inc(g.dma_start(wfc16[f], wfcb[f]), 16)
        for si in range(16):
            dc, fh = si // 2, si % 2
            cast_wp[si] = sW.inc(g.dma_start(wp16[dc, 16 * fh:16 * fh + 16],
                                             wpb[dc, 16 * fh:16 * fh + 16]), 16)

    def load_bias(r):
        @blk.gpsimd
        def _(g):
            sT.wait(g)
            sW.inc(g.dma_start(b_sb[:], brow[0:1, r * D:(r + 1) * D]), 16)

    rowA = rows_sb[0:1, 0, :]
    rowB = rows_sb[0:1, 1, :]
    rowC = tmp_sb[0:1, :]

    def emit_ln(qq):
        q0 = qq * 512
        T0, V0, S0 = sT.n, sV.n, sS.n
        ln_mm = {c: T0 + c + 1 for c in range(DC)}
        cp_done = {c: V0 + c + 1 for c in range(DC)}
        sq_done = {c: S0 + c + 1 for c in range(DC)}

        @blk.vector
        def _(v):
            if qq == 0:
                sIN.wait(v)
            if sOUT.n > 0:
                sOUT.wait(v)
            for c in range(DC):
                if c >= 2:
                    v.wait_ge(sT.h, ln_mm[c - 2])
                sV.inc(v.tensor_copy(xb_sb[:, c % 2, :], x_sb[:, c, q0:q0 + 512]))

        @blk.scalar
        def _(s):
            for c in range(DC):
                s.wait_ge(sV.h, cp_done[c])
                if c >= 2:
                    s.wait_ge(sT.h, ln_mm[c - 2])
                sS.inc(s.activation(xsq_sb[:, c % 2, :], xb_sb[:, c % 2, :], AF.Square))
            s.wait_ge(sV.h, V0 + DC + 1)
            sS.inc(s.activation(rowC, rowB, AF.Sqrt, bias=eps_sb[0:1, :]))

        @blk.tensor
        def _(t):
            for c in range(DC):
                t.wait_ge(sS.h, sq_done[c])
                if c == 0:
                    t.wait_ge(sV.h, V0)
                t.matmul(pR1[0:1, :], ones_col[:], xb_sb[:, c % 2, :], start=(c == 0), stop=(c == DC - 1))
                sT.inc(t.matmul(pR2[0:1, :], ones_col[:], xsq_sb[:, c % 2, :],
                                start=(c == 0), stop=(c == DC - 1)))
            t.wait_ge(sV.h, V0 + DC + 2)
            t.matmul(pAB[:, 0:512], ones_row[0:1, 0:128], ab_bf[0:1, 0:512], start=True, stop=True)
            sT.inc(t.matmul(pAB[:, 512:1024], ones_row[0:1, 0:128], ab_bf[0:1, 512:1024],
                            start=True, stop=True))

        @blk.vector
        def _(v):
            v.wait_ge(sT.h, T0 + DC)
            v.tensor_scalar_mul(rowA, pR1[0:1, :], 1.0 / D)
            v.tensor_tensor(rowB, rowA, rowA, ALU.mult)
            sV.inc(v.scalar_tensor_tensor(rowB, pR2[0:1, :], 1.0 / D, rowB, ALU.mult, ALU.subtract))
            v.wait_ge(sS.h, S0 + DC + 1)
            v.reciprocal(rowB, rowC)
            v.scalar_tensor_tensor(rowC, rowA, -1.0, rowB, ALU.mult, ALU.mult)
            v.tensor_copy(ab_bf[0:1, 0:512], rowB)
            sV.inc(v.tensor_copy(ab_bf[0:1, 512:1024], rowC))
            v.wait_ge(sT.h, T0 + DC + 1)
            last = None
            for c in range(DC):
                v.tensor_tensor(tmp_sb[:], x_sb[:, c, q0:q0 + 512], pAB[:, 0:512], ALU.mult)
                last = v.tensor_tensor(ln_sb[:, c, q0:q0 + 512], tmp_sb[:], pAB[:, 512:1024], ALU.add)
            sV.inc(last)

        assert sT.n == T0 + DC + 1
        assert sV.n == V0 + DC + 3
        assert sS.n == S0 + DC + 1

    def emit_ffn_era(tag):
        fes = ExitStack()
        wfc_ring = sb(fes, "wfc_ring" + tag, [128, 2, DC, 128], BF16)
        h_sb = sb(fes, "h_sb" + tag, [128, FC, 1024], BF16)
        wp_ring = sb(fes, "wp_ring" + tag, [128, 2, 16, 128], BF16)
        r_sb = sb(fes, "r_sb" + tag, [128, 2, 1024], F32)
        s_scan = sb(fes, "s_scan" + tag, [128, 512], F32)
        mem_scan = sb(fes, "mem_scan" + tag, [128, 512], F32)
        acc_scan = sb(fes, "acc_scan" + tag, [128, 512], F32)

        era_t0 = sT.n
        era_v0 = sV.n

        def yvb(dc, sub):
            r = dc % 4
            if r == 0:
                return pAB[:, sub * 512:sub * 512 + 512]
            if r == 1:
                return pCD[:, sub * 512:sub * 512 + 512]
            if r == 2:
                return (pR1 if sub == 0 else pR2)[:, :]
            return (pR3 if sub == 0 else pR4)[:, :]

        for qq in range(NQS):
            emit_ln(qq)

        for hh in range(2):
            q0 = hh * 1024
            T0, V0, S0, W0 = sT.n, sV.n, sS.n, sW.n
            u_cnt = {f: T0 + f + 1 for f in range(FC)}
            wp_mm = {si: T0 + FC + si + 1 for si in range(16)}
            h_cnt = {f: V0 + f + 1 for f in range(FC)}
            scan_cnt = {dc: V0 + FC + dc + 1 for dc in range(DC)}
            relu_cnt = {f: S0 + f + 1 for f in range(FC)}
            wfc_ld = {f: W0 + 16 * (f + 1) for f in range(FC)}
            wp_ld = {si: W0 + 16 * FC + 16 * (si + 1) for si in range(16)}

            @blk.sync
            def _(sy, hh=hh, T0=T0, u_cnt=u_cnt, wp_mm=wp_mm):
                for f in range(FC):
                    sy.wait_ge(sW.h, cast_fc[f])
                    if f >= 2:
                        sy.wait_ge(sT.h, u_cnt[f - 2])
                    elif hh == 1:
                        sy.wait_ge(sT.h, T0)
                    else:
                        if era_t0 > 0:
                            sy.wait_ge(sT.h, era_t0)
                        if era_v0 > 0:
                            sy.wait_ge(sV.h, era_v0)
                    sW.inc(sy.dma_start(wfc_ring[:, f % 2, :, :],
                                        wfc16[f].rearrange("k p m -> p k m")), 16)
                for si in range(16):
                    dc, fh = si // 2, si % 2
                    sy.wait_ge(sW.h, cast_wp[si])
                    if si >= 2:
                        sy.wait_ge(sT.h, wp_mm[si - 2])
                    elif hh == 1:
                        sy.wait_ge(sT.h, T0)
                    elif era_v0 > 0:
                        sy.wait_ge(sV.h, era_v0)
                    sW.inc(sy.dma_start(wp_ring[:, si % 2, :, :],
                                        wp16[dc, 16 * fh:16 * fh + 16].rearrange("f p m -> p f m")), 16)

            @blk.tensor
            def _(t, q0=q0, V0=V0, u_cnt=u_cnt, wp_mm=wp_mm, h_cnt=h_cnt,
                  scan_cnt=scan_cnt, relu_cnt=relu_cnt, wfc_ld=wfc_ld, wp_ld=wp_ld):
                for f in range(FC):
                    psu = pAB if f % 2 == 0 else pCD
                    t.wait_ge(sW.h, wfc_ld[f])
                    if f < 2:
                        t.wait_ge(sV.h, V0)
                    else:
                        t.wait_ge(sV.h, h_cnt[f - 2])
                    mm = None
                    for sub in range(2):
                        for k in range(DC):
                            mm = t.matmul(psu[:, sub * 512:sub * 512 + 512],
                                          wfc_ring[:, f % 2, k, :],
                                          ln_sb[:, k, q0 + sub * 512:q0 + sub * 512 + 512],
                                          start=(k == 0), stop=(k == DC - 1))
                    sT.inc(mm)
                for si in range(16):
                    dc, fh = si // 2, si % 2
                    t.wait_ge(sW.h, wp_ld[si])
                    if fh == 0:
                        if dc < 2:
                            t.wait_ge(sV.h, h_cnt[FC - 1])
                            t.wait_ge(sS.h, relu_cnt[FC - 1])
                        elif dc >= 4:
                            t.wait_ge(sV.h, scan_cnt[dc - 4])
                    mm = None
                    for sub in range(2):
                        ysl = yvb(dc, sub)
                        for fi in range(16):
                            f = 16 * fh + fi
                            mm = t.matmul(ysl, wp_ring[:, si % 2, fi, :],
                                          h_sb[:, f, sub * 512:sub * 512 + 512],
                                          start=(f == 0), stop=(f == FC - 1))
                    sT.inc(mm)

            @blk.scalar
            def _(s, u_cnt=u_cnt, h_cnt=h_cnt):
                for f in range(FC):
                    psu = pAB if f % 2 == 0 else pCD
                    s.wait_ge(sT.h, u_cnt[f])
                    if f >= 2:
                        s.wait_ge(sV.h, h_cnt[f - 2])
                    sS.inc(s.activation(r_sb[:, f % 2, :], psu[:], AF.Relu))

            def scan_ops(eng, y, xsl, ss, mem, acc, fused):
                eng.tensor_single_scalar(acc, y, 1.0, ALU.is_gt)
                if fused:
                    eng.scalar_tensor_tensor(mem, y, 1.9, acc, ALU.mult, ALU.subtract)
                else:
                    eng.tensor_scalar_mul(mem, y, 1.9)
                    eng.tensor_tensor(mem, mem, acc, ALU.subtract)
                for _step in range(3):
                    eng.tensor_single_scalar(ss, mem, 1.0, ALU.is_gt)
                    eng.tensor_tensor(acc, acc, ss, ALU.add)
                    if fused:
                        eng.scalar_tensor_tensor(mem, mem, 0.9, y, ALU.mult, ALU.add)
                    else:
                        eng.tensor_scalar_mul(mem, mem, 0.9)
                        eng.tensor_tensor(mem, mem, y, ALU.add)
                    eng.tensor_tensor(mem, mem, ss, ALU.subtract)
                eng.tensor_single_scalar(ss, mem, 1.0, ALU.is_gt)
                eng.tensor_tensor(acc, acc, ss, ALU.add)
                if fused:
                    return eng.scalar_tensor_tensor(xsl, acc, 0.1, xsl, ALU.mult, ALU.add)
                eng.tensor_scalar_mul(ss, acc, 0.1)
                return eng.tensor_tensor(xsl, xsl, ss, ALU.add)

            @blk.vector
            def _(v, q0=q0, relu_cnt=relu_cnt, wp_mm=wp_mm):
                for f in range(FC):
                    psu = pAB if f % 2 == 0 else pCD
                    v.wait_ge(sS.h, relu_cnt[f])
                    sV.inc(v.tensor_tensor(h_sb[:, f, :], psu[:], r_sb[:, f % 2, :], ALU.mult))
                for dc in range(DC):
                    v.wait_ge(sT.h, wp_mm[2 * dc + 1])
                    last = None
                    for sub in range(2):
                        y = yvb(dc, sub)
                        xsl = x_sb[:, dc, q0 + sub * 512:q0 + sub * 512 + 512]
                        last = scan_ops(v, y, xsl, s_scan[:], mem_scan[:], acc_scan[:], True)
                    sV.inc(last)

            assert sT.n == T0 + FC + 16
            assert sV.n == V0 + FC + DC
            assert sS.n == S0 + FC
            assert sW.n == W0 + 16 * (FC + 16)

        fes.close()

    emit_ffn_era("1")

    aes = ExitStack()
    q_sb = sb(aes, "q_sb", [128, DC, T], BF16)
    wq_sb = sb(aes, "wq_sb", [128, DC, D], BF16)
    bes = ExitStack()
    kstage = sb(bes, "kstage", [128, 4, 512], BF16)
    v_sb = sb(bes, "v_sb", [128, NTC, D], BF16)

    for qq in range(NQS):
        emit_ln(qq)

    def load_w(dram):
        @blk.gpsimd
        def _(g):
            sT.wait(g)
            for k in range(DC):
                sW.inc(g.dma_start(wq_sb[:, k, :], dram[128 * k:128 * k + 128, :]), 16)

    def emit_proj_q():
        T0, S0 = sT.n, sS.n
        evac = {i: S0 + i + 1 for i in range(DC * NQS)}
        mmc = {i: T0 + i + 1 for i in range(DC * NQS)}

        @blk.tensor
        def _(t):
            sW.wait(t)
            for i in range(DC * NQS):
                m, qn = divmod(i, NQS)
                psu = pAB if i % 2 == 0 else pCD
                if i < 2:
                    sV.wait(t)
                else:
                    t.wait_ge(sS.h, evac[i - 2])
                for k in range(DC):
                    t.matmul(psu[:, 0:512], wq_sb[:, k, 128 * m:128 * m + 128],
                             ln_sb[:, k, qn * 512:qn * 512 + 512], start=(k == 0), stop=False)
                sT.inc(t.matmul(psu[:, 0:512], b_sb[0:1, 128 * m:128 * m + 128],
                                ones_row[0:1, :], start=False, stop=True))

        @blk.scalar
        def _(s):
            for i in range(DC * NQS):
                m, qn = divmod(i, NQS)
                psu = pAB if i % 2 == 0 else pCD
                s.wait_ge(sT.h, mmc[i])
                sS.inc(s.copy(q_sb[:, m, qn * 512:qn * 512 + 512], psu[:, 0:512]))

    load_w(wqT)
    load_bias(0)
    emit_proj_q()

    load_w(wkT)
    load_bias(1)

    def emit_proj_k():
        T0, S0, V0, SP0 = sT.n, sS.n, sV.n, sSP.n
        kred = {qn: V0 + qn + 1 for qn in range(NQS)}
        kevac = {i: S0 + 2 * i + 1 for i in range(DC * NQS)}
        ksq = {i: S0 + 2 * i + 2 for i in range(DC * NQS)}
        proj_mm = {i: T0 + 2 * i + 1 for i in range(DC * NQS)}
        knorm_mm = {i: T0 + 2 * i + 2 for i in range(DC * NQS)}
        kspill = {i: SP0 + 16 * (i + 1) for i in range(DC * NQS)}

        @blk.tensor
        def _(t):
            sW.wait(t)
            for qn in range(NQS):
                for m in range(DC):
                    i = qn * DC + m
                    psu = pAB if i % 2 == 0 else pCD
                    if i < 2:
                        sV.wait(t)
                    else:
                        t.wait_ge(sS.h, kevac[i - 2])
                    for k in range(DC):
                        t.matmul(psu[:, 0:512], wq_sb[:, k, 128 * m:128 * m + 128],
                                 ln_sb[:, k, qn * 512:qn * 512 + 512], start=(k == 0), stop=False)
                    sT.inc(t.matmul(psu[:, 0:512], b_sb[0:1, 128 * m:128 * m + 128],
                                    ones_row[0:1, :], start=False, stop=True))
                    t.wait_ge(sS.h, ksq[i])
                    if i == 0:
                        sV.wait(t)
                    elif m == 0:
                        t.wait_ge(sV.h, kred[qn - 1])
                    sT.inc(t.matmul(pR1[0:16, :], masks[:, m, :], xsq_sb[:, i % 2, :],
                                    start=(m == 0), stop=(m == DC - 1)))

        @blk.scalar
        def _(s):
            for i in range(DC * NQS):
                psu = pAB if i % 2 == 0 else pCD
                s.wait_ge(sT.h, proj_mm[i])
                if i >= 4:
                    s.wait_ge(sSP.h, kspill[i - 4])
                if i >= 2:
                    s.wait_ge(sT.h, knorm_mm[i - 2])
                sS.inc(s.copy(kstage[:, i % 4, :], psu[:, 0:512]))
                sS.inc(s.activation(xsq_sb[:, i % 2, :], kstage[:, i % 4, :], AF.Square))

        @blk.sync
        def _(sy):
            for qn in range(NQS):
                for m in range(DC):
                    i = qn * DC + m
                    sy.wait_ge(sS.h, kevac[i])
                    sSP.inc(sy.dma_start(cc_in[128 * m:128 * m + 128, qn * 512:qn * 512 + 512],
                                         kstage[:, i % 4, :]), 16)

        @blk.vector
        def _(v):
            for qn in range(NQS):
                v.wait_ge(sT.h, knorm_mm[qn * DC + DC - 1])
                if qn == 0:
                    sV.inc(v.tensor_reduce(nmx2[:], pR1[0:16, :], AX.X, ALU.max))
                else:
                    v.tensor_reduce(nmx3[:], pR1[0:16, :], AX.X, ALU.max)
                    sV.inc(v.tensor_tensor(nmx2[:], nmx2[:], nmx3[:], ALU.max))

    emit_proj_k()

    load_w(wvT)
    load_bias(2)

    def emit_proj_v():
        T0, S0 = sT.n, sS.n
        vevac = {tc: S0 + tc + 1 for tc in range(NTC)}
        vmm = {tc: T0 + tc + 1 for tc in range(NTC)}

        @blk.tensor
        def _(t):
            sW.wait(t)
            for tc in range(NTC):
                psu = pAB if tc % 2 == 0 else pCD
                if tc < 2:
                    sV.wait(t)
                else:
                    t.wait_ge(sS.h, vevac[tc - 2])
                mm = None
                for half in range(2):
                    for k in range(DC):
                        t.matmul(psu[:, half * 512:half * 512 + 512],
                                 ln_sb[:, k, tc * 128:tc * 128 + 128],
                                 wq_sb[:, k, half * 512:half * 512 + 512],
                                 start=(k == 0), stop=False)
                    mm = t.matmul(psu[:, half * 512:half * 512 + 512],
                                  ones_row[0:1, 0:128],
                                  b_sb[0:1, half * 512:half * 512 + 512],
                                  start=False, stop=True)
                sT.inc(mm)

        @blk.scalar
        def _(s):
            for tc in range(NTC):
                psu = pAB if tc % 2 == 0 else pCD
                s.wait_ge(sT.h, vmm[tc])
                s.copy(v_sb[:, tc, 0:512], psu[:, 0:512])
                sS.inc(s.copy(v_sb[:, tc, 512:1024], psu[:, 512:1024]))

    emit_proj_v()

    @blk.sync
    def _(sy):
        sS.wait(sy)
        vdst = cc_in[1024:2048, :].rearrange("r (two d) -> (r two) d", two=2) \
                                  .rearrange("(tc p) d -> p tc d", p=128)
        sSP.inc(sy.dma_start(vdst, v_sb[:]), 16)

    @blk.gpsimd
    def _(g):
        sSP.wait(g)
        sCC.inc(g.collective_compute(
            "AllGather", ALU.bypass, ins=[cc_in], outs=[cc_out],
            replica_groups=[[0, 1], [2, 3], [4, 5], [6, 7]]))

    def emit_norms_q():
        T0, S0 = sT.n, sS.n
        V0n = sV.n
        qred = {qs: V0n + qs + 1 for qs in range(NQS)}
        sqq = {}
        nm_mm = {}
        i = 0
        for qs in range(NQS):
            for c in range(DC):
                sqq[(qs, c)] = S0 + i + 1
                nm_mm[(qs, c)] = T0 + i + 1
                i += 1

        @blk.scalar
        def _(s):
            sT.wait(s)
            first = True
            for qs in range(NQS):
                for c in range(DC):
                    if not first and not (qs == 0 and c == 1):
                        prev = (qs, c - 2) if c >= 2 else (qs - 1, DC - 2 + c)
                        s.wait_ge(sT.h, nm_mm[prev])
                    first = False
                    sS.inc(s.activation(xsq_sb[:, c % 2, :],
                                        q_sb[:, c, qs * 512:qs * 512 + 512], AF.Square))

        @blk.tensor
        def _(t):
            for qs in range(NQS):
                for c in range(DC):
                    t.wait_ge(sS.h, sqq[(qs, c)])
                    if qs == 0 and c == 0:
                        sV.wait(t)
                    elif c == 0:
                        t.wait_ge(sV.h, qred[qs - 1])
                    sT.inc(t.matmul(pR1[0:16, :], masks[:, c, :], xsq_sb[:, c % 2, :],
                                    start=(c == 0), stop=(c == DC - 1)))

        @blk.vector
        def _(v):
            for qs in range(NQS):
                v.wait_ge(sT.h, nm_mm[(qs, DC - 1)])
                if qs == 0:
                    sV.inc(v.tensor_reduce(nmx[:], pR1[0:16, :], AX.X, ALU.max))
                else:
                    v.tensor_reduce(nmx3[:], pR1[0:16, :], AX.X, ALU.max)
                    sV.inc(v.tensor_tensor(nmx[:], nmx[:], nmx3[:], ALU.max))

    emit_norms_q()

    @blk.vector
    def _(v):
        sV.wait(v)
        sV.inc(v.tensor_tensor(nmx[:], nmx[:], nmx2[:], ALU.mult))

    @blk.scalar
    def _(s):
        sV.wait(s)
        sS.inc(s.activation(nmx3[:], nmx[:], AF.Sqrt))

    @blk.vector
    def _(v):
        sS.wait(v)
        v.memset(t32a[:], 0.0)
        sV.inc(v.tensor_copy(t32a[0:16, 0:1], nmx3[:]))
        sV.wait(v)
        sV.inc(v.transpose(t32b[:], t32a[:]))
        sV.wait(v)
        sV.inc(v.tensor_copy(brow_f[:], t32b[0:1, 0:16]))

    @blk.scalar
    def _(s):
        sV.wait(s)
        sS.inc(s.copy(brow_b[:], t32b[0:1, 0:16]))

    @blk.tensor
    def _(t):
        sS.wait(t)
        t.matmul(pR2[:, 0:16], s16_row[:], brow_b[:], start=True, stop=False)
        sT.inc(t.matmul(pR2[:, 0:16], m44_row[:], ones_row[0:1, 0:16], start=False, stop=True))

    @blk.vector
    def _(v):
        sT.wait(v)
        sV.inc(v.tensor_reduce(bneg_sb[:], pR2[:, 0:16], AX.X, ALU.max, negate=True))

    bneg_ready = sV.n
    bes.close()

    load_w(woT)
    load_bias(3)

    des = ExitStack()
    kp_sb = sb(des, "kp_sb", [128, 2, 2, T], BF16)
    vp_sb = sb(des, "vp_sb", [128, 2, 2, NTC, 128], BF16)
    expS = sb(des, "expS", [128, 2, 1024], BF16)
    rec_bf = sb(des, "rec_bf", [33, 512], BF16)
    binv_sb = sb(des, "binv_sb", [128, 512], BF16)
    avstage = sb(des, "avstage", [128, 2, 512], BF16)

    kv_cnt = {}
    pair_t_done = {}
    last_norm_v = [bneg_ready]
    av_spill = {}
    prev_pair_av = {}

    def emit_kv_load(p, prev_t):
        buf = p % 2

        @blk.sync
        def _(sy):
            sCC.wait(sy)
            if prev_t:
                sy.wait_ge(sT.h, prev_t)
            for s_ in range(2):
                sKV.inc(sy.dma_start(kp_sb[:, buf, s_, :], cc_out[s_, 128 * p:128 * p + 128, :]), 16)
                vsrc = cc_out[s_, 1024:2048, :].rearrange("r (two d) -> (r two) d", two=2)
                vsrc = vsrc[:, 128 * p:128 * p + 128].rearrange("(tc p2) d -> p2 tc d", p2=128)
                sKV.inc(sy.dma_start(vp_sb[:, buf, s_, :, :], vsrc), 16)
            kv_cnt[p] = sKV.n

    emit_kv_load(0, 0)
    emit_kv_load(1, 0)

    units_norm = {}
    prev_exp = {}

    for p in range(NP):
        buf = p % 2
        T0, S0, V0, SP0 = sT.n, sS.n, sV.n, sSP.n
        qk_cnt = {}
        av_cnt = {}
        binv_mm = {}
        n = T0
        for qs in range(NQS):
            for kc in range(32):
                n += 1
                qk_cnt[(qs, kc)] = n
                if kc >= 1:
                    n += 1
                    av_cnt[(qs, kc - 1)] = n
                if kc == 8 and qs > 0:
                    n += 1
                    binv_mm[qs - 1] = n
            n += 1
            av_cnt[(qs, 31)] = n
        n += 1
        binv_mm[NQS - 1] = n
        exp_cnt = {(qs, kc): S0 + qs * 32 + kc + 1 for qs in range(NQS) for kc in range(32)}
        rec_v = {qs: V0 + qs * 2 + 1 for qs in range(NQS)}
        norm_v = {qs: V0 + qs * 2 + 2 for qs in range(NQS)}
        spill_n = {qs: SP0 + 16 * (qs + 1) for qs in range(NQS)}
        for qs in range(NQS):
            units_norm[p * NQS + qs] = norm_v[qs]

        @blk.tensor
        def _(t, p=p, buf=buf, qk_cnt=qk_cnt, av_cnt=av_cnt, binv_mm=binv_mm,
              exp_cnt=exp_cnt, rec_v=rec_v, norm_v=norm_v, prevx=prev_exp):
            def emit_av(qs, kc):
                u = p * NQS + qs
                avp = (pR2, pR3)[u % 2]
                smp = (pR1, pR4)[u % 2]
                t.wait_ge(sS.h, exp_cnt[(qs, kc)])
                if kc == 0 and u >= 2:
                    t.wait_ge(sV.h, units_norm[u - 2])
                slot, j = kc // 16, kc % 16
                t.matmul(avp[0:64, :], vp_sb[:, buf, slot, j, 0:64],
                         expS[:, kc % 2, 0:512], start=(kc == 0), stop=(kc == 31))
                t.matmul(avp[64:128, :], vp_sb[:, buf, slot, j, 64:128],
                         expS[:, kc % 2, 512:1024], start=(kc == 0), stop=(kc == 31),
                         tile_position=(0, 64), skip_group_check=True)
                t.matmul(smp[0:1, :], ones_col[:], expS[:, kc % 2, 0:512],
                         start=(kc == 0), stop=(kc == 31))
                sT.inc(t.matmul(smp[32:33, :], ones_col[:], expS[:, kc % 2, 512:1024],
                                start=(kc == 0), stop=(kc == 31), tile_position=(0, 32),
                                skip_group_check=True))

            def emit_binv(qs):
                u = p * NQS + qs
                smp = (pR1, pR4)[u % 2]
                t.wait_ge(sV.h, rec_v[qs])
                t.matmul(smp[0:64, 0:512], ones33[0:1, :], rec_bf[0:1, :], start=True, stop=False)
                sT.inc(t.matmul(smp[64:128, 0:512], ones33[32:33, :], rec_bf[32:33, :],
                                start=True, stop=True, tile_position=(32, 64),
                                skip_group_check=True))

            for qs in range(NQS):
                q0 = qs * 512
                for kc in range(32):
                    sc = pAB if kc % 2 == 0 else pCD
                    slot, j = kc // 16, kc % 16
                    if kc == 0 and qs == 0:
                        t.wait_ge(sKV.h, kv_cnt[p])
                        if p == 0:
                            t.wait_ge(sV.h, last_norm_v[0])
                    if kc >= 2:
                        t.wait_ge(sS.h, exp_cnt[(qs, kc - 2)])
                    elif qs > 0:
                        t.wait_ge(sS.h, exp_cnt[(qs - 1, 30 + kc)])
                    elif p > 0:
                        s_prev = prevx.get(30 + kc)
                        if s_prev:
                            t.wait_ge(sS.h, s_prev)
                    t.matmul(sc[:, 0:512], kp_sb[0:64, buf, slot, 128 * j:128 * j + 128],
                             q_sb[0:64, p, q0:q0 + 512], start=True, stop=True)
                    sT.inc(t.matmul(sc[:, 512:1024], kp_sb[64:128, buf, slot, 128 * j:128 * j + 128],
                                    q_sb[64:128, p, q0:q0 + 512], start=True, stop=True,
                                    tile_position=(64, 0)))
                    if kc >= 1:
                        emit_av(qs, kc - 1)
                    if kc == 8 and qs > 0:
                        emit_binv(qs - 1)
                emit_av(qs, 31)
            emit_binv(NQS - 1)

        @blk.scalar
        def _(s, p=p, qk_cnt=qk_cnt, av_cnt=av_cnt, prev=prev_pair_av):
            for qs in range(NQS):
                for kc in range(32):
                    sc = pAB if kc % 2 == 0 else pCD
                    s.wait_ge(sT.h, qk_cnt[(qs, kc)])
                    if kc >= 2:
                        s.wait_ge(sT.h, av_cnt[(qs, kc - 2)])
                    elif qs > 0:
                        s.wait_ge(sT.h, av_cnt[(qs - 1, 30 + kc)])
                    elif p > 0:
                        s.wait_ge(sT.h, prev[30 + kc])
                    sS.inc(s.activation(expS[:, kc % 2, :], sc[:], AF.Exp,
                                        bias=bneg_sb[:], scale=0.125))

        @blk.vector
        def _(v, p=p, av_cnt=av_cnt, binv_mm=binv_mm):
            for qs in range(NQS):
                u = p * NQS + qs
                avp = (pR2, pR3)[u % 2]
                smp = (pR1, pR4)[u % 2]
                v.wait_ge(sT.h, av_cnt[(qs, 31)])
                with nc.allow_low_precision(reason="softmax normalizer feeds bf16 matmul"):
                    sV.inc(v.reciprocal(rec_bf[:], smp[0:33, :]))
                v.wait_ge(sT.h, binv_mm[qs])
                if u >= 2:
                    v.wait_ge(sSP.h, av_spill[u - 2] if qs < 2 else spill_n[qs - 2])
                v.tensor_copy(binv_sb[:], smp[:, 0:512])
                sV.inc(v.tensor_tensor(avstage[:, qs % 2, :], avp[:], binv_sb[:], ALU.mult))

        @blk.sync
        def _(sy, p=p, norm_v=norm_v):
            for qs in range(NQS):
                sy.wait_ge(sV.h, norm_v[qs])
                av_spill[p * NQS + qs] = sSP.inc(sy.dma_start(
                    attnv_dram[128 * p:128 * p + 128, qs * 512:qs * 512 + 512],
                    avstage[:, qs % 2, :]), 16)

        prev_pair_av = {30: av_cnt[(NQS - 1, 30)], 31: av_cnt[(NQS - 1, 31)]}
        prev_exp = {30: exp_cnt[(NQS - 1, 30)], 31: exp_cnt[(NQS - 1, 31)]}
        last_norm_v[0] = norm_v[NQS - 1]
        assert sT.n == T0 + NQS * 65
        assert sS.n == S0 + NQS * 32
        assert sV.n == V0 + NQS * 2
        pair_t_done[p] = sT.n
        if p + 2 < NP:
            emit_kv_load(p + 2, pair_t_done[p])

    des.close()
    wes = ExitStack()
    av_ring = sb(wes, "av_ring", [128, 2, DC, 512], BF16)

    def emit_wo():
        T0, V0, KV0 = sT.n, sV.n, sKV.n
        wo_mm = {i: T0 + i + 1 for i in range(NQS * DC)}
        wo_ev = {i: V0 + i + 1 for i in range(NQS * DC)}
        av_ld = {qn: KV0 + 16 * (qn + 1) for qn in range(NQS)}

        @blk.sync
        def _(sy):
            for qn in range(NQS):
                if qn == 0:
                    sy.wait_ge(sSP.h, av_spill[NP * NQS - 1])
                if qn >= 2:
                    sy.wait_ge(sT.h, wo_mm[(qn - 2) * DC + DC - 1])
                src_ap = attnv_dram[:, qn * 512:qn * 512 + 512].rearrange("(k p) t -> p k t", p=128)
                sKV.inc(sy.dma_start(av_ring[:, qn % 2, :, :], src_ap), 16)

        @blk.tensor
        def _(t):
            sW.wait(t)
            for qn in range(NQS):
                t.wait_ge(sKV.h, av_ld[qn])
                for m in range(DC):
                    i = qn * DC + m
                    psu = pAB if i % 2 == 0 else pCD
                    if i < 2:
                        t.wait_ge(sV.h, last_norm_v[0])
                    else:
                        t.wait_ge(sV.h, wo_ev[i - 2])
                    for k in range(DC):
                        t.matmul(psu[:, 0:512], wq_sb[:, k, 128 * m:128 * m + 128],
                                 av_ring[:, qn % 2, k, :], start=(k == 0), stop=False)
                    sT.inc(t.matmul(psu[:, 0:512], b_sb[0:1, 128 * m:128 * m + 128],
                                    ones_row[0:1, :], start=False, stop=True))

        @blk.vector
        def _(v):
            for qn in range(NQS):
                for m in range(DC):
                    i = qn * DC + m
                    psu = pAB if i % 2 == 0 else pCD
                    v.wait_ge(sT.h, wo_mm[i])
                    xsl = x_sb[:, m, qn * 512:qn * 512 + 512]
                    sV.inc(v.tensor_tensor(xsl, xsl, psu[:, 0:512], ALU.add))

    emit_wo()
    wes.close()
    aes.close()

    emit_ffn_era("2")

    @blk.sync
    def _(sy):
        sV.wait(sy)
        sOUT.wait(sy)
        sOUT.inc(sy.dma_start(out_t.rearrange("(c p) t -> p c t", p=128), x_sb[:]), 16)
        sOUT.wait(sy)

    es.close()
    return nc


_NC_CACHE = None


def kernel(L0_attn_x, L0_attn_Wq, L0_attn_bq, L0_attn_Wk, L0_attn_bk,
           L0_attn_Wv, L0_attn_bv, L0_attn_Wo, L0_attn_bo,
           L0_ffn_W_fc, L0_ffn_W_proj):
    global _NC_CACHE
    x = np.asarray(L0_attn_x, dtype=np.float32)
    B, S, _ = x.shape
    assert (B, S) == (4, 4096)

    wqT = np.ascontiguousarray(np.asarray(L0_attn_Wq, np.float32).T)
    wkT = np.ascontiguousarray(np.asarray(L0_attn_Wk, np.float32).T)
    wvT = np.ascontiguousarray(np.asarray(L0_attn_Wv, np.float32).T)
    woT = np.ascontiguousarray(np.asarray(L0_attn_Wo, np.float32).T)
    wfcT = np.asarray(L0_ffn_W_fc, np.float32).T
    wfcb = np.ascontiguousarray(wfcT.reshape(DC, 128, FC, 128).transpose(2, 0, 1, 3))
    wprojT = np.asarray(L0_ffn_W_proj, np.float32).T
    wpb = np.ascontiguousarray(wprojT.reshape(FC, 128, DC, 128).transpose(2, 0, 1, 3))
    brow = np.ascontiguousarray(np.concatenate(
        [np.asarray(b, np.float32).reshape(-1)
         for b in (L0_attn_bq, L0_attn_bk, L0_attn_bv, L0_attn_bo)])[None, :])
    cstv = np.full((128, 1), EPS, np.float32)

    if _NC_CACHE is None:
        _NC_CACHE = build_program()
    nc = _NC_CACHE

    in_maps = []
    for c in range(8):
        b, h = c // 2, c % 2
        xs = np.ascontiguousarray(x[b, h * T:(h + 1) * T, :].T)
        in_maps.append(dict(x_t=xs, wqT=wqT, wkT=wkT, wvT=wvT, woT=woT,
                            wfcb=wfcb, wpb=wpb, brow=brow, cst=cstv))

    res = run_bass_kernel_spmd(nc, in_maps, core_ids=list(range(8)))
    out = np.empty((B, S, D), np.float32)
    for c in range(8):
        b, h = c // 2, c % 2
        out[b, h * T:(h + 1) * T, :] = res.results[c]["out_t"].T
    return out


# version 16
# speedup vs baseline: 1.0045x; 1.0045x over previous
import numpy as np
from contextlib import ExitStack

import concourse.bass as bass
import concourse.mybir as mybir
from concourse.bass_utils import run_bass_kernel_spmd

F32 = mybir.dt.float32
BF16 = mybir.dt.bfloat16
AF = mybir.ActivationFunctionType
ALU = mybir.AluOpType
AX = mybir.AxisListType

D = 1024
DC = 8
T = 2048
FF = 4096
FC = 32
NP = 8
QS = 512
NQS = T // QS
NTC = T // 128
EPS = 1e-5


class Sem:
    def __init__(self, es, nc, name):
        self.h = es.enter_context(nc.semaphore(name))
        self.n = 0

    def inc(self, instr, k=1):
        instr.then_inc(self.h, k)
        self.n += k
        return self.n

    def wait(self, eng, n=None):
        eng.wait_ge(self.h, self.n if n is None else n)


def build_program(debug=False):
    nc = bass.Bass("TRN2", target_bir_lowering=False)

    x_t = nc.dram_tensor("x_t", [D, T], F32, kind="ExternalInput").ap()
    wqT = nc.dram_tensor("wqT", [D, D], F32, kind="ExternalInput").ap()
    wkT = nc.dram_tensor("wkT", [D, D], F32, kind="ExternalInput").ap()
    wvT = nc.dram_tensor("wvT", [D, D], F32, kind="ExternalInput").ap()
    woT = nc.dram_tensor("woT", [D, D], F32, kind="ExternalInput").ap()
    wfcb = nc.dram_tensor("wfcb", [FC, 128, DC, 128], F32, kind="ExternalInput").ap()
    wpb = nc.dram_tensor("wpb", [DC, 2, 128, 16, 128], F32, kind="ExternalInput").ap()
    brow = nc.dram_tensor("brow", [1, 4 * D], F32, kind="ExternalInput").ap()
    cst = nc.dram_tensor("cst", [128, 1], F32, kind="ExternalInput").ap()
    out_t = nc.dram_tensor("out_t", [D, T], F32, kind="ExternalOutput").ap()
    cc_in = nc.dram_tensor("cc_in", [2048, 2048], BF16, kind="Internal").ap()
    cc_out = nc.dram_tensor("cc_out", [2, 2048, 2048], BF16, kind="Internal").ap()
    attnv_dram = nc.dram_tensor("attnv_dram", [D, T], BF16, kind="Internal").ap()
    wfc16 = nc.dram_tensor("wfc16", [FC, 128, DC, 128], BF16, kind="Internal").ap()
    wp16 = nc.dram_tensor("wp16", [DC, 2, 128, 16, 128], BF16, kind="Internal").ap()

    es = ExitStack()

    def sb(stack, name, shape, dt):
        return stack.enter_context(nc.sbuf_tensor(name, shape, dt))

    x_sb = sb(es, "x_sb", [128, DC, T], F32)
    ln_sb = sb(es, "ln_sb", [128, DC, T], BF16)
    ones_col = sb(es, "ones_col", [128, 1], BF16)
    ones_row = sb(es, "ones_row", [1, 512], BF16)
    masks = sb(es, "masks", [128, DC, 16], BF16)
    b_sb = sb(es, "b_sb", [1, D], BF16)
    eps_sb = sb(es, "eps_sb", [128, 1], F32)
    bneg_sb = sb(es, "bneg_sb", [128, 1], F32)
    rows_sb = sb(es, "rows_sb", [1, 2, 512], F32)
    ab_bf = sb(es, "ab_bf", [1, 1024], BF16)
    ones33 = sb(es, "ones33", [33, 64], BF16)
    s16_row = sb(es, "s16_row", [1, 128], BF16)
    m44_row = sb(es, "m44_row", [1, 128], BF16)
    nmx = sb(es, "nmx", [16, 1], F32)
    nmx2 = sb(es, "nmx2", [16, 1], F32)
    nmx3 = sb(es, "nmx3", [16, 1], F32)
    xb_sb = sb(es, "xb_sb", [128, 2, 512], BF16)
    xsq_sb = sb(es, "xsq_sb", [128, 2, 512], BF16)
    tmp_sb = sb(es, "tmp_sb", [128, 512], F32)
    brow_f = sb(es, "brow_f", [1, 16], F32)
    t32a = sb(es, "t32a", [32, 32], F32)
    t32b = sb(es, "t32b", [32, 32], F32)
    brow_b = sb(es, "brow_b", [1, 16], BF16)

    pAB = es.enter_context(nc.psum_tensor("pAB", [128, 1024], F32))
    pCD = es.enter_context(nc.psum_tensor("pCD", [128, 1024], F32))
    pR1 = es.enter_context(nc.psum_tensor("pR1", [128, 512], F32))
    pR2 = es.enter_context(nc.psum_tensor("pR2", [128, 512], F32))
    pR3 = es.enter_context(nc.psum_tensor("pR3", [128, 512], F32))
    pR4 = es.enter_context(nc.psum_tensor("pR4", [128, 512], F32))

    sIN = Sem(es, nc, "sIN")
    sW = Sem(es, nc, "sW")
    sT = Sem(es, nc, "sT")
    sV = Sem(es, nc, "sV")
    sS = Sem(es, nc, "sS")
    sSP = Sem(es, nc, "sSP")
    sCC = Sem(es, nc, "sCC")
    sKV = Sem(es, nc, "sKV")
    sOUT = Sem(es, nc, "sOUT")

    blk = es.enter_context(nc.Block())

    @blk.sync
    def _(sy):
        sIN.inc(sy.dma_start(x_sb[:], x_t.rearrange("(c p) t -> p c t", p=128)), 16)
        sIN.inc(sy.dma_start(eps_sb[:], cst), 16)

    @blk.vector
    def _(v):
        v.memset(ones_col[:], 1.0)
        v.memset(ones_row[:], 1.0)
        v.memset(ones33[0:1, :], 1.0)
        v.memset(ones33[32:33, :], 1.0)
        v.memset(s16_row[:], 0.125)
        v.memset(m44_row[:], -44.0)
        v.memset(masks[:], 0.0)
        for c in range(DC):
            v.memset(masks[0:64, c, 2 * c:2 * c + 1], 1.0)
        last = None
        for c in range(DC):
            last = v.memset(masks[64:128, c, 2 * c + 1:2 * c + 2], 1.0)
        sV.inc(last)

    cast_fc = {}
    cast_wp = {}

    @blk.gpsimd
    def _(g):
        for f in range(FC):
            cast_fc[f] = sW.inc(g.dma_start(wfc16[f], wfcb[f]), 16)
        for si in range(16):
            dc, fh = si // 2, si % 2
            cast_wp[si] = sW.inc(g.dma_start(wp16[dc, fh], wpb[dc, fh]), 16)

    def load_bias(r):
        @blk.gpsimd
        def _(g):
            sT.wait(g)
            sW.inc(g.dma_start(b_sb[:], brow[0:1, r * D:(r + 1) * D]), 16)

    rowA = rows_sb[0:1, 0, :]
    rowB = rows_sb[0:1, 1, :]
    rowC = tmp_sb[0:1, :]

    def emit_ln(qq):
        q0 = qq * 512
        T0, V0, S0 = sT.n, sV.n, sS.n
        ln_mm = {c: T0 + c + 1 for c in range(DC)}
        cp_done = {c: V0 + c + 1 for c in range(DC)}
        sq_done = {c: S0 + c + 1 for c in range(DC)}

        @blk.vector
        def _(v):
            if qq == 0:
                sIN.wait(v)
            if sOUT.n > 0:
                sOUT.wait(v)
            for c in range(DC):
                if c >= 2:
                    v.wait_ge(sT.h, ln_mm[c - 2])
                sV.inc(v.tensor_copy(xb_sb[:, c % 2, :], x_sb[:, c, q0:q0 + 512]))

        @blk.scalar
        def _(s):
            for c in range(DC):
                s.wait_ge(sV.h, cp_done[c])
                if c >= 2:
                    s.wait_ge(sT.h, ln_mm[c - 2])
                sS.inc(s.activation(xsq_sb[:, c % 2, :], xb_sb[:, c % 2, :], AF.Square))
            s.wait_ge(sV.h, V0 + DC + 1)
            sS.inc(s.activation(rowC, rowB, AF.Sqrt, bias=eps_sb[0:1, :]))

        @blk.tensor
        def _(t):
            for c in range(DC):
                t.wait_ge(sS.h, sq_done[c])
                if c == 0:
                    t.wait_ge(sV.h, V0)
                t.matmul(pR1[0:1, :], ones_col[:], xb_sb[:, c % 2, :], start=(c == 0), stop=(c == DC - 1))
                sT.inc(t.matmul(pR2[0:1, :], ones_col[:], xsq_sb[:, c % 2, :],
                                start=(c == 0), stop=(c == DC - 1)))
            t.wait_ge(sV.h, V0 + DC + 2)
            t.matmul(pAB[:, 0:512], ones_row[0:1, 0:128], ab_bf[0:1, 0:512], start=True, stop=True)
            sT.inc(t.matmul(pAB[:, 512:1024], ones_row[0:1, 0:128], ab_bf[0:1, 512:1024],
                            start=True, stop=True))

        @blk.vector
        def _(v):
            v.wait_ge(sT.h, T0 + DC)
            v.tensor_scalar_mul(rowA, pR1[0:1, :], 1.0 / D)
            v.tensor_tensor(rowB, rowA, rowA, ALU.mult)
            sV.inc(v.scalar_tensor_tensor(rowB, pR2[0:1, :], 1.0 / D, rowB, ALU.mult, ALU.subtract))
            v.wait_ge(sS.h, S0 + DC + 1)
            v.reciprocal(rowB, rowC)
            v.scalar_tensor_tensor(rowC, rowA, -1.0, rowB, ALU.mult, ALU.mult)
            v.tensor_copy(ab_bf[0:1, 0:512], rowB)
            sV.inc(v.tensor_copy(ab_bf[0:1, 512:1024], rowC))
            v.wait_ge(sT.h, T0 + DC + 1)
            last = None
            for c in range(DC):
                v.tensor_tensor(tmp_sb[:], x_sb[:, c, q0:q0 + 512], pAB[:, 0:512], ALU.mult)
                last = v.tensor_tensor(ln_sb[:, c, q0:q0 + 512], tmp_sb[:], pAB[:, 512:1024], ALU.add)
            sV.inc(last)

        assert sT.n == T0 + DC + 1
        assert sV.n == V0 + DC + 3
        assert sS.n == S0 + DC + 1

    def emit_ffn_era(tag):
        fes = ExitStack()
        wfc_ring = sb(fes, "wfc_ring" + tag, [128, 2, DC, 128], BF16)
        h_sb = sb(fes, "h_sb" + tag, [128, FC, 1024], BF16)
        wp_ring = sb(fes, "wp_ring" + tag, [128, 2, 16, 128], BF16)
        r_sb = sb(fes, "r_sb" + tag, [128, 2, 1024], F32)
        s_scan = sb(fes, "s_scan" + tag, [128, 512], F32)
        mem_scan = sb(fes, "mem_scan" + tag, [128, 512], F32)
        acc_scan = sb(fes, "acc_scan" + tag, [128, 512], F32)

        era_t0 = sT.n
        era_v0 = sV.n

        def yvb(dc, sub):
            r = dc % 4
            if r == 0:
                return pAB[:, sub * 512:sub * 512 + 512]
            if r == 1:
                return pCD[:, sub * 512:sub * 512 + 512]
            if r == 2:
                return (pR1 if sub == 0 else pR2)[:, :]
            return (pR3 if sub == 0 else pR4)[:, :]

        for qq in range(NQS):
            emit_ln(qq)

        for hh in range(2):
            q0 = hh * 1024
            T0, V0, S0, W0 = sT.n, sV.n, sS.n, sW.n
            u_cnt = {f: T0 + f + 1 for f in range(FC)}
            wp_mm = {si: T0 + FC + si + 1 for si in range(16)}
            h_cnt = {f: V0 + f + 1 for f in range(FC)}
            scan_cnt = {dc: V0 + FC + dc + 1 for dc in range(DC)}
            relu_cnt = {f: S0 + f + 1 for f in range(FC)}
            wfc_ld = {f: W0 + 16 * (f + 1) for f in range(FC)}
            wp_ld = {si: W0 + 16 * FC + 16 * (si + 1) for si in range(16)}

            @blk.sync
            def _(sy, hh=hh, T0=T0, u_cnt=u_cnt, wp_mm=wp_mm):
                for f in range(FC):
                    sy.wait_ge(sW.h, cast_fc[f])
                    if f >= 2:
                        sy.wait_ge(sT.h, u_cnt[f - 2])
                    elif hh == 1:
                        sy.wait_ge(sT.h, T0)
                    else:
                        if era_t0 > 0:
                            sy.wait_ge(sT.h, era_t0)
                        if era_v0 > 0:
                            sy.wait_ge(sV.h, era_v0)
                    sW.inc(sy.dma_start(wfc_ring[:, f % 2, :, :], wfc16[f]), 16)
                for si in range(16):
                    dc, fh = si // 2, si % 2
                    sy.wait_ge(sW.h, cast_wp[si])
                    if si >= 2:
                        sy.wait_ge(sT.h, wp_mm[si - 2])
                    elif hh == 1:
                        sy.wait_ge(sT.h, T0)
                    elif era_v0 > 0:
                        sy.wait_ge(sV.h, era_v0)
                    sW.inc(sy.dma_start(wp_ring[:, si % 2, :, :], wp16[dc, fh]), 16)

            @blk.tensor
            def _(t, q0=q0, V0=V0, u_cnt=u_cnt, wp_mm=wp_mm, h_cnt=h_cnt,
                  scan_cnt=scan_cnt, relu_cnt=relu_cnt, wfc_ld=wfc_ld, wp_ld=wp_ld):
                for f in range(FC):
                    psu = pAB if f % 2 == 0 else pCD
                    t.wait_ge(sW.h, wfc_ld[f])
                    if f < 2:
                        t.wait_ge(sV.h, V0)
                    else:
                        t.wait_ge(sV.h, h_cnt[f - 2])
                    mm = None
                    for sub in range(2):
                        for k in range(DC):
                            mm = t.matmul(psu[:, sub * 512:sub * 512 + 512],
                                          wfc_ring[:, f % 2, k, :],
                                          ln_sb[:, k, q0 + sub * 512:q0 + sub * 512 + 512],
                                          start=(k == 0), stop=(k == DC - 1))
                    sT.inc(mm)
                for si in range(16):
                    dc, fh = si // 2, si % 2
                    t.wait_ge(sW.h, wp_ld[si])
                    if fh == 0:
                        if dc < 2:
                            t.wait_ge(sV.h, h_cnt[FC - 1])
                            t.wait_ge(sS.h, relu_cnt[FC - 1])
                        elif dc >= 4:
                            t.wait_ge(sV.h, scan_cnt[dc - 4])
                    mm = None
                    for sub in range(2):
                        ysl = yvb(dc, sub)
                        for fi in range(16):
                            f = 16 * fh + fi
                            mm = t.matmul(ysl, wp_ring[:, si % 2, fi, :],
                                          h_sb[:, f, sub * 512:sub * 512 + 512],
                                          start=(f == 0), stop=(f == FC - 1))
                    sT.inc(mm)

            @blk.scalar
            def _(s, u_cnt=u_cnt, h_cnt=h_cnt):
                for f in range(FC):
                    psu = pAB if f % 2 == 0 else pCD
                    s.wait_ge(sT.h, u_cnt[f])
                    if f >= 2:
                        s.wait_ge(sV.h, h_cnt[f - 2])
                    sS.inc(s.activation(r_sb[:, f % 2, :], psu[:], AF.Relu))

            def scan_ops(eng, y, xsl, ss, mem, acc, fused):
                eng.tensor_single_scalar(acc, y, 1.0, ALU.is_gt)
                if fused:
                    eng.scalar_tensor_tensor(mem, y, 1.9, acc, ALU.mult, ALU.subtract)
                else:
                    eng.tensor_scalar_mul(mem, y, 1.9)
                    eng.tensor_tensor(mem, mem, acc, ALU.subtract)
                for _step in range(3):
                    eng.tensor_single_scalar(ss, mem, 1.0, ALU.is_gt)
                    eng.tensor_tensor(acc, acc, ss, ALU.add)
                    if fused:
                        eng.scalar_tensor_tensor(mem, mem, 0.9, y, ALU.mult, ALU.add)
                    else:
                        eng.tensor_scalar_mul(mem, mem, 0.9)
                        eng.tensor_tensor(mem, mem, y, ALU.add)
                    eng.tensor_tensor(mem, mem, ss, ALU.subtract)
                eng.tensor_single_scalar(ss, mem, 1.0, ALU.is_gt)
                eng.tensor_tensor(acc, acc, ss, ALU.add)
                if fused:
                    return eng.scalar_tensor_tensor(xsl, acc, 0.1, xsl, ALU.mult, ALU.add)
                eng.tensor_scalar_mul(ss, acc, 0.1)
                return eng.tensor_tensor(xsl, xsl, ss, ALU.add)

            @blk.vector
            def _(v, q0=q0, relu_cnt=relu_cnt, wp_mm=wp_mm):
                for f in range(FC):
                    psu = pAB if f % 2 == 0 else pCD
                    v.wait_ge(sS.h, relu_cnt[f])
                    sV.inc(v.tensor_tensor(h_sb[:, f, :], psu[:], r_sb[:, f % 2, :], ALU.mult))
                for dc in range(DC):
                    v.wait_ge(sT.h, wp_mm[2 * dc + 1])
                    last = None
                    for sub in range(2):
                        y = yvb(dc, sub)
                        xsl = x_sb[:, dc, q0 + sub * 512:q0 + sub * 512 + 512]
                        last = scan_ops(v, y, xsl, s_scan[:], mem_scan[:], acc_scan[:], True)
                    sV.inc(last)

            assert sT.n == T0 + FC + 16
            assert sV.n == V0 + FC + DC
            assert sS.n == S0 + FC
            assert sW.n == W0 + 16 * (FC + 16)

        fes.close()

    emit_ffn_era("1")

    aes = ExitStack()
    q_sb = sb(aes, "q_sb", [128, DC, T], BF16)
    wq_sb = sb(aes, "wq_sb", [128, DC, D], BF16)
    bes = ExitStack()
    kstage = sb(bes, "kstage", [128, 4, 512], BF16)
    v_sb = sb(bes, "v_sb", [128, NTC, D], BF16)

    for qq in range(NQS):
        emit_ln(qq)

    def load_w(dram):
        @blk.gpsimd
        def _(g):
            sT.wait(g)
            for k in range(DC):
                sW.inc(g.dma_start(wq_sb[:, k, :], dram[128 * k:128 * k + 128, :]), 16)

    def emit_proj_q():
        T0, S0 = sT.n, sS.n
        evac = {i: S0 + i + 1 for i in range(DC * NQS)}
        mmc = {i: T0 + i + 1 for i in range(DC * NQS)}

        @blk.tensor
        def _(t):
            sW.wait(t)
            for i in range(DC * NQS):
                m, qn = divmod(i, NQS)
                psu = pAB if i % 2 == 0 else pCD
                if i < 2:
                    sV.wait(t)
                else:
                    t.wait_ge(sS.h, evac[i - 2])
                for k in range(DC):
                    t.matmul(psu[:, 0:512], wq_sb[:, k, 128 * m:128 * m + 128],
                             ln_sb[:, k, qn * 512:qn * 512 + 512], start=(k == 0), stop=False)
                sT.inc(t.matmul(psu[:, 0:512], b_sb[0:1, 128 * m:128 * m + 128],
                                ones_row[0:1, :], start=False, stop=True))

        @blk.scalar
        def _(s):
            for i in range(DC * NQS):
                m, qn = divmod(i, NQS)
                psu = pAB if i % 2 == 0 else pCD
                s.wait_ge(sT.h, mmc[i])
                sS.inc(s.copy(q_sb[:, m, qn * 512:qn * 512 + 512], psu[:, 0:512]))

    load_w(wqT)
    load_bias(0)
    emit_proj_q()

    load_w(wkT)
    load_bias(1)

    def emit_proj_k():
        T0, S0, V0, SP0 = sT.n, sS.n, sV.n, sSP.n
        kred = {qn: V0 + qn + 1 for qn in range(NQS)}
        kevac = {i: S0 + 2 * i + 1 for i in range(DC * NQS)}
        ksq = {i: S0 + 2 * i + 2 for i in range(DC * NQS)}
        proj_mm = {i: T0 + 2 * i + 1 for i in range(DC * NQS)}
        knorm_mm = {i: T0 + 2 * i + 2 for i in range(DC * NQS)}
        kspill = {i: SP0 + 16 * (i + 1) for i in range(DC * NQS)}

        @blk.tensor
        def _(t):
            sW.wait(t)
            for qn in range(NQS):
                for m in range(DC):
                    i = qn * DC + m
                    psu = pAB if i % 2 == 0 else pCD
                    if i < 2:
                        sV.wait(t)
                    else:
                        t.wait_ge(sS.h, kevac[i - 2])
                    for k in range(DC):
                        t.matmul(psu[:, 0:512], wq_sb[:, k, 128 * m:128 * m + 128],
                                 ln_sb[:, k, qn * 512:qn * 512 + 512], start=(k == 0), stop=False)
                    sT.inc(t.matmul(psu[:, 0:512], b_sb[0:1, 128 * m:128 * m + 128],
                                    ones_row[0:1, :], start=False, stop=True))
                    t.wait_ge(sS.h, ksq[i])
                    if i == 0:
                        sV.wait(t)
                    elif m == 0:
                        t.wait_ge(sV.h, kred[qn - 1])
                    sT.inc(t.matmul(pR1[0:16, :], masks[:, m, :], xsq_sb[:, i % 2, :],
                                    start=(m == 0), stop=(m == DC - 1)))

        @blk.scalar
        def _(s):
            for i in range(DC * NQS):
                psu = pAB if i % 2 == 0 else pCD
                s.wait_ge(sT.h, proj_mm[i])
                if i >= 4:
                    s.wait_ge(sSP.h, kspill[i - 4])
                if i >= 2:
                    s.wait_ge(sT.h, knorm_mm[i - 2])
                sS.inc(s.copy(kstage[:, i % 4, :], psu[:, 0:512]))
                sS.inc(s.activation(xsq_sb[:, i % 2, :], kstage[:, i % 4, :], AF.Square))

        @blk.sync
        def _(sy):
            for qn in range(NQS):
                for m in range(DC):
                    i = qn * DC + m
                    sy.wait_ge(sS.h, kevac[i])
                    sSP.inc(sy.dma_start(cc_in[128 * m:128 * m + 128, qn * 512:qn * 512 + 512],
                                         kstage[:, i % 4, :]), 16)

        @blk.vector
        def _(v):
            for qn in range(NQS):
                v.wait_ge(sT.h, knorm_mm[qn * DC + DC - 1])
                if qn == 0:
                    sV.inc(v.tensor_reduce(nmx2[:], pR1[0:16, :], AX.X, ALU.max))
                else:
                    v.tensor_reduce(nmx3[:], pR1[0:16, :], AX.X, ALU.max)
                    sV.inc(v.tensor_tensor(nmx2[:], nmx2[:], nmx3[:], ALU.max))

    emit_proj_k()

    load_w(wvT)
    load_bias(2)

    def emit_proj_v():
        T0, S0 = sT.n, sS.n
        vevac = {tc: S0 + tc + 1 for tc in range(NTC)}
        vmm = {tc: T0 + tc + 1 for tc in range(NTC)}

        @blk.tensor
        def _(t):
            sW.wait(t)
            for tc in range(NTC):
                psu = pAB if tc % 2 == 0 else pCD
                if tc < 2:
                    sV.wait(t)
                else:
                    t.wait_ge(sS.h, vevac[tc - 2])
                mm = None
                for half in range(2):
                    for k in range(DC):
                        t.matmul(psu[:, half * 512:half * 512 + 512],
                                 ln_sb[:, k, tc * 128:tc * 128 + 128],
                                 wq_sb[:, k, half * 512:half * 512 + 512],
                                 start=(k == 0), stop=False)
                    mm = t.matmul(psu[:, half * 512:half * 512 + 512],
                                  ones_row[0:1, 0:128],
                                  b_sb[0:1, half * 512:half * 512 + 512],
                                  start=False, stop=True)
                sT.inc(mm)

        @blk.scalar
        def _(s):
            for tc in range(NTC):
                psu = pAB if tc % 2 == 0 else pCD
                s.wait_ge(sT.h, vmm[tc])
                s.copy(v_sb[:, tc, 0:512], psu[:, 0:512])
                sS.inc(s.copy(v_sb[:, tc, 512:1024], psu[:, 512:1024]))

    emit_proj_v()

    @blk.sync
    def _(sy):
        sS.wait(sy)
        vdst = cc_in[1024:2048, :].rearrange("r (two d) -> (r two) d", two=2) \
                                  .rearrange("(tc p) d -> p tc d", p=128)
        sSP.inc(sy.dma_start(vdst, v_sb[:]), 16)

    @blk.gpsimd
    def _(g):
        sSP.wait(g)
        sCC.inc(g.collective_compute(
            "AllGather", ALU.bypass, ins=[cc_in], outs=[cc_out],
            replica_groups=[[0, 1], [2, 3], [4, 5], [6, 7]]))

    def emit_norms_q():
        T0, S0 = sT.n, sS.n
        V0n = sV.n
        qred = {qs: V0n + qs + 1 for qs in range(NQS)}
        sqq = {}
        nm_mm = {}
        i = 0
        for qs in range(NQS):
            for c in range(DC):
                sqq[(qs, c)] = S0 + i + 1
                nm_mm[(qs, c)] = T0 + i + 1
                i += 1

        @blk.scalar
        def _(s):
            sT.wait(s)
            first = True
            for qs in range(NQS):
                for c in range(DC):
                    if not first and not (qs == 0 and c == 1):
                        prev = (qs, c - 2) if c >= 2 else (qs - 1, DC - 2 + c)
                        s.wait_ge(sT.h, nm_mm[prev])
                    first = False
                    sS.inc(s.activation(xsq_sb[:, c % 2, :],
                                        q_sb[:, c, qs * 512:qs * 512 + 512], AF.Square))

        @blk.tensor
        def _(t):
            for qs in range(NQS):
                for c in range(DC):
                    t.wait_ge(sS.h, sqq[(qs, c)])
                    if qs == 0 and c == 0:
                        sV.wait(t)
                    elif c == 0:
                        t.wait_ge(sV.h, qred[qs - 1])
                    sT.inc(t.matmul(pR1[0:16, :], masks[:, c, :], xsq_sb[:, c % 2, :],
                                    start=(c == 0), stop=(c == DC - 1)))

        @blk.vector
        def _(v):
            for qs in range(NQS):
                v.wait_ge(sT.h, nm_mm[(qs, DC - 1)])
                if qs == 0:
                    sV.inc(v.tensor_reduce(nmx[:], pR1[0:16, :], AX.X, ALU.max))
                else:
                    v.tensor_reduce(nmx3[:], pR1[0:16, :], AX.X, ALU.max)
                    sV.inc(v.tensor_tensor(nmx[:], nmx[:], nmx3[:], ALU.max))

    emit_norms_q()

    @blk.vector
    def _(v):
        sV.wait(v)
        sV.inc(v.tensor_tensor(nmx[:], nmx[:], nmx2[:], ALU.mult))

    @blk.scalar
    def _(s):
        sV.wait(s)
        sS.inc(s.activation(nmx3[:], nmx[:], AF.Sqrt))

    @blk.vector
    def _(v):
        sS.wait(v)
        v.memset(t32a[:], 0.0)
        sV.inc(v.tensor_copy(t32a[0:16, 0:1], nmx3[:]))
        sV.wait(v)
        sV.inc(v.transpose(t32b[:], t32a[:]))
        sV.wait(v)
        sV.inc(v.tensor_copy(brow_f[:], t32b[0:1, 0:16]))

    @blk.scalar
    def _(s):
        sV.wait(s)
        sS.inc(s.copy(brow_b[:], t32b[0:1, 0:16]))

    @blk.tensor
    def _(t):
        sS.wait(t)
        t.matmul(pR2[:, 0:16], s16_row[:], brow_b[:], start=True, stop=False)
        sT.inc(t.matmul(pR2[:, 0:16], m44_row[:], ones_row[0:1, 0:16], start=False, stop=True))

    @blk.vector
    def _(v):
        sT.wait(v)
        sV.inc(v.tensor_reduce(bneg_sb[:], pR2[:, 0:16], AX.X, ALU.max, negate=True))

    bneg_ready = sV.n
    bes.close()

    load_w(woT)
    load_bias(3)

    des = ExitStack()
    kp_sb = sb(des, "kp_sb", [128, 2, 2, T], BF16)
    vp_sb = sb(des, "vp_sb", [128, 2, 2, NTC, 128], BF16)
    expS = sb(des, "expS", [128, 2, 1024], BF16)
    rec_bf = sb(des, "rec_bf", [33, 512], BF16)
    binv_sb = sb(des, "binv_sb", [128, 512], BF16)
    avstage = sb(des, "avstage", [128, 2, 512], BF16)

    kv_cnt = {}
    pair_t_done = {}
    last_norm_v = [bneg_ready]
    av_spill = {}
    prev_pair_av = {}

    def emit_kv_load(p, prev_t):
        buf = p % 2

        @blk.sync
        def _(sy):
            sCC.wait(sy)
            if prev_t:
                sy.wait_ge(sT.h, prev_t)
            for s_ in range(2):
                sKV.inc(sy.dma_start(kp_sb[:, buf, s_, :], cc_out[s_, 128 * p:128 * p + 128, :]), 16)
                vsrc = cc_out[s_, 1024:2048, :].rearrange("r (two d) -> (r two) d", two=2)
                vsrc = vsrc[:, 128 * p:128 * p + 128].rearrange("(tc p2) d -> p2 tc d", p2=128)
                sKV.inc(sy.dma_start(vp_sb[:, buf, s_, :, :], vsrc), 16)
            kv_cnt[p] = sKV.n

    emit_kv_load(0, 0)
    emit_kv_load(1, 0)

    units_norm = {}
    prev_exp = {}

    for p in range(NP):
        buf = p % 2
        T0, S0, V0, SP0 = sT.n, sS.n, sV.n, sSP.n
        qk_cnt = {}
        av_cnt = {}
        binv_mm = {}
        n = T0
        for qs in range(NQS):
            for kc in range(32):
                n += 1
                qk_cnt[(qs, kc)] = n
                if kc >= 1:
                    n += 1
                    av_cnt[(qs, kc - 1)] = n
                if kc == 8 and qs > 0:
                    n += 1
                    binv_mm[qs - 1] = n
            n += 1
            av_cnt[(qs, 31)] = n
        n += 1
        binv_mm[NQS - 1] = n
        exp_cnt = {(qs, kc): S0 + qs * 32 + kc + 1 for qs in range(NQS) for kc in range(32)}
        rec_v = {qs: V0 + qs * 2 + 1 for qs in range(NQS)}
        norm_v = {qs: V0 + qs * 2 + 2 for qs in range(NQS)}
        spill_n = {qs: SP0 + 16 * (qs + 1) for qs in range(NQS)}
        for qs in range(NQS):
            units_norm[p * NQS + qs] = norm_v[qs]

        @blk.tensor
        def _(t, p=p, buf=buf, qk_cnt=qk_cnt, av_cnt=av_cnt, binv_mm=binv_mm,
              exp_cnt=exp_cnt, rec_v=rec_v, norm_v=norm_v, prevx=prev_exp):
            def emit_av(qs, kc):
                u = p * NQS + qs
                avp = (pR2, pR3)[u % 2]
                smp = (pR1, pR4)[u % 2]
                t.wait_ge(sS.h, exp_cnt[(qs, kc)])
                if kc == 0 and u >= 2:
                    t.wait_ge(sV.h, units_norm[u - 2])
                slot, j = kc // 16, kc % 16
                t.matmul(avp[0:64, :], vp_sb[:, buf, slot, j, 0:64],
                         expS[:, kc % 2, 0:512], start=(kc == 0), stop=(kc == 31))
                t.matmul(avp[64:128, :], vp_sb[:, buf, slot, j, 64:128],
                         expS[:, kc % 2, 512:1024], start=(kc == 0), stop=(kc == 31),
                         tile_position=(0, 64), skip_group_check=True)
                t.matmul(smp[0:1, :], ones_col[:], expS[:, kc % 2, 0:512],
                         start=(kc == 0), stop=(kc == 31))
                sT.inc(t.matmul(smp[32:33, :], ones_col[:], expS[:, kc % 2, 512:1024],
                                start=(kc == 0), stop=(kc == 31), tile_position=(0, 32),
                                skip_group_check=True))

            def emit_binv(qs):
                u = p * NQS + qs
                smp = (pR1, pR4)[u % 2]
                t.wait_ge(sV.h, rec_v[qs])
                t.matmul(smp[0:64, 0:512], ones33[0:1, :], rec_bf[0:1, :], start=True, stop=False)
                sT.inc(t.matmul(smp[64:128, 0:512], ones33[32:33, :], rec_bf[32:33, :],
                                start=True, stop=True, tile_position=(32, 64),
                                skip_group_check=True))

            for qs in range(NQS):
                q0 = qs * 512
                for kc in range(32):
                    sc = pAB if kc % 2 == 0 else pCD
                    slot, j = kc // 16, kc % 16
                    if kc == 0 and qs == 0:
                        t.wait_ge(sKV.h, kv_cnt[p])
                        if p == 0:
                            t.wait_ge(sV.h, last_norm_v[0])
                    if kc >= 2:
                        t.wait_ge(sS.h, exp_cnt[(qs, kc - 2)])
                    elif qs > 0:
                        t.wait_ge(sS.h, exp_cnt[(qs - 1, 30 + kc)])
                    elif p > 0:
                        s_prev = prevx.get(30 + kc)
                        if s_prev:
                            t.wait_ge(sS.h, s_prev)
                    t.matmul(sc[:, 0:512], kp_sb[0:64, buf, slot, 128 * j:128 * j + 128],
                             q_sb[0:64, p, q0:q0 + 512], start=True, stop=True)
                    sT.inc(t.matmul(sc[:, 512:1024], kp_sb[64:128, buf, slot, 128 * j:128 * j + 128],
                                    q_sb[64:128, p, q0:q0 + 512], start=True, stop=True,
                                    tile_position=(64, 0)))
                    if kc >= 1:
                        emit_av(qs, kc - 1)
                    if kc == 8 and qs > 0:
                        emit_binv(qs - 1)
                emit_av(qs, 31)
            emit_binv(NQS - 1)

        @blk.scalar
        def _(s, p=p, qk_cnt=qk_cnt, av_cnt=av_cnt, prev=prev_pair_av):
            for qs in range(NQS):
                for kc in range(32):
                    sc = pAB if kc % 2 == 0 else pCD
                    s.wait_ge(sT.h, qk_cnt[(qs, kc)])
                    if kc >= 2:
                        s.wait_ge(sT.h, av_cnt[(qs, kc - 2)])
                    elif qs > 0:
                        s.wait_ge(sT.h, av_cnt[(qs - 1, 30 + kc)])
                    elif p > 0:
                        s.wait_ge(sT.h, prev[30 + kc])
                    sS.inc(s.activation(expS[:, kc % 2, :], sc[:], AF.Exp,
                                        bias=bneg_sb[:], scale=0.125))

        @blk.vector
        def _(v, p=p, av_cnt=av_cnt, binv_mm=binv_mm):
            for qs in range(NQS):
                u = p * NQS + qs
                avp = (pR2, pR3)[u % 2]
                smp = (pR1, pR4)[u % 2]
                v.wait_ge(sT.h, av_cnt[(qs, 31)])
                with nc.allow_low_precision(reason="softmax normalizer feeds bf16 matmul"):
                    sV.inc(v.reciprocal(rec_bf[:], smp[0:33, :]))
                v.wait_ge(sT.h, binv_mm[qs])
                if u >= 2:
                    v.wait_ge(sSP.h, av_spill[u - 2] if qs < 2 else spill_n[qs - 2])
                v.tensor_copy(binv_sb[:], smp[:, 0:512])
                sV.inc(v.tensor_tensor(avstage[:, qs % 2, :], avp[:], binv_sb[:], ALU.mult))

        @blk.sync
        def _(sy, p=p, norm_v=norm_v):
            for qs in range(NQS):
                sy.wait_ge(sV.h, norm_v[qs])
                av_spill[p * NQS + qs] = sSP.inc(sy.dma_start(
                    attnv_dram[128 * p:128 * p + 128, qs * 512:qs * 512 + 512],
                    avstage[:, qs % 2, :]), 16)

        prev_pair_av = {30: av_cnt[(NQS - 1, 30)], 31: av_cnt[(NQS - 1, 31)]}
        prev_exp = {30: exp_cnt[(NQS - 1, 30)], 31: exp_cnt[(NQS - 1, 31)]}
        last_norm_v[0] = norm_v[NQS - 1]
        assert sT.n == T0 + NQS * 65
        assert sS.n == S0 + NQS * 32
        assert sV.n == V0 + NQS * 2
        pair_t_done[p] = sT.n
        if p + 2 < NP:
            emit_kv_load(p + 2, pair_t_done[p])

    des.close()
    wes = ExitStack()
    av_ring = sb(wes, "av_ring", [128, 2, DC, 512], BF16)

    def emit_wo():
        T0, V0, KV0 = sT.n, sV.n, sKV.n
        wo_mm = {i: T0 + i + 1 for i in range(NQS * DC)}
        wo_ev = {i: V0 + i + 1 for i in range(NQS * DC)}
        av_ld = {qn: KV0 + 16 * (qn + 1) for qn in range(NQS)}

        @blk.sync
        def _(sy):
            for qn in range(NQS):
                if qn == 0:
                    sy.wait_ge(sSP.h, av_spill[NP * NQS - 1])
                if qn >= 2:
                    sy.wait_ge(sT.h, wo_mm[(qn - 2) * DC + DC - 1])
                src_ap = attnv_dram[:, qn * 512:qn * 512 + 512].rearrange("(k p) t -> p k t", p=128)
                sKV.inc(sy.dma_start(av_ring[:, qn % 2, :, :], src_ap), 16)

        @blk.tensor
        def _(t):
            sW.wait(t)
            for qn in range(NQS):
                t.wait_ge(sKV.h, av_ld[qn])
                for m in range(DC):
                    i = qn * DC + m
                    psu = pAB if i % 2 == 0 else pCD
                    if i < 2:
                        t.wait_ge(sV.h, last_norm_v[0])
                    else:
                        t.wait_ge(sV.h, wo_ev[i - 2])
                    for k in range(DC):
                        t.matmul(psu[:, 0:512], wq_sb[:, k, 128 * m:128 * m + 128],
                                 av_ring[:, qn % 2, k, :], start=(k == 0), stop=False)
                    sT.inc(t.matmul(psu[:, 0:512], b_sb[0:1, 128 * m:128 * m + 128],
                                    ones_row[0:1, :], start=False, stop=True))

        @blk.vector
        def _(v):
            for qn in range(NQS):
                for m in range(DC):
                    i = qn * DC + m
                    psu = pAB if i % 2 == 0 else pCD
                    v.wait_ge(sT.h, wo_mm[i])
                    xsl = x_sb[:, m, qn * 512:qn * 512 + 512]
                    sV.inc(v.tensor_tensor(xsl, xsl, psu[:, 0:512], ALU.add))

    emit_wo()
    wes.close()
    aes.close()

    emit_ffn_era("2")

    @blk.sync
    def _(sy):
        sV.wait(sy)
        sOUT.wait(sy)
        sOUT.inc(sy.dma_start(out_t.rearrange("(c p) t -> p c t", p=128), x_sb[:]), 16)
        sOUT.wait(sy)

    es.close()
    return nc


_NC_CACHE = None


def kernel(L0_attn_x, L0_attn_Wq, L0_attn_bq, L0_attn_Wk, L0_attn_bk,
           L0_attn_Wv, L0_attn_bv, L0_attn_Wo, L0_attn_bo,
           L0_ffn_W_fc, L0_ffn_W_proj):
    global _NC_CACHE
    x = np.asarray(L0_attn_x, dtype=np.float32)
    B, S, _ = x.shape
    assert (B, S) == (4, 4096)

    wqT = np.ascontiguousarray(np.asarray(L0_attn_Wq, np.float32).T)
    wkT = np.ascontiguousarray(np.asarray(L0_attn_Wk, np.float32).T)
    wvT = np.ascontiguousarray(np.asarray(L0_attn_Wv, np.float32).T)
    woT = np.ascontiguousarray(np.asarray(L0_attn_Wo, np.float32).T)
    wfcT = np.asarray(L0_ffn_W_fc, np.float32).T
    wfcb = np.ascontiguousarray(wfcT.reshape(DC, 128, FC, 128).transpose(2, 1, 0, 3))
    wprojT = np.asarray(L0_ffn_W_proj, np.float32).T
    wpb = np.ascontiguousarray(wprojT.reshape(2, 16, 128, DC, 128).transpose(3, 0, 2, 1, 4))
    brow = np.ascontiguousarray(np.concatenate(
        [np.asarray(b, np.float32).reshape(-1)
         for b in (L0_attn_bq, L0_attn_bk, L0_attn_bv, L0_attn_bo)])[None, :])
    cstv = np.full((128, 1), EPS, np.float32)

    if _NC_CACHE is None:
        _NC_CACHE = build_program()
    nc = _NC_CACHE

    in_maps = []
    for c in range(8):
        b, h = c // 2, c % 2
        xs = np.ascontiguousarray(x[b, h * T:(h + 1) * T, :].T)
        in_maps.append(dict(x_t=xs, wqT=wqT, wkT=wkT, wvT=wvT, woT=woT,
                            wfcb=wfcb, wpb=wpb, brow=brow, cst=cstv))

    res = run_bass_kernel_spmd(nc, in_maps, core_ids=list(range(8)))
    out = np.empty((B, S, D), np.float32)
    for c in range(8):
        b, h = c // 2, c % 2
        out[b, h * T:(h + 1) * T, :] = res.results[c]["out_t"].T
    return out


# version 17
# speedup vs baseline: 1.0515x; 1.0467x over previous
import numpy as np
from contextlib import ExitStack

import concourse.bass as bass
import concourse.mybir as mybir
from concourse.bass_utils import run_bass_kernel_spmd

F32 = mybir.dt.float32
BF16 = mybir.dt.bfloat16
AF = mybir.ActivationFunctionType
ALU = mybir.AluOpType
AX = mybir.AxisListType

D = 1024
DC = 8
T = 2048
FF = 4096
FC = 32
NP = 8
QS = 512
NQS = T // QS
NTC = T // 128
EPS = 1e-5


class Sem:
    def __init__(self, es, nc, name):
        self.h = es.enter_context(nc.semaphore(name))
        self.n = 0

    def inc(self, instr, k=1):
        instr.then_inc(self.h, k)
        self.n += k
        return self.n

    def wait(self, eng, n=None):
        eng.wait_ge(self.h, self.n if n is None else n)


def build_program(debug=False):
    nc = bass.Bass("TRN2", target_bir_lowering=False)

    x_t = nc.dram_tensor("x_t", [D, T], F32, kind="ExternalInput").ap()
    wqT = nc.dram_tensor("wqT", [128, DC, D], F32, kind="ExternalInput").ap()
    wkT = nc.dram_tensor("wkT", [128, DC, D], F32, kind="ExternalInput").ap()
    wvT = nc.dram_tensor("wvT", [128, DC, D], F32, kind="ExternalInput").ap()
    woT = nc.dram_tensor("woT", [128, DC, D], F32, kind="ExternalInput").ap()
    wfcb = nc.dram_tensor("wfcb", [FC // 2, 128, 2, DC, 128], F32, kind="ExternalInput").ap()
    wpb = nc.dram_tensor("wpb", [DC, 2, 128, 16, 128], F32, kind="ExternalInput").ap()
    brow = nc.dram_tensor("brow", [1, 4 * D], F32, kind="ExternalInput").ap()
    cst = nc.dram_tensor("cst", [128, 1], F32, kind="ExternalInput").ap()
    out_t = nc.dram_tensor("out_t", [D, T], F32, kind="ExternalOutput").ap()
    cc_in = nc.dram_tensor("cc_in", [2048, 2048], BF16, kind="Internal").ap()
    cc_out = nc.dram_tensor("cc_out", [2, 2048, 2048], BF16, kind="Internal").ap()
    attnv_dram = nc.dram_tensor("attnv_dram", [D, T], BF16, kind="Internal").ap()

    es = ExitStack()

    def sb(stack, name, shape, dt):
        return stack.enter_context(nc.sbuf_tensor(name, shape, dt))

    x_sb = sb(es, "x_sb", [128, DC, T], F32)
    ln_sb = sb(es, "ln_sb", [128, DC, T], BF16)
    ones_col = sb(es, "ones_col", [128, 1], BF16)
    ones_row = sb(es, "ones_row", [1, 512], BF16)
    masks = sb(es, "masks", [128, DC, 16], BF16)
    b_sb = sb(es, "b_sb", [1, D], BF16)
    eps_sb = sb(es, "eps_sb", [128, 1], F32)
    bneg_sb = sb(es, "bneg_sb", [128, 1], F32)
    rows_sb = sb(es, "rows_sb", [1, 2, 512], F32)
    ab_bf = sb(es, "ab_bf", [1, 1024], BF16)
    ones33 = sb(es, "ones33", [33, 64], BF16)
    s16_row = sb(es, "s16_row", [1, 128], BF16)
    m44_row = sb(es, "m44_row", [1, 128], BF16)
    nmx = sb(es, "nmx", [16, 1], F32)
    nmx2 = sb(es, "nmx2", [16, 1], F32)
    nmx3 = sb(es, "nmx3", [16, 1], F32)
    xb_sb = sb(es, "xb_sb", [128, 2, 512], BF16)
    xsq_sb = sb(es, "xsq_sb", [128, 2, 512], BF16)
    tmp_sb = sb(es, "tmp_sb", [128, 512], F32)
    brow_f = sb(es, "brow_f", [1, 16], F32)
    t32a = sb(es, "t32a", [32, 32], F32)
    t32b = sb(es, "t32b", [32, 32], F32)
    brow_b = sb(es, "brow_b", [1, 16], BF16)

    pAB = es.enter_context(nc.psum_tensor("pAB", [128, 1024], F32))
    pCD = es.enter_context(nc.psum_tensor("pCD", [128, 1024], F32))
    pR1 = es.enter_context(nc.psum_tensor("pR1", [128, 512], F32))
    pR2 = es.enter_context(nc.psum_tensor("pR2", [128, 512], F32))
    pR3 = es.enter_context(nc.psum_tensor("pR3", [128, 512], F32))
    pR4 = es.enter_context(nc.psum_tensor("pR4", [128, 512], F32))

    sIN = Sem(es, nc, "sIN")
    sW = Sem(es, nc, "sW")
    sT = Sem(es, nc, "sT")
    sV = Sem(es, nc, "sV")
    sS = Sem(es, nc, "sS")
    sSP = Sem(es, nc, "sSP")
    sCC = Sem(es, nc, "sCC")
    sKV = Sem(es, nc, "sKV")
    sOUT = Sem(es, nc, "sOUT")

    blk = es.enter_context(nc.Block())

    @blk.sync
    def _(sy):
        sIN.inc(sy.dma_start(x_sb[:], x_t.rearrange("(c p) t -> p c t", p=128)), 16)
        sIN.inc(sy.dma_start(eps_sb[:], cst), 16)

    @blk.vector
    def _(v):
        v.memset(ones_col[:], 1.0)
        v.memset(ones_row[:], 1.0)
        v.memset(ones33[0:1, :], 1.0)
        v.memset(ones33[32:33, :], 1.0)
        v.memset(s16_row[:], 0.125)
        v.memset(m44_row[:], -44.0)
        v.memset(masks[:], 0.0)
        for c in range(DC):
            v.memset(masks[0:64, c, 2 * c:2 * c + 1], 1.0)
        last = None
        for c in range(DC):
            last = v.memset(masks[64:128, c, 2 * c + 1:2 * c + 2], 1.0)
        sV.inc(last)

    def load_bias(r):
        @blk.gpsimd
        def _(g):
            sT.wait(g)
            sW.inc(g.dma_start(b_sb[:], brow[0:1, r * D:(r + 1) * D]), 16)

    rowA = rows_sb[0:1, 0, :]
    rowB = rows_sb[0:1, 1, :]
    rowC = tmp_sb[0:1, :]

    def emit_ln(qq):
        q0 = qq * 512
        T0, V0, S0 = sT.n, sV.n, sS.n
        ln_mm = {c: T0 + c + 1 for c in range(DC)}
        cp_done = {c: V0 + c + 1 for c in range(DC)}
        sq_done = {c: S0 + c + 1 for c in range(DC)}

        @blk.vector
        def _(v):
            if qq == 0:
                sIN.wait(v)
            if sOUT.n > 0:
                sOUT.wait(v)
            for c in range(DC):
                if c >= 2:
                    v.wait_ge(sT.h, ln_mm[c - 2])
                sV.inc(v.tensor_copy(xb_sb[:, c % 2, :], x_sb[:, c, q0:q0 + 512]))

        @blk.scalar
        def _(s):
            for c in range(DC):
                s.wait_ge(sV.h, cp_done[c])
                if c >= 2:
                    s.wait_ge(sT.h, ln_mm[c - 2])
                sS.inc(s.activation(xsq_sb[:, c % 2, :], xb_sb[:, c % 2, :], AF.Square))
            s.wait_ge(sV.h, V0 + DC + 1)
            sS.inc(s.activation(rowC, rowB, AF.Sqrt, bias=eps_sb[0:1, :]))

        @blk.tensor
        def _(t):
            for c in range(DC):
                t.wait_ge(sS.h, sq_done[c])
                if c == 0:
                    t.wait_ge(sV.h, V0)
                t.matmul(pR1[0:1, :], ones_col[:], xb_sb[:, c % 2, :], start=(c == 0), stop=(c == DC - 1))
                sT.inc(t.matmul(pR2[0:1, :], ones_col[:], xsq_sb[:, c % 2, :],
                                start=(c == 0), stop=(c == DC - 1)))
            t.wait_ge(sV.h, V0 + DC + 2)
            t.matmul(pAB[:, 0:512], ones_row[0:1, 0:128], ab_bf[0:1, 0:512], start=True, stop=True)
            sT.inc(t.matmul(pAB[:, 512:1024], ones_row[0:1, 0:128], ab_bf[0:1, 512:1024],
                            start=True, stop=True))

        @blk.vector
        def _(v):
            v.wait_ge(sT.h, T0 + DC)
            v.tensor_scalar_mul(rowA, pR1[0:1, :], 1.0 / D)
            v.tensor_tensor(rowB, rowA, rowA, ALU.mult)
            sV.inc(v.scalar_tensor_tensor(rowB, pR2[0:1, :], 1.0 / D, rowB, ALU.mult, ALU.subtract))
            v.wait_ge(sS.h, S0 + DC + 1)
            v.reciprocal(rowB, rowC)
            v.scalar_tensor_tensor(rowC, rowA, -1.0, rowB, ALU.mult, ALU.mult)
            v.tensor_copy(ab_bf[0:1, 0:512], rowB)
            sV.inc(v.tensor_copy(ab_bf[0:1, 512:1024], rowC))
            v.wait_ge(sT.h, T0 + DC + 1)
            last = None
            for c in range(DC):
                v.tensor_tensor(tmp_sb[:], x_sb[:, c, q0:q0 + 512], pAB[:, 0:512], ALU.mult)
                last = v.tensor_tensor(ln_sb[:, c, q0:q0 + 512], tmp_sb[:], pAB[:, 512:1024], ALU.add)
            sV.inc(last)

        assert sT.n == T0 + DC + 1
        assert sV.n == V0 + DC + 3
        assert sS.n == S0 + DC + 1

    def emit_ffn_era(tag):
        fes = ExitStack()
        wfc_ring = sb(fes, "wfc_ring" + tag, [128, 2, 2, DC, 128], BF16)
        h_sb = sb(fes, "h_sb" + tag, [128, FC, 1024], BF16)
        wp_ring = sb(fes, "wp_ring" + tag, [128, 2, 16, 128], BF16)
        r_sb = sb(fes, "r_sb" + tag, [128, 2, 1024], F32)
        s_scan = sb(fes, "s_scan" + tag, [128, 512], F32)
        mem_scan = sb(fes, "mem_scan" + tag, [128, 512], F32)
        acc_scan = sb(fes, "acc_scan" + tag, [128, 512], F32)

        era_t0 = sT.n
        era_v0 = sV.n

        def yvb(dc, sub):
            r = dc % 4
            if r == 0:
                return pAB[:, sub * 512:sub * 512 + 512]
            if r == 1:
                return pCD[:, sub * 512:sub * 512 + 512]
            if r == 2:
                return (pR1 if sub == 0 else pR2)[:, :]
            return (pR3 if sub == 0 else pR4)[:, :]

        for qq in range(NQS):
            emit_ln(qq)

        for hh in range(2):
            q0 = hh * 1024
            T0, V0, S0, W0 = sT.n, sV.n, sS.n, sW.n
            u_cnt = {f: T0 + f + 1 for f in range(FC)}
            wp_mm = {si: T0 + FC + si + 1 for si in range(16)}
            h_cnt = {f: V0 + f + 1 for f in range(FC)}
            scan_cnt = {dc: V0 + FC + dc + 1 for dc in range(DC)}
            relu_cnt = {f: S0 + f + 1 for f in range(FC)}
            wfc_ld = {p: W0 + 16 * (p + 1) for p in range(FC // 2)}
            wp_ld = {si: W0 + 16 * (FC // 2) + 16 * (si + 1) for si in range(16)}

            @blk.gpsimd
            def _(sy, hh=hh, T0=T0, u_cnt=u_cnt, wp_mm=wp_mm):
                for p in range(FC // 2):
                    if p >= 2:
                        sy.wait_ge(sT.h, u_cnt[2 * p - 3])
                    elif hh == 1:
                        sy.wait_ge(sT.h, T0)
                    else:
                        if era_t0 > 0:
                            sy.wait_ge(sT.h, era_t0)
                        if era_v0 > 0:
                            sy.wait_ge(sV.h, era_v0)
                    sW.inc(sy.dma_start(wfc_ring[:, p % 2, :, :, :], wfcb[p]), 16)
                for si in range(16):
                    dc, fh = si // 2, si % 2
                    if si >= 2:
                        sy.wait_ge(sT.h, wp_mm[si - 2])
                    elif hh == 1:
                        sy.wait_ge(sT.h, T0)
                    elif era_v0 > 0:
                        sy.wait_ge(sV.h, era_v0)
                    sW.inc(sy.dma_start(wp_ring[:, si % 2, :, :], wpb[dc, fh]), 16)

            @blk.tensor
            def _(t, q0=q0, V0=V0, u_cnt=u_cnt, wp_mm=wp_mm, h_cnt=h_cnt,
                  scan_cnt=scan_cnt, relu_cnt=relu_cnt, wfc_ld=wfc_ld, wp_ld=wp_ld):
                for f in range(FC):
                    psu = pAB if f % 2 == 0 else pCD
                    t.wait_ge(sW.h, wfc_ld[f // 2])
                    if f < 2:
                        t.wait_ge(sV.h, V0)
                    else:
                        t.wait_ge(sV.h, h_cnt[f - 2])
                    mm = None
                    for sub in range(2):
                        for k in range(DC):
                            mm = t.matmul(psu[:, sub * 512:sub * 512 + 512],
                                          wfc_ring[:, (f // 2) % 2, f % 2, k, :],
                                          ln_sb[:, k, q0 + sub * 512:q0 + sub * 512 + 512],
                                          start=(k == 0), stop=(k == DC - 1))
                    sT.inc(mm)
                for si in range(16):
                    dc, fh = si // 2, si % 2
                    t.wait_ge(sW.h, wp_ld[si])
                    if fh == 0:
                        if dc < 2:
                            t.wait_ge(sV.h, h_cnt[FC - 1])
                            t.wait_ge(sS.h, relu_cnt[FC - 1])
                        elif dc >= 4:
                            t.wait_ge(sV.h, scan_cnt[dc - 4])
                    mm = None
                    for sub in range(2):
                        ysl = yvb(dc, sub)
                        for fi in range(16):
                            f = 16 * fh + fi
                            mm = t.matmul(ysl, wp_ring[:, si % 2, fi, :],
                                          h_sb[:, f, sub * 512:sub * 512 + 512],
                                          start=(f == 0), stop=(f == FC - 1))
                    sT.inc(mm)

            @blk.scalar
            def _(s, u_cnt=u_cnt, h_cnt=h_cnt):
                for f in range(FC):
                    psu = pAB if f % 2 == 0 else pCD
                    s.wait_ge(sT.h, u_cnt[f])
                    if f >= 2:
                        s.wait_ge(sV.h, h_cnt[f - 2])
                    sS.inc(s.activation(r_sb[:, f % 2, :], psu[:], AF.Relu))

            def scan_ops(eng, y, xsl, ss, mem, acc, fused):
                eng.tensor_single_scalar(acc, y, 1.0, ALU.is_gt)
                if fused:
                    eng.scalar_tensor_tensor(mem, y, 1.9, acc, ALU.mult, ALU.subtract)
                else:
                    eng.tensor_scalar_mul(mem, y, 1.9)
                    eng.tensor_tensor(mem, mem, acc, ALU.subtract)
                for _step in range(3):
                    eng.tensor_single_scalar(ss, mem, 1.0, ALU.is_gt)
                    eng.tensor_tensor(acc, acc, ss, ALU.add)
                    if fused:
                        eng.scalar_tensor_tensor(mem, mem, 0.9, y, ALU.mult, ALU.add)
                    else:
                        eng.tensor_scalar_mul(mem, mem, 0.9)
                        eng.tensor_tensor(mem, mem, y, ALU.add)
                    eng.tensor_tensor(mem, mem, ss, ALU.subtract)
                eng.tensor_single_scalar(ss, mem, 1.0, ALU.is_gt)
                eng.tensor_tensor(acc, acc, ss, ALU.add)
                if fused:
                    return eng.scalar_tensor_tensor(xsl, acc, 0.1, xsl, ALU.mult, ALU.add)
                eng.tensor_scalar_mul(ss, acc, 0.1)
                return eng.tensor_tensor(xsl, xsl, ss, ALU.add)

            @blk.vector
            def _(v, q0=q0, relu_cnt=relu_cnt, wp_mm=wp_mm):
                for f in range(FC):
                    psu = pAB if f % 2 == 0 else pCD
                    v.wait_ge(sS.h, relu_cnt[f])
                    sV.inc(v.tensor_tensor(h_sb[:, f, :], psu[:], r_sb[:, f % 2, :], ALU.mult))
                for dc in range(DC):
                    v.wait_ge(sT.h, wp_mm[2 * dc + 1])
                    last = None
                    for sub in range(2):
                        y = yvb(dc, sub)
                        xsl = x_sb[:, dc, q0 + sub * 512:q0 + sub * 512 + 512]
                        last = scan_ops(v, y, xsl, s_scan[:], mem_scan[:], acc_scan[:], True)
                    sV.inc(last)

            assert sT.n == T0 + FC + 16
            assert sV.n == V0 + FC + DC
            assert sS.n == S0 + FC
            assert sW.n == W0 + 16 * (FC // 2 + 16)

        fes.close()

    emit_ffn_era("1")

    aes = ExitStack()
    q_sb = sb(aes, "q_sb", [128, DC, T], BF16)
    wq_sb = sb(aes, "wq_sb", [128, DC, D], BF16)
    bes = ExitStack()
    kstage = sb(bes, "kstage", [128, 4, 512], BF16)
    v_sb = sb(bes, "v_sb", [128, NTC, D], BF16)

    for qq in range(NQS):
        emit_ln(qq)

    def load_w(dram):
        @blk.gpsimd
        def _(g):
            sT.wait(g)
            sW.inc(g.dma_start(wq_sb[:], dram), 16)

    def emit_proj_q():
        T0, S0 = sT.n, sS.n
        evac = {i: S0 + i + 1 for i in range(DC * NQS)}
        mmc = {i: T0 + i + 1 for i in range(DC * NQS)}

        @blk.tensor
        def _(t):
            sW.wait(t)
            for i in range(DC * NQS):
                m, qn = divmod(i, NQS)
                psu = pAB if i % 2 == 0 else pCD
                if i < 2:
                    sV.wait(t)
                else:
                    t.wait_ge(sS.h, evac[i - 2])
                for k in range(DC):
                    t.matmul(psu[:, 0:512], wq_sb[:, k, 128 * m:128 * m + 128],
                             ln_sb[:, k, qn * 512:qn * 512 + 512], start=(k == 0), stop=False)
                sT.inc(t.matmul(psu[:, 0:512], b_sb[0:1, 128 * m:128 * m + 128],
                                ones_row[0:1, :], start=False, stop=True))

        @blk.scalar
        def _(s):
            for i in range(DC * NQS):
                m, qn = divmod(i, NQS)
                psu = pAB if i % 2 == 0 else pCD
                s.wait_ge(sT.h, mmc[i])
                sS.inc(s.copy(q_sb[:, m, qn * 512:qn * 512 + 512], psu[:, 0:512]))

    load_w(wqT)
    load_bias(0)
    emit_proj_q()

    load_w(wkT)
    load_bias(1)

    def emit_proj_k():
        T0, S0, V0, SP0 = sT.n, sS.n, sV.n, sSP.n
        kred = {qn: V0 + qn + 1 for qn in range(NQS)}
        kevac = {i: S0 + 2 * i + 1 for i in range(DC * NQS)}
        ksq = {i: S0 + 2 * i + 2 for i in range(DC * NQS)}
        proj_mm = {i: T0 + 2 * i + 1 for i in range(DC * NQS)}
        knorm_mm = {i: T0 + 2 * i + 2 for i in range(DC * NQS)}
        kspill = {i: SP0 + 16 * (i + 1) for i in range(DC * NQS)}

        @blk.tensor
        def _(t):
            sW.wait(t)
            for qn in range(NQS):
                for m in range(DC):
                    i = qn * DC + m
                    psu = pAB if i % 2 == 0 else pCD
                    if i < 2:
                        sV.wait(t)
                    else:
                        t.wait_ge(sS.h, kevac[i - 2])
                    for k in range(DC):
                        t.matmul(psu[:, 0:512], wq_sb[:, k, 128 * m:128 * m + 128],
                                 ln_sb[:, k, qn * 512:qn * 512 + 512], start=(k == 0), stop=False)
                    sT.inc(t.matmul(psu[:, 0:512], b_sb[0:1, 128 * m:128 * m + 128],
                                    ones_row[0:1, :], start=False, stop=True))
                    t.wait_ge(sS.h, ksq[i])
                    if i == 0:
                        sV.wait(t)
                    elif m == 0:
                        t.wait_ge(sV.h, kred[qn - 1])
                    sT.inc(t.matmul(pR1[0:16, :], masks[:, m, :], xsq_sb[:, i % 2, :],
                                    start=(m == 0), stop=(m == DC - 1)))

        @blk.scalar
        def _(s):
            for i in range(DC * NQS):
                psu = pAB if i % 2 == 0 else pCD
                s.wait_ge(sT.h, proj_mm[i])
                if i >= 4:
                    s.wait_ge(sSP.h, kspill[i - 4])
                if i >= 2:
                    s.wait_ge(sT.h, knorm_mm[i - 2])
                sS.inc(s.copy(kstage[:, i % 4, :], psu[:, 0:512]))
                sS.inc(s.activation(xsq_sb[:, i % 2, :], kstage[:, i % 4, :], AF.Square))

        @blk.sync
        def _(sy):
            for qn in range(NQS):
                for m in range(DC):
                    i = qn * DC + m
                    sy.wait_ge(sS.h, kevac[i])
                    sSP.inc(sy.dma_start(cc_in[128 * m:128 * m + 128, qn * 512:qn * 512 + 512],
                                         kstage[:, i % 4, :]), 16)

        @blk.vector
        def _(v):
            for qn in range(NQS):
                v.wait_ge(sT.h, knorm_mm[qn * DC + DC - 1])
                if qn == 0:
                    sV.inc(v.tensor_reduce(nmx2[:], pR1[0:16, :], AX.X, ALU.max))
                else:
                    v.tensor_reduce(nmx3[:], pR1[0:16, :], AX.X, ALU.max)
                    sV.inc(v.tensor_tensor(nmx2[:], nmx2[:], nmx3[:], ALU.max))

    emit_proj_k()

    load_w(wvT)
    load_bias(2)

    def emit_proj_v():
        T0, S0 = sT.n, sS.n
        vevac = {tc: S0 + tc + 1 for tc in range(NTC)}
        vmm = {tc: T0 + tc + 1 for tc in range(NTC)}

        @blk.tensor
        def _(t):
            sW.wait(t)
            for tc in range(NTC):
                psu = pAB if tc % 2 == 0 else pCD
                if tc < 2:
                    sV.wait(t)
                else:
                    t.wait_ge(sS.h, vevac[tc - 2])
                mm = None
                for half in range(2):
                    for k in range(DC):
                        t.matmul(psu[:, half * 512:half * 512 + 512],
                                 ln_sb[:, k, tc * 128:tc * 128 + 128],
                                 wq_sb[:, k, half * 512:half * 512 + 512],
                                 start=(k == 0), stop=False)
                    mm = t.matmul(psu[:, half * 512:half * 512 + 512],
                                  ones_row[0:1, 0:128],
                                  b_sb[0:1, half * 512:half * 512 + 512],
                                  start=False, stop=True)
                sT.inc(mm)

        @blk.scalar
        def _(s):
            for tc in range(NTC):
                psu = pAB if tc % 2 == 0 else pCD
                s.wait_ge(sT.h, vmm[tc])
                s.copy(v_sb[:, tc, 0:512], psu[:, 0:512])
                sS.inc(s.copy(v_sb[:, tc, 512:1024], psu[:, 512:1024]))

    emit_proj_v()

    @blk.sync
    def _(sy):
        sS.wait(sy)
        vdst = cc_in[1024:2048, :].rearrange("r (two d) -> (r two) d", two=2) \
                                  .rearrange("(tc p) d -> p tc d", p=128)
        sSP.inc(sy.dma_start(vdst, v_sb[:]), 16)

    @blk.gpsimd
    def _(g):
        sSP.wait(g)
        sCC.inc(g.collective_compute(
            "AllGather", ALU.bypass, ins=[cc_in], outs=[cc_out],
            replica_groups=[[0, 1], [2, 3], [4, 5], [6, 7]]))

    def emit_norms_q():
        T0, S0 = sT.n, sS.n
        V0n = sV.n
        qred = {qs: V0n + qs + 1 for qs in range(NQS)}
        sqq = {}
        nm_mm = {}
        i = 0
        for qs in range(NQS):
            for c in range(DC):
                sqq[(qs, c)] = S0 + i + 1
                nm_mm[(qs, c)] = T0 + i + 1
                i += 1

        @blk.scalar
        def _(s):
            sT.wait(s)
            first = True
            for qs in range(NQS):
                for c in range(DC):
                    if not first and not (qs == 0 and c == 1):
                        prev = (qs, c - 2) if c >= 2 else (qs - 1, DC - 2 + c)
                        s.wait_ge(sT.h, nm_mm[prev])
                    first = False
                    sS.inc(s.activation(xsq_sb[:, c % 2, :],
                                        q_sb[:, c, qs * 512:qs * 512 + 512], AF.Square))

        @blk.tensor
        def _(t):
            for qs in range(NQS):
                for c in range(DC):
                    t.wait_ge(sS.h, sqq[(qs, c)])
                    if qs == 0 and c == 0:
                        sV.wait(t)
                    elif c == 0:
                        t.wait_ge(sV.h, qred[qs - 1])
                    sT.inc(t.matmul(pR1[0:16, :], masks[:, c, :], xsq_sb[:, c % 2, :],
                                    start=(c == 0), stop=(c == DC - 1)))

        @blk.vector
        def _(v):
            for qs in range(NQS):
                v.wait_ge(sT.h, nm_mm[(qs, DC - 1)])
                if qs == 0:
                    sV.inc(v.tensor_reduce(nmx[:], pR1[0:16, :], AX.X, ALU.max))
                else:
                    v.tensor_reduce(nmx3[:], pR1[0:16, :], AX.X, ALU.max)
                    sV.inc(v.tensor_tensor(nmx[:], nmx[:], nmx3[:], ALU.max))

    emit_norms_q()

    @blk.vector
    def _(v):
        sV.wait(v)
        sV.inc(v.tensor_tensor(nmx[:], nmx[:], nmx2[:], ALU.mult))

    @blk.scalar
    def _(s):
        sV.wait(s)
        sS.inc(s.activation(nmx3[:], nmx[:], AF.Sqrt))

    @blk.vector
    def _(v):
        sS.wait(v)
        v.memset(t32a[:], 0.0)
        sV.inc(v.tensor_copy(t32a[0:16, 0:1], nmx3[:]))
        sV.wait(v)
        sV.inc(v.transpose(t32b[:], t32a[:]))
        sV.wait(v)
        sV.inc(v.tensor_copy(brow_f[:], t32b[0:1, 0:16]))

    @blk.scalar
    def _(s):
        sV.wait(s)
        sS.inc(s.copy(brow_b[:], t32b[0:1, 0:16]))

    @blk.tensor
    def _(t):
        sS.wait(t)
        t.matmul(pR2[:, 0:16], s16_row[:], brow_b[:], start=True, stop=False)
        sT.inc(t.matmul(pR2[:, 0:16], m44_row[:], ones_row[0:1, 0:16], start=False, stop=True))

    @blk.vector
    def _(v):
        sT.wait(v)
        sV.inc(v.tensor_reduce(bneg_sb[:], pR2[:, 0:16], AX.X, ALU.max, negate=True))

    bneg_ready = sV.n
    bes.close()

    load_w(woT)
    load_bias(3)

    des = ExitStack()
    kp_sb = sb(des, "kp_sb", [128, 2, 2, T], BF16)
    vp_sb = sb(des, "vp_sb", [128, 2, 2, NTC, 128], BF16)
    expS = sb(des, "expS", [128, 2, 1024], BF16)
    rec_bf = sb(des, "rec_bf", [33, 512], BF16)
    binv_sb = sb(des, "binv_sb", [128, 512], BF16)
    avstage = sb(des, "avstage", [128, 2, 512], BF16)

    kv_cnt = {}
    pair_t_done = {}
    last_norm_v = [bneg_ready]
    av_spill = {}
    prev_pair_av = {}

    def emit_kv_load(p, prev_t):
        buf = p % 2

        @blk.sync
        def _(sy):
            sCC.wait(sy)
            if prev_t:
                sy.wait_ge(sT.h, prev_t)
            for s_ in range(2):
                sKV.inc(sy.dma_start(kp_sb[:, buf, s_, :], cc_out[s_, 128 * p:128 * p + 128, :]), 16)
                vsrc = cc_out[s_, 1024:2048, :].rearrange("r (two d) -> (r two) d", two=2)
                vsrc = vsrc[:, 128 * p:128 * p + 128].rearrange("(tc p2) d -> p2 tc d", p2=128)
                sKV.inc(sy.dma_start(vp_sb[:, buf, s_, :, :], vsrc), 16)
            kv_cnt[p] = sKV.n

    emit_kv_load(0, 0)
    emit_kv_load(1, 0)

    units_norm = {}
    prev_exp = {}

    for p in range(NP):
        buf = p % 2
        T0, S0, V0, SP0 = sT.n, sS.n, sV.n, sSP.n
        qk_cnt = {}
        av_cnt = {}
        binv_mm = {}
        n = T0
        for qs in range(NQS):
            for kc in range(32):
                n += 1
                qk_cnt[(qs, kc)] = n
                if kc >= 1:
                    n += 1
                    av_cnt[(qs, kc - 1)] = n
                if kc == 8 and qs > 0:
                    n += 1
                    binv_mm[qs - 1] = n
            n += 1
            av_cnt[(qs, 31)] = n
        n += 1
        binv_mm[NQS - 1] = n
        exp_cnt = {(qs, kc): S0 + qs * 32 + kc + 1 for qs in range(NQS) for kc in range(32)}
        rec_v = {qs: V0 + qs * 2 + 1 for qs in range(NQS)}
        norm_v = {qs: V0 + qs * 2 + 2 for qs in range(NQS)}
        spill_n = {qs: SP0 + 16 * (qs + 1) for qs in range(NQS)}
        for qs in range(NQS):
            units_norm[p * NQS + qs] = norm_v[qs]

        @blk.tensor
        def _(t, p=p, buf=buf, qk_cnt=qk_cnt, av_cnt=av_cnt, binv_mm=binv_mm,
              exp_cnt=exp_cnt, rec_v=rec_v, norm_v=norm_v, prevx=prev_exp):
            def emit_av(qs, kc):
                u = p * NQS + qs
                avp = (pR2, pR3)[u % 2]
                smp = (pR1, pR4)[u % 2]
                t.wait_ge(sS.h, exp_cnt[(qs, kc)])
                if kc == 0 and u >= 2:
                    t.wait_ge(sV.h, units_norm[u - 2])
                slot, j = kc // 16, kc % 16
                t.matmul(avp[0:64, :], vp_sb[:, buf, slot, j, 0:64],
                         expS[:, kc % 2, 0:512], start=(kc == 0), stop=(kc == 31))
                t.matmul(avp[64:128, :], vp_sb[:, buf, slot, j, 64:128],
                         expS[:, kc % 2, 512:1024], start=(kc == 0), stop=(kc == 31),
                         tile_position=(0, 64), skip_group_check=True)
                t.matmul(smp[0:1, :], ones_col[:], expS[:, kc % 2, 0:512],
                         start=(kc == 0), stop=(kc == 31))
                sT.inc(t.matmul(smp[32:33, :], ones_col[:], expS[:, kc % 2, 512:1024],
                                start=(kc == 0), stop=(kc == 31), tile_position=(0, 32),
                                skip_group_check=True))

            def emit_binv(qs):
                u = p * NQS + qs
                smp = (pR1, pR4)[u % 2]
                t.wait_ge(sV.h, rec_v[qs])
                t.matmul(smp[0:64, 0:512], ones33[0:1, :], rec_bf[0:1, :], start=True, stop=False)
                sT.inc(t.matmul(smp[64:128, 0:512], ones33[32:33, :], rec_bf[32:33, :],
                                start=True, stop=True, tile_position=(32, 64),
                                skip_group_check=True))

            for qs in range(NQS):
                q0 = qs * 512
                for kc in range(32):
                    sc = pAB if kc % 2 == 0 else pCD
                    slot, j = kc // 16, kc % 16
                    if kc == 0 and qs == 0:
                        t.wait_ge(sKV.h, kv_cnt[p])
                        if p == 0:
                            t.wait_ge(sV.h, last_norm_v[0])
                    if kc >= 2:
                        t.wait_ge(sS.h, exp_cnt[(qs, kc - 2)])
                    elif qs > 0:
                        t.wait_ge(sS.h, exp_cnt[(qs - 1, 30 + kc)])
                    elif p > 0:
                        s_prev = prevx.get(30 + kc)
                        if s_prev:
                            t.wait_ge(sS.h, s_prev)
                    t.matmul(sc[:, 0:512], kp_sb[0:64, buf, slot, 128 * j:128 * j + 128],
                             q_sb[0:64, p, q0:q0 + 512], start=True, stop=True)
                    sT.inc(t.matmul(sc[:, 512:1024], kp_sb[64:128, buf, slot, 128 * j:128 * j + 128],
                                    q_sb[64:128, p, q0:q0 + 512], start=True, stop=True,
                                    tile_position=(64, 0)))
                    if kc >= 1:
                        emit_av(qs, kc - 1)
                    if kc == 8 and qs > 0:
                        emit_binv(qs - 1)
                emit_av(qs, 31)
            emit_binv(NQS - 1)

        @blk.scalar
        def _(s, p=p, qk_cnt=qk_cnt, av_cnt=av_cnt, prev=prev_pair_av):
            for qs in range(NQS):
                for kc in range(32):
                    sc = pAB if kc % 2 == 0 else pCD
                    s.wait_ge(sT.h, qk_cnt[(qs, kc)])
                    if kc >= 2:
                        s.wait_ge(sT.h, av_cnt[(qs, kc - 2)])
                    elif qs > 0:
                        s.wait_ge(sT.h, av_cnt[(qs - 1, 30 + kc)])
                    elif p > 0:
                        s.wait_ge(sT.h, prev[30 + kc])
                    sS.inc(s.activation(expS[:, kc % 2, :], sc[:], AF.Exp,
                                        bias=bneg_sb[:], scale=0.125))

        @blk.vector
        def _(v, p=p, av_cnt=av_cnt, binv_mm=binv_mm):
            for qs in range(NQS):
                u = p * NQS + qs
                avp = (pR2, pR3)[u % 2]
                smp = (pR1, pR4)[u % 2]
                v.wait_ge(sT.h, av_cnt[(qs, 31)])
                with nc.allow_low_precision(reason="softmax normalizer feeds bf16 matmul"):
                    sV.inc(v.reciprocal(rec_bf[:], smp[0:33, :]))
                v.wait_ge(sT.h, binv_mm[qs])
                if u >= 2:
                    v.wait_ge(sSP.h, av_spill[u - 2] if qs < 2 else spill_n[qs - 2])
                v.tensor_copy(binv_sb[:], smp[:, 0:512])
                sV.inc(v.tensor_tensor(avstage[:, qs % 2, :], avp[:], binv_sb[:], ALU.mult))

        @blk.sync
        def _(sy, p=p, norm_v=norm_v):
            for qs in range(NQS):
                sy.wait_ge(sV.h, norm_v[qs])
                av_spill[p * NQS + qs] = sSP.inc(sy.dma_start(
                    attnv_dram[128 * p:128 * p + 128, qs * 512:qs * 512 + 512],
                    avstage[:, qs % 2, :]), 16)

        prev_pair_av = {30: av_cnt[(NQS - 1, 30)], 31: av_cnt[(NQS - 1, 31)]}
        prev_exp = {30: exp_cnt[(NQS - 1, 30)], 31: exp_cnt[(NQS - 1, 31)]}
        last_norm_v[0] = norm_v[NQS - 1]
        assert sT.n == T0 + NQS * 65
        assert sS.n == S0 + NQS * 32
        assert sV.n == V0 + NQS * 2
        pair_t_done[p] = sT.n
        if p + 2 < NP:
            emit_kv_load(p + 2, pair_t_done[p])

    des.close()
    wes = ExitStack()
    av_ring = sb(wes, "av_ring", [128, 2, DC, 512], BF16)

    def emit_wo():
        T0, V0, KV0 = sT.n, sV.n, sKV.n
        wo_mm = {i: T0 + i + 1 for i in range(NQS * DC)}
        wo_ev = {i: V0 + i + 1 for i in range(NQS * DC)}
        av_ld = {qn: KV0 + 16 * (qn + 1) for qn in range(NQS)}

        @blk.sync
        def _(sy):
            for qn in range(NQS):
                if qn == 0:
                    sy.wait_ge(sSP.h, av_spill[NP * NQS - 1])
                if qn >= 2:
                    sy.wait_ge(sT.h, wo_mm[(qn - 2) * DC + DC - 1])
                src_ap = attnv_dram[:, qn * 512:qn * 512 + 512].rearrange("(k p) t -> p k t", p=128)
                sKV.inc(sy.dma_start(av_ring[:, qn % 2, :, :], src_ap), 16)

        @blk.tensor
        def _(t):
            sW.wait(t)
            for qn in range(NQS):
                t.wait_ge(sKV.h, av_ld[qn])
                for m in range(DC):
                    i = qn * DC + m
                    psu = pAB if i % 2 == 0 else pCD
                    if i < 2:
                        t.wait_ge(sV.h, last_norm_v[0])
                    else:
                        t.wait_ge(sV.h, wo_ev[i - 2])
                    for k in range(DC):
                        t.matmul(psu[:, 0:512], wq_sb[:, k, 128 * m:128 * m + 128],
                                 av_ring[:, qn % 2, k, :], start=(k == 0), stop=False)
                    sT.inc(t.matmul(psu[:, 0:512], b_sb[0:1, 128 * m:128 * m + 128],
                                    ones_row[0:1, :], start=False, stop=True))

        @blk.vector
        def _(v):
            for qn in range(NQS):
                for m in range(DC):
                    i = qn * DC + m
                    psu = pAB if i % 2 == 0 else pCD
                    v.wait_ge(sT.h, wo_mm[i])
                    xsl = x_sb[:, m, qn * 512:qn * 512 + 512]
                    sV.inc(v.tensor_tensor(xsl, xsl, psu[:, 0:512], ALU.add))

    emit_wo()
    wes.close()
    aes.close()

    emit_ffn_era("2")

    @blk.sync
    def _(sy):
        sV.wait(sy)
        sOUT.wait(sy)
        sOUT.inc(sy.dma_start(out_t.rearrange("(c p) t -> p c t", p=128), x_sb[:]), 16)
        sOUT.wait(sy)

    es.close()
    return nc


_NC_CACHE = None


def kernel(L0_attn_x, L0_attn_Wq, L0_attn_bq, L0_attn_Wk, L0_attn_bk,
           L0_attn_Wv, L0_attn_bv, L0_attn_Wo, L0_attn_bo,
           L0_ffn_W_fc, L0_ffn_W_proj):
    global _NC_CACHE
    x = np.asarray(L0_attn_x, dtype=np.float32)
    B, S, _ = x.shape
    assert (B, S) == (4, 4096)

    wqT = np.ascontiguousarray(np.asarray(L0_attn_Wq, np.float32).T.reshape(DC, 128, D).transpose(1, 0, 2))
    wkT = np.ascontiguousarray(np.asarray(L0_attn_Wk, np.float32).T.reshape(DC, 128, D).transpose(1, 0, 2))
    wvT = np.ascontiguousarray(np.asarray(L0_attn_Wv, np.float32).T.reshape(DC, 128, D).transpose(1, 0, 2))
    woT = np.ascontiguousarray(np.asarray(L0_attn_Wo, np.float32).T.reshape(DC, 128, D).transpose(1, 0, 2))
    wfcT = np.asarray(L0_ffn_W_fc, np.float32).T
    wfcb = np.ascontiguousarray(wfcT.reshape(DC, 128, FC, 128).transpose(2, 1, 0, 3).reshape(FC // 2, 2, 128, DC, 128).transpose(0, 2, 1, 3, 4))
    wprojT = np.asarray(L0_ffn_W_proj, np.float32).T
    wpb = np.ascontiguousarray(wprojT.reshape(2, 16, 128, DC, 128).transpose(3, 0, 2, 1, 4))
    brow = np.ascontiguousarray(np.concatenate(
        [np.asarray(b, np.float32).reshape(-1)
         for b in (L0_attn_bq, L0_attn_bk, L0_attn_bv, L0_attn_bo)])[None, :])
    cstv = np.full((128, 1), EPS, np.float32)

    if _NC_CACHE is None:
        _NC_CACHE = build_program()
    nc = _NC_CACHE

    in_maps = []
    for c in range(8):
        b, h = c // 2, c % 2
        xs = np.ascontiguousarray(x[b, h * T:(h + 1) * T, :].T)
        in_maps.append(dict(x_t=xs, wqT=wqT, wkT=wkT, wvT=wvT, woT=woT,
                            wfcb=wfcb, wpb=wpb, brow=brow, cst=cstv))

    res = run_bass_kernel_spmd(nc, in_maps, core_ids=list(range(8)))
    out = np.empty((B, S, D), np.float32)
    for c in range(8):
        b, h = c // 2, c % 2
        out[b, h * T:(h + 1) * T, :] = res.results[c]["out_t"].T
    return out


# version 18
# speedup vs baseline: 1.0726x; 1.0201x over previous
import numpy as np
from contextlib import ExitStack

import concourse.bass as bass
import concourse.mybir as mybir
from concourse.bass_utils import run_bass_kernel_spmd

F32 = mybir.dt.float32
BF16 = mybir.dt.bfloat16
AF = mybir.ActivationFunctionType
ALU = mybir.AluOpType
AX = mybir.AxisListType

D = 1024
DC = 8
T = 2048
FF = 4096
FC = 32
NP = 8
QS = 512
NQS = T // QS
NTC = T // 128
EPS = 1e-5


class Sem:
    def __init__(self, es, nc, name):
        self.h = es.enter_context(nc.semaphore(name))
        self.n = 0

    def inc(self, instr, k=1):
        instr.then_inc(self.h, k)
        self.n += k
        return self.n

    def wait(self, eng, n=None):
        eng.wait_ge(self.h, self.n if n is None else n)


def build_program(debug=False):
    nc = bass.Bass("TRN2", target_bir_lowering=False)

    x_t = nc.dram_tensor("x_t", [D, T], F32, kind="ExternalInput").ap()
    wqT = nc.dram_tensor("wqT", [128, DC, D], F32, kind="ExternalInput").ap()
    wkT = nc.dram_tensor("wkT", [128, DC, D], F32, kind="ExternalInput").ap()
    wvT = nc.dram_tensor("wvT", [128, DC, D], F32, kind="ExternalInput").ap()
    woT = nc.dram_tensor("woT", [128, DC, D], F32, kind="ExternalInput").ap()
    wfcb = nc.dram_tensor("wfcb", [FC // 2, 128, 2, DC, 128], F32, kind="ExternalInput").ap()
    wpb = nc.dram_tensor("wpb", [DC, 2, 128, 16, 128], F32, kind="ExternalInput").ap()
    brow = nc.dram_tensor("brow", [1, 4 * D], F32, kind="ExternalInput").ap()
    cst = nc.dram_tensor("cst", [128, 1], F32, kind="ExternalInput").ap()
    out_t = nc.dram_tensor("out_t", [D, T], F32, kind="ExternalOutput").ap()
    cc_in = nc.dram_tensor("cc_in", [2048, 2048], BF16, kind="Internal").ap()
    cc_out = nc.dram_tensor("cc_out", [2, 2048, 2048], BF16, kind="Internal").ap()
    attnv_dram = nc.dram_tensor("attnv_dram", [D, T], BF16, kind="Internal").ap()

    es = ExitStack()

    def sb(stack, name, shape, dt):
        return stack.enter_context(nc.sbuf_tensor(name, shape, dt))

    x_sb = sb(es, "x_sb", [128, DC, T], F32)
    ln_sb = sb(es, "ln_sb", [128, DC, T], BF16)
    ones_col = sb(es, "ones_col", [128, 1], BF16)
    ones_row = sb(es, "ones_row", [1, 512], BF16)
    masks = sb(es, "masks", [128, DC, 16], BF16)
    b_sb = sb(es, "b_sb", [1, D], BF16)
    eps_sb = sb(es, "eps_sb", [128, 1], F32)
    bneg_sb = sb(es, "bneg_sb", [128, 1], F32)
    rows_sb = sb(es, "rows_sb", [1, 2, 512], F32)
    ab_bf = sb(es, "ab_bf", [1, 1024], BF16)
    ones33 = sb(es, "ones33", [33, 64], BF16)
    s16_row = sb(es, "s16_row", [1, 128], BF16)
    m44_row = sb(es, "m44_row", [1, 128], BF16)
    nmx = sb(es, "nmx", [16, 1], F32)
    nmx2 = sb(es, "nmx2", [16, 1], F32)
    nmx3 = sb(es, "nmx3", [16, 1], F32)
    xb_sb = sb(es, "xb_sb", [128, 2, 512], BF16)
    xsq_sb = sb(es, "xsq_sb", [128, 2, 512], BF16)
    tmp_sb = sb(es, "tmp_sb", [128, 512], F32)
    brow_f = sb(es, "brow_f", [1, 16], F32)
    t32a = sb(es, "t32a", [32, 32], F32)
    t32b = sb(es, "t32b", [32, 32], F32)
    brow_b = sb(es, "brow_b", [1, 16], BF16)

    pAB = es.enter_context(nc.psum_tensor("pAB", [128, 1024], F32))
    pCD = es.enter_context(nc.psum_tensor("pCD", [128, 1024], F32))
    pR1 = es.enter_context(nc.psum_tensor("pR1", [128, 512], F32))
    pR2 = es.enter_context(nc.psum_tensor("pR2", [128, 512], F32))
    pR3 = es.enter_context(nc.psum_tensor("pR3", [128, 512], F32))
    pR4 = es.enter_context(nc.psum_tensor("pR4", [128, 512], F32))

    sIN = Sem(es, nc, "sIN")
    sW = Sem(es, nc, "sW")
    sT = Sem(es, nc, "sT")
    sV = Sem(es, nc, "sV")
    sS = Sem(es, nc, "sS")
    sSP = Sem(es, nc, "sSP")
    sCC = Sem(es, nc, "sCC")
    sKV = Sem(es, nc, "sKV")
    sOUT = Sem(es, nc, "sOUT")

    blk = es.enter_context(nc.Block())

    @blk.sync
    def _(sy):
        sIN.inc(sy.dma_start(eps_sb[:], cst), 16)
        for qq in range(NQS):
            sIN.inc(sy.dma_start(x_sb[:, :, qq * 512:qq * 512 + 512],
                                 x_t[:, qq * 512:qq * 512 + 512]
                                 .rearrange("(c p) t -> p c t", p=128)), 16)

    @blk.vector
    def _(v):
        v.memset(ones_col[:], 1.0)
        v.memset(ones_row[:], 1.0)
        v.memset(ones33[0:1, :], 1.0)
        v.memset(ones33[32:33, :], 1.0)
        v.memset(s16_row[:], 0.125)
        v.memset(m44_row[:], -44.0)
        v.memset(masks[:], 0.0)
        for c in range(DC):
            v.memset(masks[0:64, c, 2 * c:2 * c + 1], 1.0)
        last = None
        for c in range(DC):
            last = v.memset(masks[64:128, c, 2 * c + 1:2 * c + 2], 1.0)
        sV.inc(last)

    def load_bias(r):
        @blk.gpsimd
        def _(g):
            sT.wait(g)
            sW.inc(g.dma_start(b_sb[:], brow[0:1, r * D:(r + 1) * D]), 16)

    rowA = rows_sb[0:1, 0, :]
    rowB = rows_sb[0:1, 1, :]
    rowC = tmp_sb[0:1, :]

    ln_apply = []

    def emit_ln(qq, xgate=None):
        q0 = qq * 512
        T0, V0, S0 = sT.n, sV.n, sS.n
        ln_apply.append(V0 + DC + 3)
        ln_mm = {c: T0 + c + 1 for c in range(DC)}
        cp_done = {c: V0 + c + 1 for c in range(DC)}
        sq_done = {c: S0 + c + 1 for c in range(DC)}

        @blk.vector
        def _(v):
            if xgate is not None:
                v.wait_ge(sIN.h, xgate)
            if sOUT.n > 0:
                sOUT.wait(v)
            for c in range(DC):
                if c >= 2:
                    v.wait_ge(sT.h, ln_mm[c - 2])
                sV.inc(v.tensor_copy(xb_sb[:, c % 2, :], x_sb[:, c, q0:q0 + 512]))

        @blk.scalar
        def _(s):
            for c in range(DC):
                s.wait_ge(sV.h, cp_done[c])
                if c >= 2:
                    s.wait_ge(sT.h, ln_mm[c - 2])
                sS.inc(s.activation(xsq_sb[:, c % 2, :], xb_sb[:, c % 2, :], AF.Square))
            s.wait_ge(sV.h, V0 + DC + 1)
            if xgate is not None and qq == 0:
                s.wait_ge(sIN.h, 16)
            sS.inc(s.activation(rowC, rowB, AF.Sqrt, bias=eps_sb[0:1, :]))

        @blk.tensor
        def _(t):
            for c in range(DC):
                t.wait_ge(sS.h, sq_done[c])
                if c == 0:
                    t.wait_ge(sV.h, V0)
                t.matmul(pR1[0:1, :], ones_col[:], xb_sb[:, c % 2, :], start=(c == 0), stop=(c == DC - 1))
                sT.inc(t.matmul(pR2[0:1, :], ones_col[:], xsq_sb[:, c % 2, :],
                                start=(c == 0), stop=(c == DC - 1)))
            t.wait_ge(sV.h, V0 + DC + 2)
            t.matmul(pR3[:, :], ones_row[0:1, 0:128], ab_bf[0:1, 0:512], start=True, stop=True)
            sT.inc(t.matmul(pR4[:, :], ones_row[0:1, 0:128], ab_bf[0:1, 512:1024],
                            start=True, stop=True))

        @blk.vector
        def _(v):
            v.wait_ge(sT.h, T0 + DC)
            v.tensor_scalar_mul(rowA, pR1[0:1, :], 1.0 / D)
            v.tensor_tensor(rowB, rowA, rowA, ALU.mult)
            sV.inc(v.scalar_tensor_tensor(rowB, pR2[0:1, :], 1.0 / D, rowB, ALU.mult, ALU.subtract))
            v.wait_ge(sS.h, S0 + DC + 1)
            v.reciprocal(rowB, rowC)
            v.scalar_tensor_tensor(rowC, rowA, -1.0, rowB, ALU.mult, ALU.mult)
            v.tensor_copy(ab_bf[0:1, 0:512], rowB)
            sV.inc(v.tensor_copy(ab_bf[0:1, 512:1024], rowC))
            v.wait_ge(sT.h, T0 + DC + 1)
            last = None
            for c in range(DC):
                v.tensor_tensor(tmp_sb[:], x_sb[:, c, q0:q0 + 512], pR3[:, :], ALU.mult)
                last = v.tensor_tensor(ln_sb[:, c, q0:q0 + 512], tmp_sb[:], pR4[:, :], ALU.add)
            sV.inc(last)

        assert sT.n == T0 + DC + 1
        assert sV.n == V0 + DC + 3
        assert sS.n == S0 + DC + 1

    def emit_ffn_era(tag):
        fes = ExitStack()
        wfc_ring = sb(fes, "wfc_ring" + tag, [128, 2, 2, DC, 128], BF16)
        h_sb = sb(fes, "h_sb" + tag, [128, FC, 1024], BF16)
        wp_ring = sb(fes, "wp_ring" + tag, [128, 2, 16, 128], BF16)
        r_sb = sb(fes, "r_sb" + tag, [128, 2, 1024], F32)
        s_scan = sb(fes, "s_scan" + tag, [128, 512], F32)
        mem_scan = sb(fes, "mem_scan" + tag, [128, 512], F32)
        acc_scan = sb(fes, "acc_scan" + tag, [128, 512], F32)

        era_t0 = sT.n
        era_v0 = sV.n

        def yvb(dc, sub):
            r = dc % 4
            if r == 0:
                return pAB[:, sub * 512:sub * 512 + 512]
            if r == 1:
                return pCD[:, sub * 512:sub * 512 + 512]
            if r == 2:
                return (pR1 if sub == 0 else pR2)[:, :]
            return (pR3 if sub == 0 else pR4)[:, :]

        apply_base = len(ln_apply)
        for qq in range(NQS):
            emit_ln(qq, xgate=(32 + 16 * qq) if tag == "1" else None)
        era_apply = ln_apply[apply_base:apply_base + NQS]
        prev_scan = None

        for hh in range(2):
            q0 = hh * 1024
            T0, V0, S0, W0 = sT.n, sV.n, sS.n, sW.n
            u_cnt = {f: T0 + f + 1 for f in range(FC)}
            wp_mm = {si: T0 + FC + si + 1 for si in range(16)}
            h_cnt = {f: V0 + f + 1 for f in range(FC)}
            scan_cnt = {dc: V0 + FC + dc + 1 for dc in range(DC)}
            relu_cnt = {f: S0 + f + 1 for f in range(FC)}
            wfc_ld = {p: W0 + 16 * (p + 1) for p in range(FC // 2)}
            wp_ld = {si: W0 + 16 * (FC // 2) + 16 * (si + 1) for si in range(16)}

            @blk.gpsimd
            def _(sy, hh=hh, T0=T0, u_cnt=u_cnt, wp_mm=wp_mm):
                for p in range(FC // 2):
                    if p >= 2:
                        sy.wait_ge(sT.h, u_cnt[2 * p - 3])
                    elif hh == 1:
                        sy.wait_ge(sT.h, T0)
                    else:
                        if era_t0 > 0:
                            sy.wait_ge(sT.h, era_t0)
                        if era_v0 > 0:
                            sy.wait_ge(sV.h, era_v0)
                    sW.inc(sy.dma_start(wfc_ring[:, p % 2, :, :, :], wfcb[p]), 16)
                for si in range(16):
                    dc, fh = si // 2, si % 2
                    if si >= 2:
                        sy.wait_ge(sT.h, wp_mm[si - 2])
                    elif hh == 1:
                        sy.wait_ge(sT.h, T0)
                    elif era_v0 > 0:
                        sy.wait_ge(sV.h, era_v0)
                    sW.inc(sy.dma_start(wp_ring[:, si % 2, :, :], wpb[dc, fh]), 16)

            @blk.tensor
            def _(t, q0=q0, V0=V0, u_cnt=u_cnt, wp_mm=wp_mm, h_cnt=h_cnt,
                  scan_cnt=scan_cnt, relu_cnt=relu_cnt, wfc_ld=wfc_ld, wp_ld=wp_ld,
                  hh=hh, prev_scan=prev_scan):
                for f in range(FC):
                    psu = pAB if f % 2 == 0 else pCD
                    t.wait_ge(sW.h, wfc_ld[f // 2])
                    if f < 2:
                        if hh == 0:
                            if era_v0 > 0:
                                t.wait_ge(sV.h, era_v0)
                        else:
                            t.wait_ge(sV.h, prev_scan[4 + f])
                    else:
                        t.wait_ge(sV.h, h_cnt[f - 2])
                    mm = None
                    for sub in range(2):
                        if f == 0:
                            t.wait_ge(sV.h, era_apply[2 * hh + sub])
                        for k in range(DC):
                            mm = t.matmul(psu[:, sub * 512:sub * 512 + 512],
                                          wfc_ring[:, (f // 2) % 2, f % 2, k, :],
                                          ln_sb[:, k, q0 + sub * 512:q0 + sub * 512 + 512],
                                          start=(k == 0), stop=(k == DC - 1))
                    sT.inc(mm)
                for si in range(16):
                    dc, fh = si // 2, si % 2
                    t.wait_ge(sW.h, wp_ld[si])
                    if fh == 0:
                        if dc < 2:
                            t.wait_ge(sV.h, h_cnt[FC - 1])
                            t.wait_ge(sS.h, relu_cnt[FC - 1])
                        elif dc >= 4:
                            t.wait_ge(sV.h, scan_cnt[dc - 4])
                    mm = None
                    for sub in range(2):
                        ysl = yvb(dc, sub)
                        for fi in range(16):
                            f = 16 * fh + fi
                            mm = t.matmul(ysl, wp_ring[:, si % 2, fi, :],
                                          h_sb[:, f, sub * 512:sub * 512 + 512],
                                          start=(f == 0), stop=(f == FC - 1))
                    sT.inc(mm)

            @blk.scalar
            def _(s, u_cnt=u_cnt, h_cnt=h_cnt):
                for f in range(FC):
                    psu = pAB if f % 2 == 0 else pCD
                    s.wait_ge(sT.h, u_cnt[f])
                    if f >= 2:
                        s.wait_ge(sV.h, h_cnt[f - 2])
                    sS.inc(s.activation(r_sb[:, f % 2, :], psu[:], AF.Relu))

            def scan_ops(eng, y, xsl, ss, mem, acc, fused):
                eng.tensor_single_scalar(acc, y, 1.0, ALU.is_gt)
                if fused:
                    eng.scalar_tensor_tensor(mem, y, 1.9, acc, ALU.mult, ALU.subtract)
                else:
                    eng.tensor_scalar_mul(mem, y, 1.9)
                    eng.tensor_tensor(mem, mem, acc, ALU.subtract)
                for _step in range(3):
                    eng.tensor_single_scalar(ss, mem, 1.0, ALU.is_gt)
                    eng.tensor_tensor(acc, acc, ss, ALU.add)
                    if fused:
                        eng.scalar_tensor_tensor(mem, mem, 0.9, y, ALU.mult, ALU.add)
                    else:
                        eng.tensor_scalar_mul(mem, mem, 0.9)
                        eng.tensor_tensor(mem, mem, y, ALU.add)
                    eng.tensor_tensor(mem, mem, ss, ALU.subtract)
                eng.tensor_single_scalar(ss, mem, 1.0, ALU.is_gt)
                eng.tensor_tensor(acc, acc, ss, ALU.add)
                if fused:
                    return eng.scalar_tensor_tensor(xsl, acc, 0.1, xsl, ALU.mult, ALU.add)
                eng.tensor_scalar_mul(ss, acc, 0.1)
                return eng.tensor_tensor(xsl, xsl, ss, ALU.add)

            @blk.vector
            def _(v, q0=q0, relu_cnt=relu_cnt, wp_mm=wp_mm):
                for f in range(FC):
                    psu = pAB if f % 2 == 0 else pCD
                    v.wait_ge(sS.h, relu_cnt[f])
                    sV.inc(v.tensor_tensor(h_sb[:, f, :], psu[:], r_sb[:, f % 2, :], ALU.mult))
                for dc in range(DC):
                    v.wait_ge(sT.h, wp_mm[2 * dc + 1])
                    last = None
                    for sub in range(2):
                        y = yvb(dc, sub)
                        xsl = x_sb[:, dc, q0 + sub * 512:q0 + sub * 512 + 512]
                        last = scan_ops(v, y, xsl, s_scan[:], mem_scan[:], acc_scan[:], True)
                    sV.inc(last)

            assert sT.n == T0 + FC + 16
            assert sV.n == V0 + FC + DC
            assert sS.n == S0 + FC
            assert sW.n == W0 + 16 * (FC // 2 + 16)
            prev_scan = scan_cnt

        fes.close()

    emit_ffn_era("1")

    aes = ExitStack()
    q_sb = sb(aes, "q_sb", [128, DC, T], BF16)
    wq_sb = sb(aes, "wq_sb", [128, DC, D], BF16)
    bes = ExitStack()
    kstage = sb(bes, "kstage", [128, 4, 512], BF16)
    v_sb = sb(bes, "v_sb", [128, NTC, D], BF16)

    ln2_base = len(ln_apply)
    for qq in range(NQS):
        emit_ln(qq)
    ln2_apply = ln_apply[ln2_base:ln2_base + NQS]

    def load_w(dram):
        @blk.gpsimd
        def _(g):
            sT.wait(g)
            sW.inc(g.dma_start(wq_sb[:], dram), 16)

    def emit_proj_q():
        T0, S0 = sT.n, sS.n
        evac = {i: S0 + i + 1 for i in range(DC * NQS)}
        mmc = {i: T0 + i + 1 for i in range(DC * NQS)}

        @blk.tensor
        def _(t):
            sW.wait(t)
            for i in range(DC * NQS):
                m, qn = divmod(i, NQS)
                psu = pAB if i % 2 == 0 else pCD
                if i < NQS:
                    t.wait_ge(sV.h, ln2_apply[qn])
                if i >= 2:
                    t.wait_ge(sS.h, evac[i - 2])
                for k in range(DC):
                    t.matmul(psu[:, 0:512], wq_sb[:, k, 128 * m:128 * m + 128],
                             ln_sb[:, k, qn * 512:qn * 512 + 512], start=(k == 0), stop=False)
                sT.inc(t.matmul(psu[:, 0:512], b_sb[0:1, 128 * m:128 * m + 128],
                                ones_row[0:1, :], start=False, stop=True))

        @blk.scalar
        def _(s):
            for i in range(DC * NQS):
                m, qn = divmod(i, NQS)
                psu = pAB if i % 2 == 0 else pCD
                s.wait_ge(sT.h, mmc[i])
                sS.inc(s.copy(q_sb[:, m, qn * 512:qn * 512 + 512], psu[:, 0:512]))

    load_w(wqT)
    load_bias(0)
    emit_proj_q()

    load_w(wkT)
    load_bias(1)

    def emit_proj_k():
        T0, S0, V0, SP0 = sT.n, sS.n, sV.n, sSP.n
        kred = {qn: V0 + qn + 1 for qn in range(NQS)}
        kevac = {i: S0 + 2 * i + 1 for i in range(DC * NQS)}
        ksq = {i: S0 + 2 * i + 2 for i in range(DC * NQS)}
        proj_mm = {i: T0 + 2 * i + 1 for i in range(DC * NQS)}
        knorm_mm = {i: T0 + 2 * i + 2 for i in range(DC * NQS)}
        kspill = {i: SP0 + 16 * (i + 1) for i in range(DC * NQS)}

        @blk.tensor
        def _(t):
            sW.wait(t)
            for qn in range(NQS):
                for m in range(DC):
                    i = qn * DC + m
                    psu = pAB if i % 2 == 0 else pCD
                    if i < 2:
                        sV.wait(t)
                    else:
                        t.wait_ge(sS.h, kevac[i - 2])
                    for k in range(DC):
                        t.matmul(psu[:, 0:512], wq_sb[:, k, 128 * m:128 * m + 128],
                                 ln_sb[:, k, qn * 512:qn * 512 + 512], start=(k == 0), stop=False)
                    sT.inc(t.matmul(psu[:, 0:512], b_sb[0:1, 128 * m:128 * m + 128],
                                    ones_row[0:1, :], start=False, stop=True))
                    t.wait_ge(sS.h, ksq[i])
                    if i == 0:
                        sV.wait(t)
                    elif m == 0:
                        t.wait_ge(sV.h, kred[qn - 1])
                    sT.inc(t.matmul(pR1[0:16, :], masks[:, m, :], xsq_sb[:, i % 2, :],
                                    start=(m == 0), stop=(m == DC - 1)))

        @blk.scalar
        def _(s):
            for i in range(DC * NQS):
                psu = pAB if i % 2 == 0 else pCD
                s.wait_ge(sT.h, proj_mm[i])
                if i >= 4:
                    s.wait_ge(sSP.h, kspill[i - 4])
                if i >= 2:
                    s.wait_ge(sT.h, knorm_mm[i - 2])
                sS.inc(s.copy(kstage[:, i % 4, :], psu[:, 0:512]))
                sS.inc(s.activation(xsq_sb[:, i % 2, :], kstage[:, i % 4, :], AF.Square))

        @blk.sync
        def _(sy):
            for qn in range(NQS):
                for m in range(DC):
                    i = qn * DC + m
                    sy.wait_ge(sS.h, kevac[i])
                    sSP.inc(sy.dma_start(cc_in[128 * m:128 * m + 128, qn * 512:qn * 512 + 512],
                                         kstage[:, i % 4, :]), 16)

        @blk.vector
        def _(v):
            for qn in range(NQS):
                v.wait_ge(sT.h, knorm_mm[qn * DC + DC - 1])
                if qn == 0:
                    sV.inc(v.tensor_reduce(nmx2[:], pR1[0:16, :], AX.X, ALU.max))
                else:
                    v.tensor_reduce(nmx3[:], pR1[0:16, :], AX.X, ALU.max)
                    sV.inc(v.tensor_tensor(nmx2[:], nmx2[:], nmx3[:], ALU.max))

    emit_proj_k()

    load_w(wvT)
    load_bias(2)

    def emit_proj_v():
        T0, S0 = sT.n, sS.n
        vevac = {tc: S0 + tc + 1 for tc in range(NTC)}
        vmm = {tc: T0 + tc + 1 for tc in range(NTC)}

        @blk.tensor
        def _(t):
            sW.wait(t)
            for tc in range(NTC):
                psu = pAB if tc % 2 == 0 else pCD
                if tc < 2:
                    sV.wait(t)
                else:
                    t.wait_ge(sS.h, vevac[tc - 2])
                mm = None
                for half in range(2):
                    for k in range(DC):
                        t.matmul(psu[:, half * 512:half * 512 + 512],
                                 ln_sb[:, k, tc * 128:tc * 128 + 128],
                                 wq_sb[:, k, half * 512:half * 512 + 512],
                                 start=(k == 0), stop=False)
                    mm = t.matmul(psu[:, half * 512:half * 512 + 512],
                                  ones_row[0:1, 0:128],
                                  b_sb[0:1, half * 512:half * 512 + 512],
                                  start=False, stop=True)
                sT.inc(mm)

        @blk.scalar
        def _(s):
            for tc in range(NTC):
                psu = pAB if tc % 2 == 0 else pCD
                s.wait_ge(sT.h, vmm[tc])
                s.copy(v_sb[:, tc, 0:512], psu[:, 0:512])
                sS.inc(s.copy(v_sb[:, tc, 512:1024], psu[:, 512:1024]))

    emit_proj_v()

    @blk.sync
    def _(sy):
        sS.wait(sy)
        vdst = cc_in[1024:2048, :].rearrange("r (two d) -> (r two) d", two=2) \
                                  .rearrange("(tc p) d -> p tc d", p=128)
        sSP.inc(sy.dma_start(vdst, v_sb[:]), 16)

    @blk.gpsimd
    def _(g):
        sSP.wait(g)
        sCC.inc(g.collective_compute(
            "AllGather", ALU.bypass, ins=[cc_in], outs=[cc_out],
            replica_groups=[[0, 1], [2, 3], [4, 5], [6, 7]]))

    def emit_norms_q():
        T0, S0 = sT.n, sS.n
        V0n = sV.n
        qred = {qs: V0n + qs + 1 for qs in range(NQS)}
        sqq = {}
        nm_mm = {}
        i = 0
        for qs in range(NQS):
            for c in range(DC):
                sqq[(qs, c)] = S0 + i + 1
                nm_mm[(qs, c)] = T0 + i + 1
                i += 1

        @blk.scalar
        def _(s):
            sT.wait(s)
            first = True
            for qs in range(NQS):
                for c in range(DC):
                    if not first and not (qs == 0 and c == 1):
                        prev = (qs, c - 2) if c >= 2 else (qs - 1, DC - 2 + c)
                        s.wait_ge(sT.h, nm_mm[prev])
                    first = False
                    sS.inc(s.activation(xsq_sb[:, c % 2, :],
                                        q_sb[:, c, qs * 512:qs * 512 + 512], AF.Square))

        @blk.tensor
        def _(t):
            for qs in range(NQS):
                for c in range(DC):
                    t.wait_ge(sS.h, sqq[(qs, c)])
                    if qs == 0 and c == 0:
                        sV.wait(t)
                    elif c == 0:
                        t.wait_ge(sV.h, qred[qs - 1])
                    sT.inc(t.matmul(pR1[0:16, :], masks[:, c, :], xsq_sb[:, c % 2, :],
                                    start=(c == 0), stop=(c == DC - 1)))

        @blk.vector
        def _(v):
            for qs in range(NQS):
                v.wait_ge(sT.h, nm_mm[(qs, DC - 1)])
                if qs == 0:
                    sV.inc(v.tensor_reduce(nmx[:], pR1[0:16, :], AX.X, ALU.max))
                else:
                    v.tensor_reduce(nmx3[:], pR1[0:16, :], AX.X, ALU.max)
                    sV.inc(v.tensor_tensor(nmx[:], nmx[:], nmx3[:], ALU.max))

    emit_norms_q()

    @blk.vector
    def _(v):
        sV.wait(v)
        sV.inc(v.tensor_tensor(nmx[:], nmx[:], nmx2[:], ALU.mult))

    @blk.scalar
    def _(s):
        sV.wait(s)
        sS.inc(s.activation(nmx3[:], nmx[:], AF.Sqrt))

    @blk.vector
    def _(v):
        sS.wait(v)
        v.memset(t32a[:], 0.0)
        sV.inc(v.tensor_copy(t32a[0:16, 0:1], nmx3[:]))
        sV.wait(v)
        sV.inc(v.transpose(t32b[:], t32a[:]))
        sV.wait(v)
        sV.inc(v.tensor_copy(brow_f[:], t32b[0:1, 0:16]))

    @blk.scalar
    def _(s):
        sV.wait(s)
        sS.inc(s.copy(brow_b[:], t32b[0:1, 0:16]))

    @blk.tensor
    def _(t):
        sS.wait(t)
        t.matmul(pR2[:, 0:16], s16_row[:], brow_b[:], start=True, stop=False)
        sT.inc(t.matmul(pR2[:, 0:16], m44_row[:], ones_row[0:1, 0:16], start=False, stop=True))

    @blk.vector
    def _(v):
        sT.wait(v)
        sV.inc(v.tensor_reduce(bneg_sb[:], pR2[:, 0:16], AX.X, ALU.max, negate=True))

    bneg_ready = sV.n
    bes.close()

    load_w(woT)
    load_bias(3)

    des = ExitStack()
    kp_sb = sb(des, "kp_sb", [128, 2, 2, T], BF16)
    vp_sb = sb(des, "vp_sb", [128, 2, 2, NTC, 128], BF16)
    expS = sb(des, "expS", [128, 2, 1024], BF16)
    rec_bf = sb(des, "rec_bf", [33, 512], BF16)
    binv_sb = sb(des, "binv_sb", [128, 512], BF16)
    avstage = sb(des, "avstage", [128, 2, 512], BF16)

    kv_cnt = {}
    pair_t_done = {}
    last_norm_v = [bneg_ready]
    av_spill = {}
    prev_pair_av = {}

    def emit_kv_load(p, prev_t):
        buf = p % 2

        @blk.sync
        def _(sy):
            sCC.wait(sy)
            if prev_t:
                sy.wait_ge(sT.h, prev_t)
            for s_ in range(2):
                sKV.inc(sy.dma_start(kp_sb[:, buf, s_, :], cc_out[s_, 128 * p:128 * p + 128, :]), 16)
                vsrc = cc_out[s_, 1024:2048, :].rearrange("r (two d) -> (r two) d", two=2)
                vsrc = vsrc[:, 128 * p:128 * p + 128].rearrange("(tc p2) d -> p2 tc d", p2=128)
                sKV.inc(sy.dma_start(vp_sb[:, buf, s_, :, :], vsrc), 16)
            kv_cnt[p] = sKV.n

    emit_kv_load(0, 0)
    emit_kv_load(1, 0)

    units_norm = {}
    prev_exp = {}

    for p in range(NP):
        buf = p % 2
        T0, S0, V0, SP0 = sT.n, sS.n, sV.n, sSP.n
        qk_cnt = {}
        av_cnt = {}
        binv_mm = {}
        n = T0
        for qs in range(NQS):
            for kc in range(32):
                n += 1
                qk_cnt[(qs, kc)] = n
                if kc >= 1:
                    n += 1
                    av_cnt[(qs, kc - 1)] = n
                if kc == 8 and qs > 0:
                    n += 1
                    binv_mm[qs - 1] = n
            n += 1
            av_cnt[(qs, 31)] = n
        n += 1
        binv_mm[NQS - 1] = n
        exp_cnt = {(qs, kc): S0 + qs * 32 + kc + 1 for qs in range(NQS) for kc in range(32)}
        rec_v = {qs: V0 + qs * 2 + 1 for qs in range(NQS)}
        norm_v = {qs: V0 + qs * 2 + 2 for qs in range(NQS)}
        spill_n = {qs: SP0 + 16 * (qs + 1) for qs in range(NQS)}
        for qs in range(NQS):
            units_norm[p * NQS + qs] = norm_v[qs]

        @blk.tensor
        def _(t, p=p, buf=buf, qk_cnt=qk_cnt, av_cnt=av_cnt, binv_mm=binv_mm,
              exp_cnt=exp_cnt, rec_v=rec_v, norm_v=norm_v, prevx=prev_exp):
            def emit_av(qs, kc):
                u = p * NQS + qs
                avp = (pR2, pR3)[u % 2]
                smp = (pR1, pR4)[u % 2]
                t.wait_ge(sS.h, exp_cnt[(qs, kc)])
                if kc == 0 and u >= 2:
                    t.wait_ge(sV.h, units_norm[u - 2])
                slot, j = kc // 16, kc % 16
                t.matmul(avp[0:64, :], vp_sb[:, buf, slot, j, 0:64],
                         expS[:, kc % 2, 0:512], start=(kc == 0), stop=(kc == 31))
                t.matmul(avp[64:128, :], vp_sb[:, buf, slot, j, 64:128],
                         expS[:, kc % 2, 512:1024], start=(kc == 0), stop=(kc == 31),
                         tile_position=(0, 64), skip_group_check=True)
                t.matmul(smp[0:1, :], ones_col[:], expS[:, kc % 2, 0:512],
                         start=(kc == 0), stop=(kc == 31))
                sT.inc(t.matmul(smp[32:33, :], ones_col[:], expS[:, kc % 2, 512:1024],
                                start=(kc == 0), stop=(kc == 31), tile_position=(0, 32),
                                skip_group_check=True))

            def emit_binv(qs):
                u = p * NQS + qs
                smp = (pR1, pR4)[u % 2]
                t.wait_ge(sV.h, rec_v[qs])
                t.matmul(smp[0:64, 0:512], ones33[0:1, :], rec_bf[0:1, :], start=True, stop=False)
                sT.inc(t.matmul(smp[64:128, 0:512], ones33[32:33, :], rec_bf[32:33, :],
                                start=True, stop=True, tile_position=(32, 64),
                                skip_group_check=True))

            for qs in range(NQS):
                q0 = qs * 512
                for kc in range(32):
                    sc = pAB if kc % 2 == 0 else pCD
                    slot, j = kc // 16, kc % 16
                    if kc == 0 and qs == 0:
                        t.wait_ge(sKV.h, kv_cnt[p])
                        if p == 0:
                            t.wait_ge(sV.h, last_norm_v[0])
                    if kc >= 2:
                        t.wait_ge(sS.h, exp_cnt[(qs, kc - 2)])
                    elif qs > 0:
                        t.wait_ge(sS.h, exp_cnt[(qs - 1, 30 + kc)])
                    elif p > 0:
                        s_prev = prevx.get(30 + kc)
                        if s_prev:
                            t.wait_ge(sS.h, s_prev)
                    t.matmul(sc[:, 0:512], kp_sb[0:64, buf, slot, 128 * j:128 * j + 128],
                             q_sb[0:64, p, q0:q0 + 512], start=True, stop=True)
                    sT.inc(t.matmul(sc[:, 512:1024], kp_sb[64:128, buf, slot, 128 * j:128 * j + 128],
                                    q_sb[64:128, p, q0:q0 + 512], start=True, stop=True,
                                    tile_position=(64, 0)))
                    if kc >= 1:
                        emit_av(qs, kc - 1)
                    if kc == 8 and qs > 0:
                        emit_binv(qs - 1)
                emit_av(qs, 31)
            emit_binv(NQS - 1)

        @blk.scalar
        def _(s, p=p, qk_cnt=qk_cnt, av_cnt=av_cnt, prev=prev_pair_av):
            for qs in range(NQS):
                for kc in range(32):
                    sc = pAB if kc % 2 == 0 else pCD
                    s.wait_ge(sT.h, qk_cnt[(qs, kc)])
                    if kc >= 2:
                        s.wait_ge(sT.h, av_cnt[(qs, kc - 2)])
                    elif qs > 0:
                        s.wait_ge(sT.h, av_cnt[(qs - 1, 30 + kc)])
                    elif p > 0:
                        s.wait_ge(sT.h, prev[30 + kc])
                    sS.inc(s.activation(expS[:, kc % 2, :], sc[:], AF.Exp,
                                        bias=bneg_sb[:], scale=0.125))

        @blk.vector
        def _(v, p=p, av_cnt=av_cnt, binv_mm=binv_mm):
            for qs in range(NQS):
                u = p * NQS + qs
                avp = (pR2, pR3)[u % 2]
                smp = (pR1, pR4)[u % 2]
                v.wait_ge(sT.h, av_cnt[(qs, 31)])
                with nc.allow_low_precision(reason="softmax normalizer feeds bf16 matmul"):
                    sV.inc(v.reciprocal(rec_bf[:], smp[0:33, :]))
                v.wait_ge(sT.h, binv_mm[qs])
                if u >= 2:
                    v.wait_ge(sSP.h, av_spill[u - 2] if qs < 2 else spill_n[qs - 2])
                v.tensor_copy(binv_sb[:], smp[:, 0:512])
                sV.inc(v.tensor_tensor(avstage[:, qs % 2, :], avp[:], binv_sb[:], ALU.mult))

        @blk.sync
        def _(sy, p=p, norm_v=norm_v):
            for qs in range(NQS):
                sy.wait_ge(sV.h, norm_v[qs])
                av_spill[p * NQS + qs] = sSP.inc(sy.dma_start(
                    attnv_dram[128 * p:128 * p + 128, qs * 512:qs * 512 + 512],
                    avstage[:, qs % 2, :]), 16)

        prev_pair_av = {30: av_cnt[(NQS - 1, 30)], 31: av_cnt[(NQS - 1, 31)]}
        prev_exp = {30: exp_cnt[(NQS - 1, 30)], 31: exp_cnt[(NQS - 1, 31)]}
        last_norm_v[0] = norm_v[NQS - 1]
        assert sT.n == T0 + NQS * 65
        assert sS.n == S0 + NQS * 32
        assert sV.n == V0 + NQS * 2
        pair_t_done[p] = sT.n
        if p + 2 < NP:
            emit_kv_load(p + 2, pair_t_done[p])

    des.close()
    wes = ExitStack()
    av_ring = sb(wes, "av_ring", [128, 2, DC, 512], BF16)

    def emit_wo():
        T0, V0, KV0 = sT.n, sV.n, sKV.n
        wo_mm = {i: T0 + i + 1 for i in range(NQS * DC)}
        wo_ev = {i: V0 + i + 1 for i in range(NQS * DC)}
        av_ld = {qn: KV0 + 16 * (qn + 1) for qn in range(NQS)}

        @blk.sync
        def _(sy):
            for qn in range(NQS):
                if qn == 0:
                    sy.wait_ge(sSP.h, av_spill[NP * NQS - 1])
                if qn >= 2:
                    sy.wait_ge(sT.h, wo_mm[(qn - 2) * DC + DC - 1])
                src_ap = attnv_dram[:, qn * 512:qn * 512 + 512].rearrange("(k p) t -> p k t", p=128)
                sKV.inc(sy.dma_start(av_ring[:, qn % 2, :, :], src_ap), 16)

        @blk.tensor
        def _(t):
            sW.wait(t)
            for qn in range(NQS):
                t.wait_ge(sKV.h, av_ld[qn])
                for m in range(DC):
                    i = qn * DC + m
                    psu = pAB if i % 2 == 0 else pCD
                    if i < 2:
                        t.wait_ge(sV.h, last_norm_v[0])
                    else:
                        t.wait_ge(sV.h, wo_ev[i - 2])
                    for k in range(DC):
                        t.matmul(psu[:, 0:512], wq_sb[:, k, 128 * m:128 * m + 128],
                                 av_ring[:, qn % 2, k, :], start=(k == 0), stop=False)
                    sT.inc(t.matmul(psu[:, 0:512], b_sb[0:1, 128 * m:128 * m + 128],
                                    ones_row[0:1, :], start=False, stop=True))

        @blk.vector
        def _(v):
            for qn in range(NQS):
                for m in range(DC):
                    i = qn * DC + m
                    psu = pAB if i % 2 == 0 else pCD
                    v.wait_ge(sT.h, wo_mm[i])
                    xsl = x_sb[:, m, qn * 512:qn * 512 + 512]
                    sV.inc(v.tensor_tensor(xsl, xsl, psu[:, 0:512], ALU.add))

    emit_wo()
    wes.close()
    aes.close()

    emit_ffn_era("2")

    @blk.sync
    def _(sy):
        sV.wait(sy)
        sOUT.wait(sy)
        sOUT.inc(sy.dma_start(out_t.rearrange("(c p) t -> p c t", p=128), x_sb[:]), 16)
        sOUT.wait(sy)

    es.close()
    return nc


_NC_CACHE = None


def kernel(L0_attn_x, L0_attn_Wq, L0_attn_bq, L0_attn_Wk, L0_attn_bk,
           L0_attn_Wv, L0_attn_bv, L0_attn_Wo, L0_attn_bo,
           L0_ffn_W_fc, L0_ffn_W_proj):
    global _NC_CACHE
    x = np.asarray(L0_attn_x, dtype=np.float32)
    B, S, _ = x.shape
    assert (B, S) == (4, 4096)

    wqT = np.ascontiguousarray(np.asarray(L0_attn_Wq, np.float32).T.reshape(DC, 128, D).transpose(1, 0, 2))
    wkT = np.ascontiguousarray(np.asarray(L0_attn_Wk, np.float32).T.reshape(DC, 128, D).transpose(1, 0, 2))
    wvT = np.ascontiguousarray(np.asarray(L0_attn_Wv, np.float32).T.reshape(DC, 128, D).transpose(1, 0, 2))
    woT = np.ascontiguousarray(np.asarray(L0_attn_Wo, np.float32).T.reshape(DC, 128, D).transpose(1, 0, 2))
    wfcT = np.asarray(L0_ffn_W_fc, np.float32).T
    wfcb = np.ascontiguousarray(wfcT.reshape(DC, 128, FC, 128).transpose(2, 1, 0, 3).reshape(FC // 2, 2, 128, DC, 128).transpose(0, 2, 1, 3, 4))
    wprojT = np.asarray(L0_ffn_W_proj, np.float32).T
    wpb = np.ascontiguousarray(wprojT.reshape(2, 16, 128, DC, 128).transpose(3, 0, 2, 1, 4))
    brow = np.ascontiguousarray(np.concatenate(
        [np.asarray(b, np.float32).reshape(-1)
         for b in (L0_attn_bq, L0_attn_bk, L0_attn_bv, L0_attn_bo)])[None, :])
    cstv = np.full((128, 1), EPS, np.float32)

    if _NC_CACHE is None:
        _NC_CACHE = build_program()
    nc = _NC_CACHE

    in_maps = []
    for c in range(8):
        b, h = c // 2, c % 2
        xs = np.ascontiguousarray(x[b, h * T:(h + 1) * T, :].T)
        in_maps.append(dict(x_t=xs, wqT=wqT, wkT=wkT, wvT=wvT, woT=woT,
                            wfcb=wfcb, wpb=wpb, brow=brow, cst=cstv))

    res = run_bass_kernel_spmd(nc, in_maps, core_ids=list(range(8)))
    out = np.empty((B, S, D), np.float32)
    for c in range(8):
        b, h = c // 2, c % 2
        out[b, h * T:(h + 1) * T, :] = res.results[c]["out_t"].T
    return out


# version 19
# speedup vs baseline: 1.1265x; 1.0503x over previous
import numpy as np
from contextlib import ExitStack

import concourse.bass as bass
import concourse.mybir as mybir
from concourse.bass_utils import run_bass_kernel_spmd

F32 = mybir.dt.float32
BF16 = mybir.dt.bfloat16
AF = mybir.ActivationFunctionType
ALU = mybir.AluOpType
AX = mybir.AxisListType

D = 1024
DC = 8
T = 2048
FF = 4096
FC = 32
NP = 8
QS = 512
NQS = T // QS
NTC = T // 128
EPS = 1e-5


class Sem:
    def __init__(self, es, nc, name):
        self.h = es.enter_context(nc.semaphore(name))
        self.n = 0

    def inc(self, instr, k=1):
        instr.then_inc(self.h, k)
        self.n += k
        return self.n

    def wait(self, eng, n=None):
        eng.wait_ge(self.h, self.n if n is None else n)


def build_program(debug=False):
    nc = bass.Bass("TRN2", target_bir_lowering=False)

    x_t = nc.dram_tensor("x_t", [D, T], F32, kind="ExternalInput").ap()
    wqT = nc.dram_tensor("wqT", [128, DC, D], F32, kind="ExternalInput").ap()
    wkT = nc.dram_tensor("wkT", [128, DC, D], F32, kind="ExternalInput").ap()
    wvT = nc.dram_tensor("wvT", [128, DC, D], F32, kind="ExternalInput").ap()
    woT = nc.dram_tensor("woT", [128, DC, D], F32, kind="ExternalInput").ap()
    wfcb = nc.dram_tensor("wfcb", [FC // 2, 128, 2, DC, 128], F32, kind="ExternalInput").ap()
    wpb = nc.dram_tensor("wpb", [DC, 2, 128, 16, 128], F32, kind="ExternalInput").ap()
    brow = nc.dram_tensor("brow", [1, 4 * D], F32, kind="ExternalInput").ap()
    cst = nc.dram_tensor("cst", [128, 1], F32, kind="ExternalInput").ap()
    out_t = nc.dram_tensor("out_t", [D, T], F32, kind="ExternalOutput").ap()
    cc_inK = nc.dram_tensor("cc_inK", [1024, 2048], BF16, kind="Internal").ap()
    cc_inV = nc.dram_tensor("cc_inV", [1024, 2048], BF16, kind="Internal").ap()
    cc_outK = nc.dram_tensor("cc_outK", [2, 1024, 2048], BF16, kind="Internal").ap()
    cc_outV = nc.dram_tensor("cc_outV", [2, 1024, 2048], BF16, kind="Internal").ap()
    attnv_dram = nc.dram_tensor("attnv_dram", [D, T], BF16, kind="Internal").ap()

    es = ExitStack()

    def sb(stack, name, shape, dt):
        return stack.enter_context(nc.sbuf_tensor(name, shape, dt))

    x_sb = sb(es, "x_sb", [128, DC, T], F32)
    ln_sb = sb(es, "ln_sb", [128, DC, T], BF16)
    ones_col = sb(es, "ones_col", [128, 1], BF16)
    ones_row = sb(es, "ones_row", [1, 512], BF16)
    masks = sb(es, "masks", [128, DC, 16], BF16)
    b_sb = sb(es, "b_sb", [1, D], BF16)
    eps_sb = sb(es, "eps_sb", [128, 1], F32)
    bneg_sb = sb(es, "bneg_sb", [128, 1], F32)
    rows_sb = sb(es, "rows_sb", [1, 2, 512], F32)
    ab_bf = sb(es, "ab_bf", [1, 1024], BF16)
    ones33 = sb(es, "ones33", [33, 64], BF16)
    s16_row = sb(es, "s16_row", [1, 128], BF16)
    m44_row = sb(es, "m44_row", [1, 128], BF16)
    nmx = sb(es, "nmx", [16, 1], F32)
    nmx2 = sb(es, "nmx2", [16, 1], F32)
    nmx3 = sb(es, "nmx3", [16, 1], F32)
    xb_sb = sb(es, "xb_sb", [128, 2, 512], BF16)
    xsq_sb = sb(es, "xsq_sb", [128, 2, 512], BF16)
    tmp_sb = sb(es, "tmp_sb", [128, 512], F32)
    brow_f = sb(es, "brow_f", [1, 16], F32)
    t32a = sb(es, "t32a", [32, 32], F32)
    t32b = sb(es, "t32b", [32, 32], F32)
    brow_b = sb(es, "brow_b", [1, 16], BF16)

    pAB = es.enter_context(nc.psum_tensor("pAB", [128, 1024], F32))
    pCD = es.enter_context(nc.psum_tensor("pCD", [128, 1024], F32))
    pR1 = es.enter_context(nc.psum_tensor("pR1", [128, 512], F32))
    pR2 = es.enter_context(nc.psum_tensor("pR2", [128, 512], F32))
    pR3 = es.enter_context(nc.psum_tensor("pR3", [128, 512], F32))
    pR4 = es.enter_context(nc.psum_tensor("pR4", [128, 512], F32))

    sIN = Sem(es, nc, "sIN")
    sW = Sem(es, nc, "sW")
    sT = Sem(es, nc, "sT")
    sV = Sem(es, nc, "sV")
    sS = Sem(es, nc, "sS")
    sSP = Sem(es, nc, "sSP")
    sCC = Sem(es, nc, "sCC")
    sKV = Sem(es, nc, "sKV")
    sOUT = Sem(es, nc, "sOUT")

    blk = es.enter_context(nc.Block())

    @blk.sync
    def _(sy):
        sIN.inc(sy.dma_start(eps_sb[:], cst), 16)
        for qq in range(NQS):
            sIN.inc(sy.dma_start(x_sb[:, :, qq * 512:qq * 512 + 512],
                                 x_t[:, qq * 512:qq * 512 + 512]
                                 .rearrange("(c p) t -> p c t", p=128)), 16)

    @blk.vector
    def _(v):
        v.memset(ones_col[:], 1.0)
        v.memset(ones_row[:], 1.0)
        v.memset(ones33[0:1, :], 1.0)
        v.memset(ones33[32:33, :], 1.0)
        v.memset(s16_row[:], 0.125)
        v.memset(m44_row[:], -44.0)
        v.memset(masks[:], 0.0)
        for c in range(DC):
            v.memset(masks[0:64, c, 2 * c:2 * c + 1], 1.0)
        last = None
        for c in range(DC):
            last = v.memset(masks[64:128, c, 2 * c + 1:2 * c + 2], 1.0)
        sV.inc(last)

    def load_bias(r):
        @blk.gpsimd
        def _(g):
            sT.wait(g)
            sW.inc(g.dma_start(b_sb[:], brow[0:1, r * D:(r + 1) * D]), 16)

    rowA = rows_sb[0:1, 0, :]
    rowB = rows_sb[0:1, 1, :]
    rowC = tmp_sb[0:1, :]

    ln_apply = []

    def emit_ln(qq, xgate=None):
        q0 = qq * 512
        T0, V0, S0 = sT.n, sV.n, sS.n
        ln_apply.append(V0 + DC + 3)
        ln_mm = {c: T0 + c + 1 for c in range(DC)}
        cp_done = {c: V0 + c + 1 for c in range(DC)}
        sq_done = {c: S0 + c + 1 for c in range(DC)}

        @blk.vector
        def _(v):
            if xgate is not None:
                v.wait_ge(sIN.h, xgate)
            if sOUT.n > 0:
                sOUT.wait(v)
            for c in range(DC):
                if c >= 2:
                    v.wait_ge(sT.h, ln_mm[c - 2])
                sV.inc(v.tensor_copy(xb_sb[:, c % 2, :], x_sb[:, c, q0:q0 + 512]))

        @blk.scalar
        def _(s):
            for c in range(DC):
                s.wait_ge(sV.h, cp_done[c])
                if c >= 2:
                    s.wait_ge(sT.h, ln_mm[c - 2])
                sS.inc(s.activation(xsq_sb[:, c % 2, :], xb_sb[:, c % 2, :], AF.Square))
            s.wait_ge(sV.h, V0 + DC + 1)
            if xgate is not None and qq == 0:
                s.wait_ge(sIN.h, 16)
            sS.inc(s.activation(rowC, rowB, AF.Sqrt, bias=eps_sb[0:1, :]))

        @blk.tensor
        def _(t):
            for c in range(DC):
                t.wait_ge(sS.h, sq_done[c])
                if c == 0:
                    t.wait_ge(sV.h, V0)
                t.matmul(pR1[0:1, :], ones_col[:], xb_sb[:, c % 2, :], start=(c == 0), stop=(c == DC - 1))
                sT.inc(t.matmul(pR2[0:1, :], ones_col[:], xsq_sb[:, c % 2, :],
                                start=(c == 0), stop=(c == DC - 1)))
            t.wait_ge(sV.h, V0 + DC + 2)
            t.matmul(pR3[:, :], ones_row[0:1, 0:128], ab_bf[0:1, 0:512], start=True, stop=True)
            sT.inc(t.matmul(pR4[:, :], ones_row[0:1, 0:128], ab_bf[0:1, 512:1024],
                            start=True, stop=True))

        @blk.vector
        def _(v):
            v.wait_ge(sT.h, T0 + DC)
            v.tensor_scalar_mul(rowA, pR1[0:1, :], 1.0 / D)
            v.tensor_tensor(rowB, rowA, rowA, ALU.mult)
            sV.inc(v.scalar_tensor_tensor(rowB, pR2[0:1, :], 1.0 / D, rowB, ALU.mult, ALU.subtract))
            v.wait_ge(sS.h, S0 + DC + 1)
            v.reciprocal(rowB, rowC)
            v.scalar_tensor_tensor(rowC, rowA, -1.0, rowB, ALU.mult, ALU.mult)
            v.tensor_copy(ab_bf[0:1, 0:512], rowB)
            sV.inc(v.tensor_copy(ab_bf[0:1, 512:1024], rowC))
            v.wait_ge(sT.h, T0 + DC + 1)
            last = None
            for c in range(DC):
                v.tensor_tensor(tmp_sb[:], x_sb[:, c, q0:q0 + 512], pR3[:, :], ALU.mult)
                last = v.tensor_tensor(ln_sb[:, c, q0:q0 + 512], tmp_sb[:], pR4[:, :], ALU.add)
            sV.inc(last)

        assert sT.n == T0 + DC + 1
        assert sV.n == V0 + DC + 3
        assert sS.n == S0 + DC + 1

    def emit_ffn_era(tag):
        fes = ExitStack()
        wfc_ring = sb(fes, "wfc_ring" + tag, [128, 2, 2, DC, 128], BF16)
        h_sb = sb(fes, "h_sb" + tag, [128, FC, 1024], BF16)
        wp_ring = sb(fes, "wp_ring" + tag, [128, 2, 16, 128], BF16)
        r_sb = sb(fes, "r_sb" + tag, [128, 2, 1024], F32)
        s_scan = sb(fes, "s_scan" + tag, [128, 512], F32)
        mem_scan = sb(fes, "mem_scan" + tag, [128, 512], F32)
        acc_scan = sb(fes, "acc_scan" + tag, [128, 512], F32)

        era_t0 = sT.n
        era_v0 = sV.n

        def yvb(dc, sub):
            r = dc % 4
            if r == 0:
                return pAB[:, sub * 512:sub * 512 + 512]
            if r == 1:
                return pCD[:, sub * 512:sub * 512 + 512]
            if r == 2:
                return (pR1 if sub == 0 else pR2)[:, :]
            return (pR3 if sub == 0 else pR4)[:, :]

        apply_base = len(ln_apply)
        for qq in range(NQS):
            emit_ln(qq, xgate=(32 + 16 * qq) if tag == "1" else None)
        era_apply = ln_apply[apply_base:apply_base + NQS]
        prev_scan = None

        for hh in range(2):
            q0 = hh * 1024
            T0, V0, S0, W0 = sT.n, sV.n, sS.n, sW.n
            u_cnt = {f: T0 + f + 1 for f in range(FC)}
            wp_mm = {si: T0 + FC + si + 1 for si in range(16)}
            h_cnt = {f: V0 + f + 1 for f in range(FC)}
            scan_cnt = {dc: V0 + FC + dc + 1 for dc in range(DC)}
            relu_cnt = {f: S0 + f + 1 for f in range(FC)}
            wfc_ld = {p: W0 + 16 * (p + 1) for p in range(FC // 2)}
            wp_ld = {si: W0 + 16 * (FC // 2) + 16 * (si + 1) for si in range(16)}

            @blk.gpsimd
            def _(sy, hh=hh, T0=T0, u_cnt=u_cnt, wp_mm=wp_mm):
                for p in range(FC // 2):
                    if p >= 2:
                        sy.wait_ge(sT.h, u_cnt[2 * p - 3])
                    elif hh == 1:
                        sy.wait_ge(sT.h, T0)
                    else:
                        if era_t0 > 0:
                            sy.wait_ge(sT.h, era_t0)
                        if era_v0 > 0:
                            sy.wait_ge(sV.h, era_v0)
                    sW.inc(sy.dma_start(wfc_ring[:, p % 2, :, :, :], wfcb[p]), 16)
                for si in range(16):
                    dc, fh = si // 2, si % 2
                    if si >= 2:
                        sy.wait_ge(sT.h, wp_mm[si - 2])
                    elif hh == 1:
                        sy.wait_ge(sT.h, T0)
                    elif era_v0 > 0:
                        sy.wait_ge(sV.h, era_v0)
                    sW.inc(sy.dma_start(wp_ring[:, si % 2, :, :], wpb[dc, fh]), 16)

            @blk.tensor
            def _(t, q0=q0, V0=V0, u_cnt=u_cnt, wp_mm=wp_mm, h_cnt=h_cnt,
                  scan_cnt=scan_cnt, relu_cnt=relu_cnt, wfc_ld=wfc_ld, wp_ld=wp_ld,
                  hh=hh, prev_scan=prev_scan):
                for f in range(FC):
                    psu = pAB if f % 2 == 0 else pCD
                    t.wait_ge(sW.h, wfc_ld[f // 2])
                    if f < 2:
                        if hh == 0:
                            if era_v0 > 0:
                                t.wait_ge(sV.h, era_v0)
                        else:
                            t.wait_ge(sV.h, prev_scan[4 + f])
                    else:
                        t.wait_ge(sV.h, h_cnt[f - 2])
                    mm = None
                    for sub in range(2):
                        if f == 0:
                            t.wait_ge(sV.h, era_apply[2 * hh + sub])
                        for k in range(DC):
                            mm = t.matmul(psu[:, sub * 512:sub * 512 + 512],
                                          wfc_ring[:, (f // 2) % 2, f % 2, k, :],
                                          ln_sb[:, k, q0 + sub * 512:q0 + sub * 512 + 512],
                                          start=(k == 0), stop=(k == DC - 1))
                    sT.inc(mm)
                for si in range(16):
                    dc, fh = si // 2, si % 2
                    t.wait_ge(sW.h, wp_ld[si])
                    if fh == 0:
                        if dc < 2:
                            t.wait_ge(sV.h, h_cnt[FC - 1])
                            t.wait_ge(sS.h, relu_cnt[FC - 1])
                        elif dc >= 4:
                            t.wait_ge(sV.h, scan_cnt[dc - 4])
                    mm = None
                    for sub in range(2):
                        ysl = yvb(dc, sub)
                        for fi in range(16):
                            f = 16 * fh + fi
                            mm = t.matmul(ysl, wp_ring[:, si % 2, fi, :],
                                          h_sb[:, f, sub * 512:sub * 512 + 512],
                                          start=(f == 0), stop=(f == FC - 1))
                    sT.inc(mm)

            @blk.scalar
            def _(s, u_cnt=u_cnt, h_cnt=h_cnt):
                for f in range(FC):
                    psu = pAB if f % 2 == 0 else pCD
                    s.wait_ge(sT.h, u_cnt[f])
                    if f >= 2:
                        s.wait_ge(sV.h, h_cnt[f - 2])
                    sS.inc(s.activation(r_sb[:, f % 2, :], psu[:], AF.Relu))

            def scan_ops(eng, y, xsl, ss, mem, acc, fused):
                eng.tensor_single_scalar(acc, y, 1.0, ALU.is_gt)
                if fused:
                    eng.scalar_tensor_tensor(mem, y, 1.9, acc, ALU.mult, ALU.subtract)
                else:
                    eng.tensor_scalar_mul(mem, y, 1.9)
                    eng.tensor_tensor(mem, mem, acc, ALU.subtract)
                for _step in range(3):
                    eng.tensor_single_scalar(ss, mem, 1.0, ALU.is_gt)
                    eng.tensor_tensor(acc, acc, ss, ALU.add)
                    if fused:
                        eng.scalar_tensor_tensor(mem, mem, 0.9, y, ALU.mult, ALU.add)
                    else:
                        eng.tensor_scalar_mul(mem, mem, 0.9)
                        eng.tensor_tensor(mem, mem, y, ALU.add)
                    eng.tensor_tensor(mem, mem, ss, ALU.subtract)
                eng.tensor_single_scalar(ss, mem, 1.0, ALU.is_gt)
                eng.tensor_tensor(acc, acc, ss, ALU.add)
                if fused:
                    return eng.scalar_tensor_tensor(xsl, acc, 0.1, xsl, ALU.mult, ALU.add)
                eng.tensor_scalar_mul(ss, acc, 0.1)
                return eng.tensor_tensor(xsl, xsl, ss, ALU.add)

            @blk.vector
            def _(v, q0=q0, relu_cnt=relu_cnt, wp_mm=wp_mm):
                for f in range(FC):
                    psu = pAB if f % 2 == 0 else pCD
                    v.wait_ge(sS.h, relu_cnt[f])
                    sV.inc(v.tensor_tensor(h_sb[:, f, :], psu[:], r_sb[:, f % 2, :], ALU.mult))
                for dc in range(DC):
                    v.wait_ge(sT.h, wp_mm[2 * dc + 1])
                    last = None
                    for sub in range(2):
                        y = yvb(dc, sub)
                        xsl = x_sb[:, dc, q0 + sub * 512:q0 + sub * 512 + 512]
                        last = scan_ops(v, y, xsl, s_scan[:], mem_scan[:], acc_scan[:], True)
                    sV.inc(last)

            assert sT.n == T0 + FC + 16
            assert sV.n == V0 + FC + DC
            assert sS.n == S0 + FC
            assert sW.n == W0 + 16 * (FC // 2 + 16)
            prev_scan = scan_cnt

        fes.close()

    emit_ffn_era("1")

    aes = ExitStack()
    q_sb = sb(aes, "q_sb", [128, DC, T], BF16)
    wq_sb = sb(aes, "wq_sb", [128, DC, D], BF16)
    bes = ExitStack()
    kstage = sb(bes, "kstage", [128, 4, 512], BF16)
    v_sb = sb(bes, "v_sb", [128, NTC, D], BF16)

    ln2_base = len(ln_apply)
    for qq in range(NQS):
        emit_ln(qq)
    ln2_apply = ln_apply[ln2_base:ln2_base + NQS]

    def load_w(dram):
        @blk.gpsimd
        def _(g):
            sT.wait(g)
            sW.inc(g.dma_start(wq_sb[:], dram), 16)

    def emit_proj_q():
        T0, S0 = sT.n, sS.n
        evac = {i: S0 + i + 1 for i in range(DC * NQS)}
        mmc = {i: T0 + i + 1 for i in range(DC * NQS)}

        @blk.tensor
        def _(t):
            sW.wait(t)
            for i in range(DC * NQS):
                m, qn = divmod(i, NQS)
                psu = pAB if i % 2 == 0 else pCD
                if i == 0:
                    sS.wait(t)
                if i >= 2:
                    t.wait_ge(sS.h, evac[i - 2])
                for k in range(DC):
                    t.matmul(psu[:, 0:512], wq_sb[:, k, 128 * m:128 * m + 128],
                             ln_sb[:, k, qn * 512:qn * 512 + 512], start=(k == 0), stop=False)
                sT.inc(t.matmul(psu[:, 0:512], b_sb[0:1, 128 * m:128 * m + 128],
                                ones_row[0:1, :], start=False, stop=True))

        @blk.scalar
        def _(s):
            for i in range(DC * NQS):
                m, qn = divmod(i, NQS)
                psu = pAB if i % 2 == 0 else pCD
                s.wait_ge(sT.h, mmc[i])
                sS.inc(s.copy(q_sb[:, m, qn * 512:qn * 512 + 512], psu[:, 0:512]))

    load_w(wkT)
    load_bias(1)

    def emit_proj_k():
        T0, S0, V0, SP0 = sT.n, sS.n, sV.n, sSP.n
        kred = {qn: V0 + qn + 1 for qn in range(NQS)}
        kevac = {i: S0 + 2 * i + 1 for i in range(DC * NQS)}
        ksq = {i: S0 + 2 * i + 2 for i in range(DC * NQS)}
        proj_mm = {i: T0 + 2 * i + 1 for i in range(DC * NQS)}
        knorm_mm = {i: T0 + 2 * i + 2 for i in range(DC * NQS)}
        kspill = {i: SP0 + 16 * (i + 1) for i in range(DC * NQS)}

        @blk.tensor
        def _(t):
            sW.wait(t)
            for qn in range(NQS):
                for m in range(DC):
                    i = qn * DC + m
                    psu = pAB if i % 2 == 0 else pCD
                    if m == 0 and qn < NQS:
                        t.wait_ge(sV.h, ln2_apply[qn])
                    if i >= 2:
                        t.wait_ge(sS.h, kevac[i - 2])
                    for k in range(DC):
                        t.matmul(psu[:, 0:512], wq_sb[:, k, 128 * m:128 * m + 128],
                                 ln_sb[:, k, qn * 512:qn * 512 + 512], start=(k == 0), stop=False)
                    sT.inc(t.matmul(psu[:, 0:512], b_sb[0:1, 128 * m:128 * m + 128],
                                    ones_row[0:1, :], start=False, stop=True))
                    t.wait_ge(sS.h, ksq[i])
                    if i == 0:
                        t.wait_ge(sV.h, ln2_apply[NQS - 1])
                    elif m == 0:
                        t.wait_ge(sV.h, kred[qn - 1])
                    sT.inc(t.matmul(pR1[0:16, :], masks[:, m, :], xsq_sb[:, i % 2, :],
                                    start=(m == 0), stop=(m == DC - 1)))

        @blk.scalar
        def _(s):
            for i in range(DC * NQS):
                psu = pAB if i % 2 == 0 else pCD
                s.wait_ge(sT.h, proj_mm[i])
                if i >= 4:
                    s.wait_ge(sSP.h, kspill[i - 4])
                if i >= 2:
                    s.wait_ge(sT.h, knorm_mm[i - 2])
                sS.inc(s.copy(kstage[:, i % 4, :], psu[:, 0:512]))
                sS.inc(s.activation(xsq_sb[:, i % 2, :], kstage[:, i % 4, :], AF.Square))

        @blk.sync
        def _(sy):
            for qn in range(NQS):
                for m in range(DC):
                    i = qn * DC + m
                    sy.wait_ge(sS.h, kevac[i])
                    sSP.inc(sy.dma_start(cc_inK[128 * m:128 * m + 128, qn * 512:qn * 512 + 512],
                                         kstage[:, i % 4, :]), 16)

        @blk.vector
        def _(v):
            for qn in range(NQS):
                v.wait_ge(sT.h, knorm_mm[qn * DC + DC - 1])
                if qn == 0:
                    sV.inc(v.tensor_reduce(nmx2[:], pR1[0:16, :], AX.X, ALU.max))
                else:
                    v.tensor_reduce(nmx3[:], pR1[0:16, :], AX.X, ALU.max)
                    sV.inc(v.tensor_tensor(nmx2[:], nmx2[:], nmx3[:], ALU.max))

    emit_proj_k()

    load_w(wvT)
    load_bias(2)

    @blk.gpsimd
    def _(g):
        sSP.wait(g)
        sCC.inc(g.collective_compute(
            "AllGather", ALU.bypass, ins=[cc_inK], outs=[cc_outK],
            replica_groups=[[0, 1], [2, 3], [4, 5], [6, 7]]))

    def emit_proj_v():
        T0, S0 = sT.n, sS.n
        vevac = {tc: S0 + tc + 1 for tc in range(NTC)}
        vmm = {tc: T0 + tc + 1 for tc in range(NTC)}

        @blk.tensor
        def _(t):
            sW.wait(t)
            for tc in range(NTC):
                psu = pAB if tc % 2 == 0 else pCD
                if tc < 2:
                    sV.wait(t)
                else:
                    t.wait_ge(sS.h, vevac[tc - 2])
                mm = None
                for half in range(2):
                    for k in range(DC):
                        t.matmul(psu[:, half * 512:half * 512 + 512],
                                 ln_sb[:, k, tc * 128:tc * 128 + 128],
                                 wq_sb[:, k, half * 512:half * 512 + 512],
                                 start=(k == 0), stop=False)
                    mm = t.matmul(psu[:, half * 512:half * 512 + 512],
                                  ones_row[0:1, 0:128],
                                  b_sb[0:1, half * 512:half * 512 + 512],
                                  start=False, stop=True)
                sT.inc(mm)

        @blk.scalar
        def _(s):
            for tc in range(NTC):
                psu = pAB if tc % 2 == 0 else pCD
                s.wait_ge(sT.h, vmm[tc])
                s.copy(v_sb[:, tc, 0:512], psu[:, 0:512])
                sS.inc(s.copy(v_sb[:, tc, 512:1024], psu[:, 512:1024]))

    emit_proj_v()

    @blk.sync
    def _(sy):
        sS.wait(sy)
        vdst = cc_inV.rearrange("r (two d) -> (r two) d", two=2) \
                     .rearrange("(tc p) d -> p tc d", p=128)
        sSP.inc(sy.dma_start(vdst, v_sb[:]), 16)

    @blk.gpsimd
    def _(g):
        sSP.wait(g)
        sCC.inc(g.collective_compute(
            "AllGather", ALU.bypass, ins=[cc_inV], outs=[cc_outV],
            replica_groups=[[0, 1], [2, 3], [4, 5], [6, 7]]))

    load_w(wqT)
    load_bias(0)
    emit_proj_q()

    def emit_norms_q():
        T0, S0 = sT.n, sS.n
        V0n = sV.n
        qred = {qs: V0n + qs + 1 for qs in range(NQS)}
        sqq = {}
        nm_mm = {}
        i = 0
        for qs in range(NQS):
            for c in range(DC):
                sqq[(qs, c)] = S0 + i + 1
                nm_mm[(qs, c)] = T0 + i + 1
                i += 1

        @blk.scalar
        def _(s):
            sT.wait(s)
            first = True
            for qs in range(NQS):
                for c in range(DC):
                    if not first and not (qs == 0 and c == 1):
                        prev = (qs, c - 2) if c >= 2 else (qs - 1, DC - 2 + c)
                        s.wait_ge(sT.h, nm_mm[prev])
                    first = False
                    sS.inc(s.activation(xsq_sb[:, c % 2, :],
                                        q_sb[:, c, qs * 512:qs * 512 + 512], AF.Square))

        @blk.tensor
        def _(t):
            for qs in range(NQS):
                for c in range(DC):
                    t.wait_ge(sS.h, sqq[(qs, c)])
                    if qs == 0 and c == 0:
                        sV.wait(t)
                    elif c == 0:
                        t.wait_ge(sV.h, qred[qs - 1])
                    sT.inc(t.matmul(pR1[0:16, :], masks[:, c, :], xsq_sb[:, c % 2, :],
                                    start=(c == 0), stop=(c == DC - 1)))

        @blk.vector
        def _(v):
            for qs in range(NQS):
                v.wait_ge(sT.h, nm_mm[(qs, DC - 1)])
                if qs == 0:
                    sV.inc(v.tensor_reduce(nmx[:], pR1[0:16, :], AX.X, ALU.max))
                else:
                    v.tensor_reduce(nmx3[:], pR1[0:16, :], AX.X, ALU.max)
                    sV.inc(v.tensor_tensor(nmx[:], nmx[:], nmx3[:], ALU.max))

    emit_norms_q()

    @blk.vector
    def _(v):
        sV.wait(v)
        sV.inc(v.tensor_tensor(nmx[:], nmx[:], nmx2[:], ALU.mult))

    @blk.scalar
    def _(s):
        sV.wait(s)
        sS.inc(s.activation(nmx3[:], nmx[:], AF.Sqrt))

    @blk.vector
    def _(v):
        sS.wait(v)
        v.memset(t32a[:], 0.0)
        sV.inc(v.tensor_copy(t32a[0:16, 0:1], nmx3[:]))
        sV.wait(v)
        sV.inc(v.transpose(t32b[:], t32a[:]))
        sV.wait(v)
        sV.inc(v.tensor_copy(brow_f[:], t32b[0:1, 0:16]))

    @blk.scalar
    def _(s):
        sV.wait(s)
        sS.inc(s.copy(brow_b[:], t32b[0:1, 0:16]))

    @blk.tensor
    def _(t):
        sS.wait(t)
        t.matmul(pR2[:, 0:16], s16_row[:], brow_b[:], start=True, stop=False)
        sT.inc(t.matmul(pR2[:, 0:16], m44_row[:], ones_row[0:1, 0:16], start=False, stop=True))

    @blk.vector
    def _(v):
        sT.wait(v)
        sV.inc(v.tensor_reduce(bneg_sb[:], pR2[:, 0:16], AX.X, ALU.max, negate=True))

    bneg_ready = sV.n
    bes.close()

    load_w(woT)
    load_bias(3)

    des = ExitStack()
    kp_sb = sb(des, "kp_sb", [128, 2, 2, T], BF16)
    vp_sb = sb(des, "vp_sb", [128, 2, 2, NTC, 128], BF16)
    expS = sb(des, "expS", [128, 2, 1024], BF16)
    rec_bf = sb(des, "rec_bf", [33, 512], BF16)
    binv_sb = sb(des, "binv_sb", [128, 512], BF16)
    avstage = sb(des, "avstage", [128, 2, 512], BF16)

    kv_cnt = {}
    pair_t_done = {}
    last_norm_v = [bneg_ready]
    av_spill = {}
    prev_pair_av = {}

    def emit_kv_load(p, prev_t):
        buf = p % 2

        @blk.sync
        def _(sy):
            sy.wait_ge(sCC.h, 1)
            if prev_t:
                sy.wait_ge(sT.h, prev_t)
            for s_ in range(2):
                sKV.inc(sy.dma_start(kp_sb[:, buf, s_, :], cc_outK[s_, 128 * p:128 * p + 128, :]), 16)
            sy.wait_ge(sCC.h, 2)
            for s_ in range(2):
                vsrc = cc_outV[s_].rearrange("r (two d) -> (r two) d", two=2)
                vsrc = vsrc[:, 128 * p:128 * p + 128].rearrange("(tc p2) d -> p2 tc d", p2=128)
                sKV.inc(sy.dma_start(vp_sb[:, buf, s_, :, :], vsrc), 16)
            kv_cnt[p] = sKV.n

    emit_kv_load(0, 0)
    emit_kv_load(1, 0)

    units_norm = {}
    prev_exp = {}

    for p in range(NP):
        buf = p % 2
        T0, S0, V0, SP0 = sT.n, sS.n, sV.n, sSP.n
        qk_cnt = {}
        av_cnt = {}
        binv_mm = {}
        n = T0
        for qs in range(NQS):
            for kc in range(32):
                n += 1
                qk_cnt[(qs, kc)] = n
                if kc >= 1:
                    n += 1
                    av_cnt[(qs, kc - 1)] = n
                if kc == 8 and qs > 0:
                    n += 1
                    binv_mm[qs - 1] = n
            n += 1
            av_cnt[(qs, 31)] = n
        n += 1
        binv_mm[NQS - 1] = n
        exp_cnt = {(qs, kc): S0 + qs * 32 + kc + 1 for qs in range(NQS) for kc in range(32)}
        rec_v = {qs: V0 + qs * 2 + 1 for qs in range(NQS)}
        norm_v = {qs: V0 + qs * 2 + 2 for qs in range(NQS)}
        spill_n = {qs: SP0 + 16 * (qs + 1) for qs in range(NQS)}
        for qs in range(NQS):
            units_norm[p * NQS + qs] = norm_v[qs]

        @blk.tensor
        def _(t, p=p, buf=buf, qk_cnt=qk_cnt, av_cnt=av_cnt, binv_mm=binv_mm,
              exp_cnt=exp_cnt, rec_v=rec_v, norm_v=norm_v, prevx=prev_exp):
            def emit_av(qs, kc):
                u = p * NQS + qs
                avp = (pR2, pR3)[u % 2]
                smp = (pR1, pR4)[u % 2]
                t.wait_ge(sS.h, exp_cnt[(qs, kc)])
                if kc == 0 and u >= 2:
                    t.wait_ge(sV.h, units_norm[u - 2])
                slot, j = kc // 16, kc % 16
                t.matmul(avp[0:64, :], vp_sb[:, buf, slot, j, 0:64],
                         expS[:, kc % 2, 0:512], start=(kc == 0), stop=(kc == 31))
                t.matmul(avp[64:128, :], vp_sb[:, buf, slot, j, 64:128],
                         expS[:, kc % 2, 512:1024], start=(kc == 0), stop=(kc == 31),
                         tile_position=(0, 64), skip_group_check=True)
                t.matmul(smp[0:1, :], ones_col[:], expS[:, kc % 2, 0:512],
                         start=(kc == 0), stop=(kc == 31))
                sT.inc(t.matmul(smp[32:33, :], ones_col[:], expS[:, kc % 2, 512:1024],
                                start=(kc == 0), stop=(kc == 31), tile_position=(0, 32),
                                skip_group_check=True))

            def emit_binv(qs):
                u = p * NQS + qs
                smp = (pR1, pR4)[u % 2]
                t.wait_ge(sV.h, rec_v[qs])
                t.matmul(smp[0:64, 0:512], ones33[0:1, :], rec_bf[0:1, :], start=True, stop=False)
                sT.inc(t.matmul(smp[64:128, 0:512], ones33[32:33, :], rec_bf[32:33, :],
                                start=True, stop=True, tile_position=(32, 64),
                                skip_group_check=True))

            for qs in range(NQS):
                q0 = qs * 512
                for kc in range(32):
                    sc = pAB if kc % 2 == 0 else pCD
                    slot, j = kc // 16, kc % 16
                    if kc == 0 and qs == 0:
                        t.wait_ge(sKV.h, kv_cnt[p])
                        if p == 0:
                            t.wait_ge(sV.h, last_norm_v[0])
                    if kc >= 2:
                        t.wait_ge(sS.h, exp_cnt[(qs, kc - 2)])
                    elif qs > 0:
                        t.wait_ge(sS.h, exp_cnt[(qs - 1, 30 + kc)])
                    elif p > 0:
                        s_prev = prevx.get(30 + kc)
                        if s_prev:
                            t.wait_ge(sS.h, s_prev)
                    t.matmul(sc[:, 0:512], kp_sb[0:64, buf, slot, 128 * j:128 * j + 128],
                             q_sb[0:64, p, q0:q0 + 512], start=True, stop=True)
                    sT.inc(t.matmul(sc[:, 512:1024], kp_sb[64:128, buf, slot, 128 * j:128 * j + 128],
                                    q_sb[64:128, p, q0:q0 + 512], start=True, stop=True,
                                    tile_position=(64, 0)))
                    if kc >= 1:
                        emit_av(qs, kc - 1)
                    if kc == 8 and qs > 0:
                        emit_binv(qs - 1)
                emit_av(qs, 31)
            emit_binv(NQS - 1)

        @blk.scalar
        def _(s, p=p, qk_cnt=qk_cnt, av_cnt=av_cnt, prev=prev_pair_av):
            for qs in range(NQS):
                for kc in range(32):
                    sc = pAB if kc % 2 == 0 else pCD
                    s.wait_ge(sT.h, qk_cnt[(qs, kc)])
                    if kc >= 2:
                        s.wait_ge(sT.h, av_cnt[(qs, kc - 2)])
                    elif qs > 0:
                        s.wait_ge(sT.h, av_cnt[(qs - 1, 30 + kc)])
                    elif p > 0:
                        s.wait_ge(sT.h, prev[30 + kc])
                    sS.inc(s.activation(expS[:, kc % 2, :], sc[:], AF.Exp,
                                        bias=bneg_sb[:], scale=0.125))

        @blk.vector
        def _(v, p=p, av_cnt=av_cnt, binv_mm=binv_mm):
            for qs in range(NQS):
                u = p * NQS + qs
                avp = (pR2, pR3)[u % 2]
                smp = (pR1, pR4)[u % 2]
                v.wait_ge(sT.h, av_cnt[(qs, 31)])
                with nc.allow_low_precision(reason="softmax normalizer feeds bf16 matmul"):
                    sV.inc(v.reciprocal(rec_bf[:], smp[0:33, :]))
                v.wait_ge(sT.h, binv_mm[qs])
                if u >= 2:
                    v.wait_ge(sSP.h, av_spill[u - 2] if qs < 2 else spill_n[qs - 2])
                v.tensor_copy(binv_sb[:], smp[:, 0:512])
                sV.inc(v.tensor_tensor(avstage[:, qs % 2, :], avp[:], binv_sb[:], ALU.mult))

        @blk.sync
        def _(sy, p=p, norm_v=norm_v):
            for qs in range(NQS):
                sy.wait_ge(sV.h, norm_v[qs])
                av_spill[p * NQS + qs] = sSP.inc(sy.dma_start(
                    attnv_dram[128 * p:128 * p + 128, qs * 512:qs * 512 + 512],
                    avstage[:, qs % 2, :]), 16)

        prev_pair_av = {30: av_cnt[(NQS - 1, 30)], 31: av_cnt[(NQS - 1, 31)]}
        prev_exp = {30: exp_cnt[(NQS - 1, 30)], 31: exp_cnt[(NQS - 1, 31)]}
        last_norm_v[0] = norm_v[NQS - 1]
        assert sT.n == T0 + NQS * 65
        assert sS.n == S0 + NQS * 32
        assert sV.n == V0 + NQS * 2
        pair_t_done[p] = sT.n
        if p + 2 < NP:
            emit_kv_load(p + 2, pair_t_done[p])

    des.close()
    wes = ExitStack()
    av_ring = sb(wes, "av_ring", [128, 2, DC, 512], BF16)

    def emit_wo():
        T0, V0, KV0 = sT.n, sV.n, sKV.n
        wo_mm = {i: T0 + i + 1 for i in range(NQS * DC)}
        wo_ev = {i: V0 + i + 1 for i in range(NQS * DC)}
        av_ld = {qn: KV0 + 16 * (qn + 1) for qn in range(NQS)}

        @blk.sync
        def _(sy):
            for qn in range(NQS):
                if qn == 0:
                    sy.wait_ge(sSP.h, av_spill[NP * NQS - 1])
                if qn >= 2:
                    sy.wait_ge(sT.h, wo_mm[(qn - 2) * DC + DC - 1])
                src_ap = attnv_dram[:, qn * 512:qn * 512 + 512].rearrange("(k p) t -> p k t", p=128)
                sKV.inc(sy.dma_start(av_ring[:, qn % 2, :, :], src_ap), 16)

        @blk.tensor
        def _(t):
            sW.wait(t)
            for qn in range(NQS):
                t.wait_ge(sKV.h, av_ld[qn])
                for m in range(DC):
                    i = qn * DC + m
                    psu = pAB if i % 2 == 0 else pCD
                    if i < 2:
                        t.wait_ge(sV.h, last_norm_v[0])
                    else:
                        t.wait_ge(sV.h, wo_ev[i - 2])
                    for k in range(DC):
                        t.matmul(psu[:, 0:512], wq_sb[:, k, 128 * m:128 * m + 128],
                                 av_ring[:, qn % 2, k, :], start=(k == 0), stop=False)
                    sT.inc(t.matmul(psu[:, 0:512], b_sb[0:1, 128 * m:128 * m + 128],
                                    ones_row[0:1, :], start=False, stop=True))

        @blk.vector
        def _(v):
            for qn in range(NQS):
                for m in range(DC):
                    i = qn * DC + m
                    psu = pAB if i % 2 == 0 else pCD
                    v.wait_ge(sT.h, wo_mm[i])
                    xsl = x_sb[:, m, qn * 512:qn * 512 + 512]
                    sV.inc(v.tensor_tensor(xsl, xsl, psu[:, 0:512], ALU.add))

    emit_wo()
    wes.close()
    aes.close()

    emit_ffn_era("2")

    @blk.sync
    def _(sy):
        sV.wait(sy)
        sOUT.wait(sy)
        sOUT.inc(sy.dma_start(out_t.rearrange("(c p) t -> p c t", p=128), x_sb[:]), 16)
        sOUT.wait(sy)

    es.close()
    return nc


_NC_CACHE = None


def kernel(L0_attn_x, L0_attn_Wq, L0_attn_bq, L0_attn_Wk, L0_attn_bk,
           L0_attn_Wv, L0_attn_bv, L0_attn_Wo, L0_attn_bo,
           L0_ffn_W_fc, L0_ffn_W_proj):
    global _NC_CACHE
    x = np.asarray(L0_attn_x, dtype=np.float32)
    B, S, _ = x.shape
    assert (B, S) == (4, 4096)

    wqT = np.ascontiguousarray(np.asarray(L0_attn_Wq, np.float32).T.reshape(DC, 128, D).transpose(1, 0, 2))
    wkT = np.ascontiguousarray(np.asarray(L0_attn_Wk, np.float32).T.reshape(DC, 128, D).transpose(1, 0, 2))
    wvT = np.ascontiguousarray(np.asarray(L0_attn_Wv, np.float32).T.reshape(DC, 128, D).transpose(1, 0, 2))
    woT = np.ascontiguousarray(np.asarray(L0_attn_Wo, np.float32).T.reshape(DC, 128, D).transpose(1, 0, 2))
    wfcT = np.asarray(L0_ffn_W_fc, np.float32).T
    wfcb = np.ascontiguousarray(wfcT.reshape(DC, 128, FC, 128).transpose(2, 1, 0, 3).reshape(FC // 2, 2, 128, DC, 128).transpose(0, 2, 1, 3, 4))
    wprojT = np.asarray(L0_ffn_W_proj, np.float32).T
    wpb = np.ascontiguousarray(wprojT.reshape(2, 16, 128, DC, 128).transpose(3, 0, 2, 1, 4))
    brow = np.ascontiguousarray(np.concatenate(
        [np.asarray(b, np.float32).reshape(-1)
         for b in (L0_attn_bq, L0_attn_bk, L0_attn_bv, L0_attn_bo)])[None, :])
    cstv = np.full((128, 1), EPS, np.float32)

    if _NC_CACHE is None:
        _NC_CACHE = build_program()
    nc = _NC_CACHE

    in_maps = []
    for c in range(8):
        b, h = c // 2, c % 2
        xs = np.ascontiguousarray(x[b, h * T:(h + 1) * T, :].T)
        in_maps.append(dict(x_t=xs, wqT=wqT, wkT=wkT, wvT=wvT, woT=woT,
                            wfcb=wfcb, wpb=wpb, brow=brow, cst=cstv))

    res = run_bass_kernel_spmd(nc, in_maps, core_ids=list(range(8)))
    out = np.empty((B, S, D), np.float32)
    for c in range(8):
        b, h = c // 2, c % 2
        out[b, h * T:(h + 1) * T, :] = res.results[c]["out_t"].T
    return out


# version 20
# speedup vs baseline: 1.1470x; 1.0182x over previous
import numpy as np
from contextlib import ExitStack

import concourse.bass as bass
import concourse.mybir as mybir
from concourse.bass_utils import run_bass_kernel_spmd

F32 = mybir.dt.float32
BF16 = mybir.dt.bfloat16
AF = mybir.ActivationFunctionType
ALU = mybir.AluOpType
AX = mybir.AxisListType

D = 1024
DC = 8
T = 2048
FF = 4096
FC = 32
NP = 8
QS = 512
NQS = T // QS
NTC = T // 128
EPS = 1e-5


class Sem:
    def __init__(self, es, nc, name):
        self.h = es.enter_context(nc.semaphore(name))
        self.n = 0

    def inc(self, instr, k=1):
        instr.then_inc(self.h, k)
        self.n += k
        return self.n

    def wait(self, eng, n=None):
        eng.wait_ge(self.h, self.n if n is None else n)


def build_program(debug=False):
    nc = bass.Bass("TRN2", target_bir_lowering=False)

    x_t = nc.dram_tensor("x_t", [D, T], F32, kind="ExternalInput").ap()
    wqT = nc.dram_tensor("wqT", [128, DC, D], F32, kind="ExternalInput").ap()
    wkT = nc.dram_tensor("wkT", [128, DC, D], F32, kind="ExternalInput").ap()
    wvT = nc.dram_tensor("wvT", [128, DC, D], F32, kind="ExternalInput").ap()
    woT = nc.dram_tensor("woT", [128, DC, D], F32, kind="ExternalInput").ap()
    wfcb = nc.dram_tensor("wfcb", [FC // 2, 128, 2, DC, 128], F32, kind="ExternalInput").ap()
    wpb = nc.dram_tensor("wpb", [DC, 2, 128, 16, 128], F32, kind="ExternalInput").ap()
    brow = nc.dram_tensor("brow", [1, 4 * D], F32, kind="ExternalInput").ap()
    cst = nc.dram_tensor("cst", [128, 1], F32, kind="ExternalInput").ap()
    out_t = nc.dram_tensor("out_t", [D, T], F32, kind="ExternalOutput").ap()
    cc_inK = nc.dram_tensor("cc_inK", [1024, 2048], BF16, kind="Internal").ap()
    cc_inV = nc.dram_tensor("cc_inV", [1024, 2048], BF16, kind="Internal").ap()
    cc_outK = nc.dram_tensor("cc_outK", [2, 1024, 2048], BF16, kind="Internal").ap()
    cc_outV = nc.dram_tensor("cc_outV", [2, 1024, 2048], BF16, kind="Internal").ap()
    attnv_dram = nc.dram_tensor("attnv_dram", [D, T], BF16, kind="Internal").ap()

    es = ExitStack()

    def sb(stack, name, shape, dt):
        return stack.enter_context(nc.sbuf_tensor(name, shape, dt))

    x_sb = sb(es, "x_sb", [128, DC, T], F32)
    ln_sb = sb(es, "ln_sb", [128, DC, T], BF16)
    ones_col = sb(es, "ones_col", [128, 1], BF16)
    ones_row = sb(es, "ones_row", [1, 512], BF16)
    masks = sb(es, "masks", [128, DC, 16], BF16)
    b_sb = sb(es, "b_sb", [1, D], BF16)
    eps_sb = sb(es, "eps_sb", [128, 1], F32)
    bneg_sb = sb(es, "bneg_sb", [128, 1], F32)
    rows_sb = sb(es, "rows_sb", [1, 2, 512], F32)
    ab_bf = sb(es, "ab_bf", [1, 1024], BF16)
    ones33 = sb(es, "ones33", [33, 64], BF16)
    s16_row = sb(es, "s16_row", [1, 128], BF16)
    m44_row = sb(es, "m44_row", [1, 128], BF16)
    nmx = sb(es, "nmx", [16, 1], F32)
    nmx2 = sb(es, "nmx2", [16, 1], F32)
    nmx3 = sb(es, "nmx3", [16, 1], F32)
    xb_sb = sb(es, "xb_sb", [128, 2, 512], BF16)
    xsq_sb = sb(es, "xsq_sb", [128, 2, 512], BF16)
    tmp_sb = sb(es, "tmp_sb", [128, 512], F32)
    brow_f = sb(es, "brow_f", [1, 16], F32)
    t32a = sb(es, "t32a", [32, 32], F32)
    t32b = sb(es, "t32b", [32, 32], F32)
    brow_b = sb(es, "brow_b", [1, 16], BF16)

    pAB = es.enter_context(nc.psum_tensor("pAB", [128, 1024], F32))
    pCD = es.enter_context(nc.psum_tensor("pCD", [128, 1024], F32))
    pR1 = es.enter_context(nc.psum_tensor("pR1", [128, 512], F32))
    pR2 = es.enter_context(nc.psum_tensor("pR2", [128, 512], F32))
    pR3 = es.enter_context(nc.psum_tensor("pR3", [128, 512], F32))
    pR4 = es.enter_context(nc.psum_tensor("pR4", [128, 512], F32))

    sIN = Sem(es, nc, "sIN")
    sW = Sem(es, nc, "sW")
    sT = Sem(es, nc, "sT")
    sV = Sem(es, nc, "sV")
    sS = Sem(es, nc, "sS")
    sSP = Sem(es, nc, "sSP")
    sCC = Sem(es, nc, "sCC")
    sKV = Sem(es, nc, "sKV")
    sOUT = Sem(es, nc, "sOUT")

    blk = es.enter_context(nc.Block())

    @blk.sync
    def _(sy):
        sIN.inc(sy.dma_start(eps_sb[:], cst), 16)
        for qq in range(NQS):
            sIN.inc(sy.dma_start(x_sb[:, :, qq * 512:qq * 512 + 512],
                                 x_t[:, qq * 512:qq * 512 + 512]
                                 .rearrange("(c p) t -> p c t", p=128)), 16)

    @blk.vector
    def _(v):
        v.memset(ones_col[:], 1.0)
        v.memset(ones_row[:], 1.0)
        v.memset(ones33[0:1, :], 1.0)
        v.memset(ones33[32:33, :], 1.0)
        v.memset(s16_row[:], 0.125)
        v.memset(m44_row[:], -44.0)
        v.memset(masks[:], 0.0)
        for c in range(DC):
            v.memset(masks[0:64, c, 2 * c:2 * c + 1], 1.0)
        last = None
        for c in range(DC):
            last = v.memset(masks[64:128, c, 2 * c + 1:2 * c + 2], 1.0)
        sV.inc(last)

    def load_bias(r):
        @blk.gpsimd
        def _(g):
            sT.wait(g)
            sW.inc(g.dma_start(b_sb[:], brow[0:1, r * D:(r + 1) * D]), 16)

    rowA = rows_sb[0:1, 0, :]
    rowB = rows_sb[0:1, 1, :]
    rowC = tmp_sb[0:1, :]

    ln_apply = []

    def emit_ln(qq, xgate=None):
        q0 = qq * 512
        T0, V0, S0 = sT.n, sV.n, sS.n
        ln_apply.append(V0 + DC + 3)
        ln_mm = {c: T0 + c + 1 for c in range(DC)}
        cp_done = {c: V0 + c + 1 for c in range(DC)}
        sq_done = {c: S0 + c + 1 for c in range(DC)}

        @blk.vector
        def _(v):
            if xgate is not None:
                v.wait_ge(sIN.h, xgate)
            if sOUT.n > 0:
                sOUT.wait(v)
            for c in range(DC):
                if c >= 2:
                    v.wait_ge(sT.h, ln_mm[c - 2])
                sV.inc(v.tensor_copy(xb_sb[:, c % 2, :], x_sb[:, c, q0:q0 + 512]))

        @blk.scalar
        def _(s):
            for c in range(DC):
                s.wait_ge(sV.h, cp_done[c])
                if c >= 2:
                    s.wait_ge(sT.h, ln_mm[c - 2])
                sS.inc(s.activation(xsq_sb[:, c % 2, :], xb_sb[:, c % 2, :], AF.Square))
            s.wait_ge(sV.h, V0 + DC + 1)
            if xgate is not None and qq == 0:
                s.wait_ge(sIN.h, 16)
            sS.inc(s.activation(rowC, rowB, AF.Sqrt, bias=eps_sb[0:1, :]))

        @blk.tensor
        def _(t):
            for c in range(DC):
                t.wait_ge(sS.h, sq_done[c])
                if c == 0:
                    t.wait_ge(sV.h, V0)
                t.matmul(pR1[0:1, :], ones_col[:], xb_sb[:, c % 2, :], start=(c == 0), stop=(c == DC - 1))
                sT.inc(t.matmul(pR2[0:1, :], ones_col[:], xsq_sb[:, c % 2, :],
                                start=(c == 0), stop=(c == DC - 1)))
            t.wait_ge(sV.h, V0 + DC + 2)
            t.matmul(pR3[:, :], ones_row[0:1, 0:128], ab_bf[0:1, 0:512], start=True, stop=True)
            sT.inc(t.matmul(pR4[:, :], ones_row[0:1, 0:128], ab_bf[0:1, 512:1024],
                            start=True, stop=True))

        @blk.vector
        def _(v):
            v.wait_ge(sT.h, T0 + DC)
            v.tensor_scalar_mul(rowA, pR1[0:1, :], 1.0 / D)
            v.tensor_tensor(rowB, rowA, rowA, ALU.mult)
            sV.inc(v.scalar_tensor_tensor(rowB, pR2[0:1, :], 1.0 / D, rowB, ALU.mult, ALU.subtract))
            v.wait_ge(sS.h, S0 + DC + 1)
            v.reciprocal(rowB, rowC)
            v.scalar_tensor_tensor(rowC, rowA, -1.0, rowB, ALU.mult, ALU.mult)
            v.tensor_copy(ab_bf[0:1, 0:512], rowB)
            sV.inc(v.tensor_copy(ab_bf[0:1, 512:1024], rowC))
            v.wait_ge(sT.h, T0 + DC + 1)
            last = None
            for c in range(DC):
                v.tensor_tensor(tmp_sb[:], x_sb[:, c, q0:q0 + 512], pR3[:, :], ALU.mult)
                last = v.tensor_tensor(ln_sb[:, c, q0:q0 + 512], tmp_sb[:], pR4[:, :], ALU.add)
            sV.inc(last)

        assert sT.n == T0 + DC + 1
        assert sV.n == V0 + DC + 3
        assert sS.n == S0 + DC + 1

    def emit_ffn_era(tag):
        fes = ExitStack()
        wfc_ring = sb(fes, "wfc_ring" + tag, [128, 2, 2, DC, 128], BF16)
        h_sb = sb(fes, "h_sb" + tag, [128, FC, 1024], BF16)
        wp_ring = sb(fes, "wp_ring" + tag, [128, 2, 16, 128], BF16)
        r_sb = sb(fes, "r_sb" + tag, [128, 2, 1024], F32)
        s_scan = sb(fes, "s_scan" + tag, [128, 512], F32)
        mem_scan = sb(fes, "mem_scan" + tag, [128, 512], F32)
        acc_scan = sb(fes, "acc_scan" + tag, [128, 512], F32)

        era_t0 = sT.n
        era_v0 = sV.n
        half_scan = []

        def yvb(dc, sub):
            r = dc % 4
            if r == 0:
                return pAB[:, sub * 512:sub * 512 + 512]
            if r == 1:
                return pCD[:, sub * 512:sub * 512 + 512]
            if r == 2:
                return (pR1 if sub == 0 else pR2)[:, :]
            return (pR3 if sub == 0 else pR4)[:, :]

        apply_base = len(ln_apply)
        for qq in range(NQS):
            emit_ln(qq, xgate=(32 + 16 * qq) if tag == "1" else None)
        era_apply = ln_apply[apply_base:apply_base + NQS]
        prev_scan = None

        for hh in range(2):
            q0 = hh * 1024
            T0, V0, S0, W0 = sT.n, sV.n, sS.n, sW.n
            u_cnt = {f: T0 + f + 1 for f in range(FC)}
            wp_mm = {si: T0 + FC + si + 1 for si in range(16)}
            scan_cnt = {dc: V0 + dc + 1 for dc in range(DC)}
            relu_cnt = {f: S0 + 2 * f + 1 for f in range(FC)}
            sq_cnt = {f: S0 + 2 * f + 2 for f in range(FC)}
            wfc_ld = {p: W0 + 16 * (p + 1) for p in range(FC // 2)}
            wp_ld = {si: W0 + 16 * (FC // 2) + 16 * (si + 1) for si in range(16)}

            @blk.gpsimd
            def _(sy, hh=hh, T0=T0, u_cnt=u_cnt, wp_mm=wp_mm):
                for p in range(FC // 2):
                    if p >= 2:
                        sy.wait_ge(sT.h, u_cnt[2 * p - 3])
                    elif hh == 1:
                        sy.wait_ge(sT.h, T0)
                    else:
                        if era_t0 > 0:
                            sy.wait_ge(sT.h, era_t0)
                        if era_v0 > 0:
                            sy.wait_ge(sV.h, era_v0)
                    sW.inc(sy.dma_start(wfc_ring[:, p % 2, :, :, :], wfcb[p]), 16)
                for si in range(16):
                    dc, fh = si // 2, si % 2
                    if si >= 2:
                        sy.wait_ge(sT.h, wp_mm[si - 2])
                    elif hh == 1:
                        sy.wait_ge(sT.h, T0)
                    elif era_v0 > 0:
                        sy.wait_ge(sV.h, era_v0)
                    sW.inc(sy.dma_start(wp_ring[:, si % 2, :, :], wpb[dc, fh]), 16)

            @blk.tensor
            def _(t, q0=q0, V0=V0, u_cnt=u_cnt, wp_mm=wp_mm, sq_cnt=sq_cnt,
                  scan_cnt=scan_cnt, relu_cnt=relu_cnt, wfc_ld=wfc_ld, wp_ld=wp_ld,
                  hh=hh, prev_scan=prev_scan):
                for f in range(FC):
                    psu = pAB if f % 2 == 0 else pCD
                    t.wait_ge(sW.h, wfc_ld[f // 2])
                    if f < 2:
                        if hh == 0:
                            if era_v0 > 0:
                                t.wait_ge(sV.h, era_v0)
                        else:
                            t.wait_ge(sV.h, prev_scan[4 + f])
                    else:
                        t.wait_ge(sS.h, relu_cnt[f - 2])
                    mm = None
                    for sub in range(2):
                        if f == 0:
                            t.wait_ge(sV.h, era_apply[2 * hh + sub])
                        for k in range(DC):
                            mm = t.matmul(psu[:, sub * 512:sub * 512 + 512],
                                          wfc_ring[:, (f // 2) % 2, f % 2, k, :],
                                          ln_sb[:, k, q0 + sub * 512:q0 + sub * 512 + 512],
                                          start=(k == 0), stop=(k == DC - 1))
                    sT.inc(mm)
                for si in range(16):
                    dc, fh = si // 2, si % 2
                    t.wait_ge(sW.h, wp_ld[si])
                    if si == 0:
                        t.wait_ge(sS.h, sq_cnt[15])
                    elif si == 1:
                        t.wait_ge(sS.h, sq_cnt[FC - 1])
                    if fh == 0 and dc >= 4:
                        t.wait_ge(sV.h, scan_cnt[dc - 4])
                    mm = None
                    for sub in range(2):
                        ysl = yvb(dc, sub)
                        for fi in range(16):
                            f = 16 * fh + fi
                            mm = t.matmul(ysl, wp_ring[:, si % 2, fi, :],
                                          h_sb[:, f, sub * 512:sub * 512 + 512],
                                          start=(f == 0), stop=(f == FC - 1))
                    sT.inc(mm)

            @blk.scalar
            def _(s, u_cnt=u_cnt):
                for f in range(FC):
                    psu = pAB if f % 2 == 0 else pCD
                    s.wait_ge(sT.h, u_cnt[f])
                    sS.inc(s.activation(r_sb[:, f % 2, :], psu[:], AF.Relu))
                    sS.inc(s.activation(h_sb[:, f, :], r_sb[:, f % 2, :], AF.Square))

            def scan_ops(eng, y, xsl, ss, mem, acc, fused):
                eng.tensor_single_scalar(acc, y, 1.0, ALU.is_gt)
                if fused:
                    eng.scalar_tensor_tensor(mem, y, 1.9, acc, ALU.mult, ALU.subtract)
                else:
                    eng.tensor_scalar_mul(mem, y, 1.9)
                    eng.tensor_tensor(mem, mem, acc, ALU.subtract)
                for _step in range(3):
                    eng.tensor_single_scalar(ss, mem, 1.0, ALU.is_gt)
                    eng.tensor_tensor(acc, acc, ss, ALU.add)
                    if fused:
                        eng.scalar_tensor_tensor(mem, mem, 0.9, y, ALU.mult, ALU.add)
                    else:
                        eng.tensor_scalar_mul(mem, mem, 0.9)
                        eng.tensor_tensor(mem, mem, y, ALU.add)
                    eng.tensor_tensor(mem, mem, ss, ALU.subtract)
                eng.tensor_single_scalar(ss, mem, 1.0, ALU.is_gt)
                eng.tensor_tensor(acc, acc, ss, ALU.add)
                if fused:
                    return eng.scalar_tensor_tensor(xsl, acc, 0.1, xsl, ALU.mult, ALU.add)
                eng.tensor_scalar_mul(ss, acc, 0.1)
                return eng.tensor_tensor(xsl, xsl, ss, ALU.add)

            @blk.vector
            def _(v, q0=q0, wp_mm=wp_mm):
                for dc in range(DC):
                    v.wait_ge(sT.h, wp_mm[2 * dc + 1])
                    last = None
                    for sub in range(2):
                        y = yvb(dc, sub)
                        xsl = x_sb[:, dc, q0 + sub * 512:q0 + sub * 512 + 512]
                        last = scan_ops(v, y, xsl, s_scan[:], mem_scan[:], acc_scan[:], True)
                    sV.inc(last)

            assert sT.n == T0 + FC + 16
            assert sV.n == V0 + DC
            assert sS.n == S0 + 2 * FC
            assert sW.n == W0 + 16 * (FC // 2 + 16)
            prev_scan = scan_cnt
            half_scan.append(scan_cnt[DC - 1])

        fes.close()
        return half_scan

    emit_ffn_era("1")

    aes = ExitStack()
    q_sb = sb(aes, "q_sb", [128, DC, T], BF16)
    wq_sb = sb(aes, "wq_sb", [128, DC, D], BF16)
    bes = ExitStack()
    kstage = sb(bes, "kstage", [128, 4, 512], BF16)
    v_sb = sb(bes, "v_sb", [128, NTC, D], BF16)

    ln2_base = len(ln_apply)
    for qq in range(NQS):
        emit_ln(qq)
    ln2_apply = ln_apply[ln2_base:ln2_base + NQS]

    def load_w(dram):
        @blk.gpsimd
        def _(g):
            sT.wait(g)
            sW.inc(g.dma_start(wq_sb[:], dram), 16)

    def emit_proj_q():
        T0, S0 = sT.n, sS.n
        evac = {i: S0 + i + 1 for i in range(DC * NQS)}
        mmc = {i: T0 + i + 1 for i in range(DC * NQS)}

        @blk.tensor
        def _(t):
            sW.wait(t)
            for i in range(DC * NQS):
                m, qn = divmod(i, NQS)
                psu = pAB if i % 2 == 0 else pCD
                if i == 0:
                    sS.wait(t)
                if i >= 2:
                    t.wait_ge(sS.h, evac[i - 2])
                for k in range(DC):
                    t.matmul(psu[:, 0:512], wq_sb[:, k, 128 * m:128 * m + 128],
                             ln_sb[:, k, qn * 512:qn * 512 + 512], start=(k == 0), stop=False)
                sT.inc(t.matmul(psu[:, 0:512], b_sb[0:1, 128 * m:128 * m + 128],
                                ones_row[0:1, :], start=False, stop=True))

        @blk.scalar
        def _(s):
            for i in range(DC * NQS):
                m, qn = divmod(i, NQS)
                psu = pAB if i % 2 == 0 else pCD
                s.wait_ge(sT.h, mmc[i])
                sS.inc(s.copy(q_sb[:, m, qn * 512:qn * 512 + 512], psu[:, 0:512]))

    load_w(wkT)
    load_bias(1)

    def emit_proj_k():
        T0, S0, V0, SP0 = sT.n, sS.n, sV.n, sSP.n
        kred = {qn: V0 + qn + 1 for qn in range(NQS)}
        kevac = {i: S0 + 2 * i + 1 for i in range(DC * NQS)}
        ksq = {i: S0 + 2 * i + 2 for i in range(DC * NQS)}
        proj_mm = {i: T0 + 2 * i + 1 for i in range(DC * NQS)}
        knorm_mm = {i: T0 + 2 * i + 2 for i in range(DC * NQS)}
        kspill = {i: SP0 + 16 * (i + 1) for i in range(DC * NQS)}

        @blk.tensor
        def _(t):
            sW.wait(t)
            for qn in range(NQS):
                for m in range(DC):
                    i = qn * DC + m
                    psu = pAB if i % 2 == 0 else pCD
                    if m == 0 and qn < NQS:
                        t.wait_ge(sV.h, ln2_apply[qn])
                    if i >= 2:
                        t.wait_ge(sS.h, kevac[i - 2])
                    for k in range(DC):
                        t.matmul(psu[:, 0:512], wq_sb[:, k, 128 * m:128 * m + 128],
                                 ln_sb[:, k, qn * 512:qn * 512 + 512], start=(k == 0), stop=False)
                    sT.inc(t.matmul(psu[:, 0:512], b_sb[0:1, 128 * m:128 * m + 128],
                                    ones_row[0:1, :], start=False, stop=True))
                    t.wait_ge(sS.h, ksq[i])
                    if i == 0:
                        t.wait_ge(sV.h, ln2_apply[NQS - 1])
                    elif m == 0:
                        t.wait_ge(sV.h, kred[qn - 1])
                    sT.inc(t.matmul(pR1[0:16, :], masks[:, m, :], xsq_sb[:, i % 2, :],
                                    start=(m == 0), stop=(m == DC - 1)))

        @blk.scalar
        def _(s):
            for i in range(DC * NQS):
                psu = pAB if i % 2 == 0 else pCD
                s.wait_ge(sT.h, proj_mm[i])
                if i >= 4:
                    s.wait_ge(sSP.h, kspill[i - 4])
                if i >= 2:
                    s.wait_ge(sT.h, knorm_mm[i - 2])
                sS.inc(s.copy(kstage[:, i % 4, :], psu[:, 0:512]))
                sS.inc(s.activation(xsq_sb[:, i % 2, :], kstage[:, i % 4, :], AF.Square))

        @blk.sync
        def _(sy):
            for qn in range(NQS):
                for m in range(DC):
                    i = qn * DC + m
                    sy.wait_ge(sS.h, kevac[i])
                    sSP.inc(sy.dma_start(cc_inK[128 * m:128 * m + 128, qn * 512:qn * 512 + 512],
                                         kstage[:, i % 4, :]), 16)

        @blk.vector
        def _(v):
            for qn in range(NQS):
                v.wait_ge(sT.h, knorm_mm[qn * DC + DC - 1])
                if qn == 0:
                    sV.inc(v.tensor_reduce(nmx2[:], pR1[0:16, :], AX.X, ALU.max))
                else:
                    v.tensor_reduce(nmx3[:], pR1[0:16, :], AX.X, ALU.max)
                    sV.inc(v.tensor_tensor(nmx2[:], nmx2[:], nmx3[:], ALU.max))

    emit_proj_k()

    load_w(wvT)
    load_bias(2)

    @blk.gpsimd
    def _(g):
        sSP.wait(g)
        sCC.inc(g.collective_compute(
            "AllGather", ALU.bypass, ins=[cc_inK], outs=[cc_outK],
            replica_groups=[[0, 1], [2, 3], [4, 5], [6, 7]]))

    def emit_proj_v():
        T0, S0 = sT.n, sS.n
        vevac = {tc: S0 + tc + 1 for tc in range(NTC)}
        vmm = {tc: T0 + tc + 1 for tc in range(NTC)}

        @blk.tensor
        def _(t):
            sW.wait(t)
            for tc in range(NTC):
                psu = pAB if tc % 2 == 0 else pCD
                if tc < 2:
                    sV.wait(t)
                else:
                    t.wait_ge(sS.h, vevac[tc - 2])
                mm = None
                for half in range(2):
                    for k in range(DC):
                        t.matmul(psu[:, half * 512:half * 512 + 512],
                                 ln_sb[:, k, tc * 128:tc * 128 + 128],
                                 wq_sb[:, k, half * 512:half * 512 + 512],
                                 start=(k == 0), stop=False)
                    mm = t.matmul(psu[:, half * 512:half * 512 + 512],
                                  ones_row[0:1, 0:128],
                                  b_sb[0:1, half * 512:half * 512 + 512],
                                  start=False, stop=True)
                sT.inc(mm)

        @blk.scalar
        def _(s):
            for tc in range(NTC):
                psu = pAB if tc % 2 == 0 else pCD
                s.wait_ge(sT.h, vmm[tc])
                s.copy(v_sb[:, tc, 0:512], psu[:, 0:512])
                sS.inc(s.copy(v_sb[:, tc, 512:1024], psu[:, 512:1024]))

    emit_proj_v()

    @blk.sync
    def _(sy):
        sS.wait(sy)
        vdst = cc_inV.rearrange("r (two d) -> (r two) d", two=2) \
                     .rearrange("(tc p) d -> p tc d", p=128)
        sSP.inc(sy.dma_start(vdst, v_sb[:]), 16)

    @blk.gpsimd
    def _(g):
        sSP.wait(g)
        sCC.inc(g.collective_compute(
            "AllGather", ALU.bypass, ins=[cc_inV], outs=[cc_outV],
            replica_groups=[[0, 1], [2, 3], [4, 5], [6, 7]]))

    load_w(wqT)
    load_bias(0)
    emit_proj_q()

    def emit_norms_q():
        T0, S0 = sT.n, sS.n
        V0n = sV.n
        qred = {qs: V0n + qs + 1 for qs in range(NQS)}
        sqq = {}
        nm_mm = {}
        i = 0
        for qs in range(NQS):
            for c in range(DC):
                sqq[(qs, c)] = S0 + i + 1
                nm_mm[(qs, c)] = T0 + i + 1
                i += 1

        @blk.scalar
        def _(s):
            sT.wait(s)
            first = True
            for qs in range(NQS):
                for c in range(DC):
                    if not first and not (qs == 0 and c == 1):
                        prev = (qs, c - 2) if c >= 2 else (qs - 1, DC - 2 + c)
                        s.wait_ge(sT.h, nm_mm[prev])
                    first = False
                    sS.inc(s.activation(xsq_sb[:, c % 2, :],
                                        q_sb[:, c, qs * 512:qs * 512 + 512], AF.Square))

        @blk.tensor
        def _(t):
            for qs in range(NQS):
                for c in range(DC):
                    t.wait_ge(sS.h, sqq[(qs, c)])
                    if qs == 0 and c == 0:
                        sV.wait(t)
                    elif c == 0:
                        t.wait_ge(sV.h, qred[qs - 1])
                    sT.inc(t.matmul(pR1[0:16, :], masks[:, c, :], xsq_sb[:, c % 2, :],
                                    start=(c == 0), stop=(c == DC - 1)))

        @blk.vector
        def _(v):
            for qs in range(NQS):
                v.wait_ge(sT.h, nm_mm[(qs, DC - 1)])
                if qs == 0:
                    sV.inc(v.tensor_reduce(nmx[:], pR1[0:16, :], AX.X, ALU.max))
                else:
                    v.tensor_reduce(nmx3[:], pR1[0:16, :], AX.X, ALU.max)
                    sV.inc(v.tensor_tensor(nmx[:], nmx[:], nmx3[:], ALU.max))

    emit_norms_q()

    @blk.vector
    def _(v):
        sV.wait(v)
        sV.inc(v.tensor_tensor(nmx[:], nmx[:], nmx2[:], ALU.mult))

    @blk.scalar
    def _(s):
        sV.wait(s)
        sS.inc(s.activation(nmx3[:], nmx[:], AF.Sqrt))

    @blk.vector
    def _(v):
        sS.wait(v)
        v.memset(t32a[:], 0.0)
        sV.inc(v.tensor_copy(t32a[0:16, 0:1], nmx3[:]))
        sV.wait(v)
        sV.inc(v.transpose(t32b[:], t32a[:]))
        sV.wait(v)
        sV.inc(v.tensor_copy(brow_f[:], t32b[0:1, 0:16]))

    @blk.scalar
    def _(s):
        sV.wait(s)
        sS.inc(s.copy(brow_b[:], t32b[0:1, 0:16]))

    @blk.tensor
    def _(t):
        sS.wait(t)
        t.matmul(pR2[:, 0:16], s16_row[:], brow_b[:], start=True, stop=False)
        sT.inc(t.matmul(pR2[:, 0:16], m44_row[:], ones_row[0:1, 0:16], start=False, stop=True))

    @blk.vector
    def _(v):
        sT.wait(v)
        sV.inc(v.tensor_reduce(bneg_sb[:], pR2[:, 0:16], AX.X, ALU.max, negate=True))

    bneg_ready = sV.n
    bes.close()

    load_w(woT)
    load_bias(3)

    des = ExitStack()
    kp_sb = sb(des, "kp_sb", [128, 2, 2, T], BF16)
    vp_sb = sb(des, "vp_sb", [128, 2, 2, NTC, 128], BF16)
    expS = sb(des, "expS", [128, 2, 1024], BF16)
    rec_bf = sb(des, "rec_bf", [33, 512], BF16)
    binv_sb = sb(des, "binv_sb", [128, 512], BF16)
    avstage = sb(des, "avstage", [128, 2, 512], BF16)

    kv_cnt = {}
    pair_t_done = {}
    last_norm_v = [bneg_ready]
    av_spill = {}
    prev_pair_av = {}

    def emit_kv_load(p, prev_t):
        buf = p % 2

        @blk.sync
        def _(sy):
            sy.wait_ge(sCC.h, 1)
            if prev_t:
                sy.wait_ge(sT.h, prev_t)
            for s_ in range(2):
                sKV.inc(sy.dma_start(kp_sb[:, buf, s_, :], cc_outK[s_, 128 * p:128 * p + 128, :]), 16)
            sy.wait_ge(sCC.h, 2)
            for s_ in range(2):
                vsrc = cc_outV[s_].rearrange("r (two d) -> (r two) d", two=2)
                vsrc = vsrc[:, 128 * p:128 * p + 128].rearrange("(tc p2) d -> p2 tc d", p2=128)
                sKV.inc(sy.dma_start(vp_sb[:, buf, s_, :, :], vsrc), 16)
            kv_cnt[p] = sKV.n

    emit_kv_load(0, 0)
    emit_kv_load(1, 0)

    units_norm = {}
    prev_exp = {}

    for p in range(NP):
        buf = p % 2
        T0, S0, V0, SP0 = sT.n, sS.n, sV.n, sSP.n
        qk_cnt = {}
        av_cnt = {}
        binv_mm = {}
        n = T0
        for qs in range(NQS):
            for kc in range(32):
                n += 1
                qk_cnt[(qs, kc)] = n
                if kc >= 1:
                    n += 1
                    av_cnt[(qs, kc - 1)] = n
                if kc == 8 and qs > 0:
                    n += 1
                    binv_mm[qs - 1] = n
            n += 1
            av_cnt[(qs, 31)] = n
        n += 1
        binv_mm[NQS - 1] = n
        exp_cnt = {(qs, kc): S0 + qs * 32 + kc + 1 for qs in range(NQS) for kc in range(32)}
        rec_v = {qs: V0 + qs * 2 + 1 for qs in range(NQS)}
        norm_v = {qs: V0 + qs * 2 + 2 for qs in range(NQS)}
        spill_n = {qs: SP0 + 16 * (qs + 1) for qs in range(NQS)}
        for qs in range(NQS):
            units_norm[p * NQS + qs] = norm_v[qs]

        @blk.tensor
        def _(t, p=p, buf=buf, qk_cnt=qk_cnt, av_cnt=av_cnt, binv_mm=binv_mm,
              exp_cnt=exp_cnt, rec_v=rec_v, norm_v=norm_v, prevx=prev_exp):
            def emit_av(qs, kc):
                u = p * NQS + qs
                avp = (pR2, pR3)[u % 2]
                smp = (pR1, pR4)[u % 2]
                t.wait_ge(sS.h, exp_cnt[(qs, kc)])
                if kc == 0 and u >= 2:
                    t.wait_ge(sV.h, units_norm[u - 2])
                slot, j = kc // 16, kc % 16
                t.matmul(avp[0:64, :], vp_sb[:, buf, slot, j, 0:64],
                         expS[:, kc % 2, 0:512], start=(kc == 0), stop=(kc == 31))
                t.matmul(avp[64:128, :], vp_sb[:, buf, slot, j, 64:128],
                         expS[:, kc % 2, 512:1024], start=(kc == 0), stop=(kc == 31),
                         tile_position=(0, 64), skip_group_check=True)
                t.matmul(smp[0:1, :], ones_col[:], expS[:, kc % 2, 0:512],
                         start=(kc == 0), stop=(kc == 31))
                sT.inc(t.matmul(smp[32:33, :], ones_col[:], expS[:, kc % 2, 512:1024],
                                start=(kc == 0), stop=(kc == 31), tile_position=(0, 32),
                                skip_group_check=True))

            def emit_binv(qs):
                u = p * NQS + qs
                smp = (pR1, pR4)[u % 2]
                t.wait_ge(sV.h, rec_v[qs])
                t.matmul(smp[0:64, 0:512], ones33[0:1, :], rec_bf[0:1, :], start=True, stop=False)
                sT.inc(t.matmul(smp[64:128, 0:512], ones33[32:33, :], rec_bf[32:33, :],
                                start=True, stop=True, tile_position=(32, 64),
                                skip_group_check=True))

            for qs in range(NQS):
                q0 = qs * 512
                for kc in range(32):
                    sc = pAB if kc % 2 == 0 else pCD
                    slot, j = kc // 16, kc % 16
                    if kc == 0 and qs == 0:
                        t.wait_ge(sKV.h, kv_cnt[p])
                        if p == 0:
                            t.wait_ge(sV.h, last_norm_v[0])
                    if kc >= 2:
                        t.wait_ge(sS.h, exp_cnt[(qs, kc - 2)])
                    elif qs > 0:
                        t.wait_ge(sS.h, exp_cnt[(qs - 1, 30 + kc)])
                    elif p > 0:
                        s_prev = prevx.get(30 + kc)
                        if s_prev:
                            t.wait_ge(sS.h, s_prev)
                    t.matmul(sc[:, 0:512], kp_sb[0:64, buf, slot, 128 * j:128 * j + 128],
                             q_sb[0:64, p, q0:q0 + 512], start=True, stop=True)
                    sT.inc(t.matmul(sc[:, 512:1024], kp_sb[64:128, buf, slot, 128 * j:128 * j + 128],
                                    q_sb[64:128, p, q0:q0 + 512], start=True, stop=True,
                                    tile_position=(64, 0)))
                    if kc >= 1:
                        emit_av(qs, kc - 1)
                    if kc == 8 and qs > 0:
                        emit_binv(qs - 1)
                emit_av(qs, 31)
            emit_binv(NQS - 1)

        @blk.scalar
        def _(s, p=p, qk_cnt=qk_cnt, av_cnt=av_cnt, prev=prev_pair_av):
            for qs in range(NQS):
                for kc in range(32):
                    sc = pAB if kc % 2 == 0 else pCD
                    s.wait_ge(sT.h, qk_cnt[(qs, kc)])
                    if kc >= 2:
                        s.wait_ge(sT.h, av_cnt[(qs, kc - 2)])
                    elif qs > 0:
                        s.wait_ge(sT.h, av_cnt[(qs - 1, 30 + kc)])
                    elif p > 0:
                        s.wait_ge(sT.h, prev[30 + kc])
                    sS.inc(s.activation(expS[:, kc % 2, :], sc[:], AF.Exp,
                                        bias=bneg_sb[:], scale=0.125))

        @blk.vector
        def _(v, p=p, av_cnt=av_cnt, binv_mm=binv_mm):
            for qs in range(NQS):
                u = p * NQS + qs
                avp = (pR2, pR3)[u % 2]
                smp = (pR1, pR4)[u % 2]
                v.wait_ge(sT.h, av_cnt[(qs, 31)])
                with nc.allow_low_precision(reason="softmax normalizer feeds bf16 matmul"):
                    sV.inc(v.reciprocal(rec_bf[:], smp[0:33, :]))
                v.wait_ge(sT.h, binv_mm[qs])
                if u >= 2:
                    v.wait_ge(sSP.h, av_spill[u - 2] if qs < 2 else spill_n[qs - 2])
                v.tensor_copy(binv_sb[:], smp[:, 0:512])
                sV.inc(v.tensor_tensor(avstage[:, qs % 2, :], avp[:], binv_sb[:], ALU.mult))

        @blk.sync
        def _(sy, p=p, norm_v=norm_v):
            for qs in range(NQS):
                sy.wait_ge(sV.h, norm_v[qs])
                av_spill[p * NQS + qs] = sSP.inc(sy.dma_start(
                    attnv_dram[128 * p:128 * p + 128, qs * 512:qs * 512 + 512],
                    avstage[:, qs % 2, :]), 16)

        prev_pair_av = {30: av_cnt[(NQS - 1, 30)], 31: av_cnt[(NQS - 1, 31)]}
        prev_exp = {30: exp_cnt[(NQS - 1, 30)], 31: exp_cnt[(NQS - 1, 31)]}
        last_norm_v[0] = norm_v[NQS - 1]
        assert sT.n == T0 + NQS * 65
        assert sS.n == S0 + NQS * 32
        assert sV.n == V0 + NQS * 2
        pair_t_done[p] = sT.n
        if p + 2 < NP:
            emit_kv_load(p + 2, pair_t_done[p])

    des.close()
    wes = ExitStack()
    av_ring = sb(wes, "av_ring", [128, 2, DC, 512], BF16)

    def emit_wo():
        T0, V0, KV0 = sT.n, sV.n, sKV.n
        wo_mm = {i: T0 + i + 1 for i in range(NQS * DC)}
        wo_ev = {i: V0 + i + 1 for i in range(NQS * DC)}
        av_ld = {qn: KV0 + 16 * (qn + 1) for qn in range(NQS)}

        @blk.sync
        def _(sy):
            for qn in range(NQS):
                if qn == 0:
                    sy.wait_ge(sSP.h, av_spill[NP * NQS - 1])
                if qn >= 2:
                    sy.wait_ge(sT.h, wo_mm[(qn - 2) * DC + DC - 1])
                src_ap = attnv_dram[:, qn * 512:qn * 512 + 512].rearrange("(k p) t -> p k t", p=128)
                sKV.inc(sy.dma_start(av_ring[:, qn % 2, :, :], src_ap), 16)

        @blk.tensor
        def _(t):
            sW.wait(t)
            for qn in range(NQS):
                t.wait_ge(sKV.h, av_ld[qn])
                for m in range(DC):
                    i = qn * DC + m
                    psu = pAB if i % 2 == 0 else pCD
                    if i < 2:
                        t.wait_ge(sV.h, last_norm_v[0])
                    else:
                        t.wait_ge(sV.h, wo_ev[i - 2])
                    for k in range(DC):
                        t.matmul(psu[:, 0:512], wq_sb[:, k, 128 * m:128 * m + 128],
                                 av_ring[:, qn % 2, k, :], start=(k == 0), stop=False)
                    sT.inc(t.matmul(psu[:, 0:512], b_sb[0:1, 128 * m:128 * m + 128],
                                    ones_row[0:1, :], start=False, stop=True))

        @blk.vector
        def _(v):
            for qn in range(NQS):
                for m in range(DC):
                    i = qn * DC + m
                    psu = pAB if i % 2 == 0 else pCD
                    v.wait_ge(sT.h, wo_mm[i])
                    xsl = x_sb[:, m, qn * 512:qn * 512 + 512]
                    sV.inc(v.tensor_tensor(xsl, xsl, psu[:, 0:512], ALU.add))

    emit_wo()
    wes.close()
    aes.close()

    e2_half = emit_ffn_era("2")

    @blk.sync
    def _(sy):
        dst = out_t.rearrange("(c p) t -> p c t", p=128)
        for hh in range(2):
            sy.wait_ge(sV.h, e2_half[hh])
            sOUT.inc(sy.dma_start(dst[:, :, hh * 1024:hh * 1024 + 1024],
                                  x_sb[:, :, hh * 1024:hh * 1024 + 1024]), 16)
        sOUT.wait(sy)

    es.close()
    return nc


_NC_CACHE = None


def kernel(L0_attn_x, L0_attn_Wq, L0_attn_bq, L0_attn_Wk, L0_attn_bk,
           L0_attn_Wv, L0_attn_bv, L0_attn_Wo, L0_attn_bo,
           L0_ffn_W_fc, L0_ffn_W_proj):
    global _NC_CACHE
    x = np.asarray(L0_attn_x, dtype=np.float32)
    B, S, _ = x.shape
    assert (B, S) == (4, 4096)

    wqT = np.ascontiguousarray(np.asarray(L0_attn_Wq, np.float32).T.reshape(DC, 128, D).transpose(1, 0, 2))
    wkT = np.ascontiguousarray(np.asarray(L0_attn_Wk, np.float32).T.reshape(DC, 128, D).transpose(1, 0, 2))
    wvT = np.ascontiguousarray(np.asarray(L0_attn_Wv, np.float32).T.reshape(DC, 128, D).transpose(1, 0, 2))
    woT = np.ascontiguousarray(np.asarray(L0_attn_Wo, np.float32).T.reshape(DC, 128, D).transpose(1, 0, 2))
    wfcT = np.asarray(L0_ffn_W_fc, np.float32).T
    wfcb = np.ascontiguousarray(wfcT.reshape(DC, 128, FC, 128).transpose(2, 1, 0, 3).reshape(FC // 2, 2, 128, DC, 128).transpose(0, 2, 1, 3, 4))
    wprojT = np.asarray(L0_ffn_W_proj, np.float32).T
    wpb = np.ascontiguousarray(wprojT.reshape(2, 16, 128, DC, 128).transpose(3, 0, 2, 1, 4))
    brow = np.ascontiguousarray(np.concatenate(
        [np.asarray(b, np.float32).reshape(-1)
         for b in (L0_attn_bq, L0_attn_bk, L0_attn_bv, L0_attn_bo)])[None, :])
    cstv = np.full((128, 1), EPS, np.float32)

    if _NC_CACHE is None:
        _NC_CACHE = build_program()
    nc = _NC_CACHE

    in_maps = []
    for c in range(8):
        b, h = c // 2, c % 2
        xs = np.ascontiguousarray(x[b, h * T:(h + 1) * T, :].T)
        in_maps.append(dict(x_t=xs, wqT=wqT, wkT=wkT, wvT=wvT, woT=woT,
                            wfcb=wfcb, wpb=wpb, brow=brow, cst=cstv))

    res = run_bass_kernel_spmd(nc, in_maps, core_ids=list(range(8)))
    out = np.empty((B, S, D), np.float32)
    for c in range(8):
        b, h = c // 2, c % 2
        out[b, h * T:(h + 1) * T, :] = res.results[c]["out_t"].T
    return out


# version 21
# speedup vs baseline: 1.1595x; 1.0109x over previous
import numpy as np
from contextlib import ExitStack

import concourse.bass as bass
import concourse.mybir as mybir
from concourse.bass_utils import run_bass_kernel_spmd

F32 = mybir.dt.float32
BF16 = mybir.dt.bfloat16
AF = mybir.ActivationFunctionType
ALU = mybir.AluOpType
AX = mybir.AxisListType

D = 1024
DC = 8
T = 2048
FF = 4096
FC = 32
NP = 8
QS = 512
NQS = T // QS
NTC = T // 128
EPS = 1e-5


class Sem:
    def __init__(self, es, nc, name):
        self.h = es.enter_context(nc.semaphore(name))
        self.n = 0

    def inc(self, instr, k=1):
        instr.then_inc(self.h, k)
        self.n += k
        return self.n

    def wait(self, eng, n=None):
        eng.wait_ge(self.h, self.n if n is None else n)


def build_program(debug=False):
    nc = bass.Bass("TRN2", target_bir_lowering=False)

    x_t = nc.dram_tensor("x_t", [D, T], F32, kind="ExternalInput").ap()
    wqT = nc.dram_tensor("wqT", [128, DC, D], F32, kind="ExternalInput").ap()
    wkT = nc.dram_tensor("wkT", [128, DC, D], F32, kind="ExternalInput").ap()
    wvT = nc.dram_tensor("wvT", [128, DC, D], F32, kind="ExternalInput").ap()
    woT = nc.dram_tensor("woT", [128, DC, D], F32, kind="ExternalInput").ap()
    wfcb = nc.dram_tensor("wfcb", [FC // 2, 128, 2, DC, 128], F32, kind="ExternalInput").ap()
    wpb = nc.dram_tensor("wpb", [DC, 2, 128, 16, 128], F32, kind="ExternalInput").ap()
    brow = nc.dram_tensor("brow", [1, 4 * D], F32, kind="ExternalInput").ap()
    cst = nc.dram_tensor("cst", [128, 1], F32, kind="ExternalInput").ap()
    out_t = nc.dram_tensor("out_t", [D, T], F32, kind="ExternalOutput").ap()
    cc_inK = nc.dram_tensor("cc_inK", [1024, 2048], BF16, kind="Internal").ap()
    cc_inV = nc.dram_tensor("cc_inV", [1024, 2048], BF16, kind="Internal").ap()
    cc_outK = nc.dram_tensor("cc_outK", [2, 1024, 2048], BF16, kind="Internal").ap()
    cc_outV = nc.dram_tensor("cc_outV", [2, 1024, 2048], BF16, kind="Internal").ap()
    attnv_dram = nc.dram_tensor("attnv_dram", [D, T], BF16, kind="Internal").ap()

    es = ExitStack()

    def sb(stack, name, shape, dt):
        return stack.enter_context(nc.sbuf_tensor(name, shape, dt))

    x_sb = sb(es, "x_sb", [128, DC, T], F32)
    ln_sb = sb(es, "ln_sb", [128, DC, T], BF16)
    ones_col = sb(es, "ones_col", [128, 1], BF16)
    ones_row = sb(es, "ones_row", [1, 512], BF16)
    masks = sb(es, "masks", [128, DC, 16], BF16)
    b_sb = sb(es, "b_sb", [1, 2, D], BF16)
    eps_sb = sb(es, "eps_sb", [128, 1], F32)
    bneg_sb = sb(es, "bneg_sb", [128, 1], F32)
    rows_sb = sb(es, "rows_sb", [1, 2, 512], F32)
    ab_bf = sb(es, "ab_bf", [1, 1024], BF16)
    ones33 = sb(es, "ones33", [33, 64], BF16)
    s16_row = sb(es, "s16_row", [1, 128], BF16)
    m44_row = sb(es, "m44_row", [1, 128], BF16)
    nmx = sb(es, "nmx", [16, 1], F32)
    nmx2 = sb(es, "nmx2", [16, 1], F32)
    nmx3 = sb(es, "nmx3", [16, 1], F32)
    xb_sb = sb(es, "xb_sb", [128, 2, 512], BF16)
    xsq_sb = sb(es, "xsq_sb", [128, 2, 512], BF16)
    tmp_sb = sb(es, "tmp_sb", [128, 512], F32)
    brow_f = sb(es, "brow_f", [1, 16], F32)
    t32a = sb(es, "t32a", [32, 32], F32)
    t32b = sb(es, "t32b", [32, 32], F32)
    brow_b = sb(es, "brow_b", [1, 16], BF16)

    pAB = es.enter_context(nc.psum_tensor("pAB", [128, 1024], F32))
    pCD = es.enter_context(nc.psum_tensor("pCD", [128, 1024], F32))
    pR1 = es.enter_context(nc.psum_tensor("pR1", [128, 512], F32))
    pR2 = es.enter_context(nc.psum_tensor("pR2", [128, 512], F32))
    pR3 = es.enter_context(nc.psum_tensor("pR3", [128, 512], F32))
    pR4 = es.enter_context(nc.psum_tensor("pR4", [128, 512], F32))

    sIN = Sem(es, nc, "sIN")
    sW = Sem(es, nc, "sW")
    sT = Sem(es, nc, "sT")
    sV = Sem(es, nc, "sV")
    sS = Sem(es, nc, "sS")
    sSP = Sem(es, nc, "sSP")
    sCC = Sem(es, nc, "sCC")
    sKV = Sem(es, nc, "sKV")
    sOUT = Sem(es, nc, "sOUT")

    blk = es.enter_context(nc.Block())

    @blk.sync
    def _(sy):
        sIN.inc(sy.dma_start(eps_sb[:], cst), 16)
        for qq in range(NQS):
            sIN.inc(sy.dma_start(x_sb[:, :, qq * 512:qq * 512 + 512],
                                 x_t[:, qq * 512:qq * 512 + 512]
                                 .rearrange("(c p) t -> p c t", p=128)), 16)

    @blk.vector
    def _(v):
        v.memset(ones_col[:], 1.0)
        v.memset(ones_row[:], 1.0)
        v.memset(ones33[0:1, :], 1.0)
        v.memset(ones33[32:33, :], 1.0)
        v.memset(s16_row[:], 0.125)
        v.memset(m44_row[:], -44.0)
        v.memset(masks[:], 0.0)
        for c in range(DC):
            v.memset(masks[0:64, c, 2 * c:2 * c + 1], 1.0)
        last = None
        for c in range(DC):
            last = v.memset(masks[64:128, c, 2 * c + 1:2 * c + 2], 1.0)
        sV.inc(last)

    def load_bias(r, slot):
        @blk.gpsimd
        def _(g):
            sT.wait(g)
            sW.inc(g.dma_start(b_sb[0:1, slot, :], brow[0:1, r * D:(r + 1) * D]), 16)

    def load_bias2(r, slot, gate):
        @blk.gpsimd
        def _(g):
            g.wait_ge(sT.h, gate)
            sW.inc(g.dma_start(b_sb[0:1, slot, :], brow[0:1, r * D:(r + 1) * D]), 16)
        return sW.n

    def load_w2(dram, gate, dst=None):
        @blk.gpsimd
        def _(g):
            g.wait_ge(sT.h, gate)
            sW.inc(g.dma_start(dst if dst is not None else wq_sb[:], dram), 16)
        return sW.n

    rowA = rows_sb[0:1, 0, :]
    rowB = rows_sb[0:1, 1, :]
    rowC = tmp_sb[0:1, :]

    ln_apply = []

    def emit_ln(qq, xgate=None):
        q0 = qq * 512
        T0, V0, S0 = sT.n, sV.n, sS.n
        ln_apply.append(V0 + DC + 3)
        ln_mm = {c: T0 + c + 1 for c in range(DC)}
        cp_done = {c: V0 + c + 1 for c in range(DC)}
        sq_done = {c: S0 + c + 1 for c in range(DC)}

        @blk.vector
        def _(v):
            if xgate is not None:
                v.wait_ge(sIN.h, xgate)
            if sOUT.n > 0:
                sOUT.wait(v)
            for c in range(DC):
                if c >= 2:
                    v.wait_ge(sT.h, ln_mm[c - 2])
                sV.inc(v.tensor_copy(xb_sb[:, c % 2, :], x_sb[:, c, q0:q0 + 512]))

        @blk.scalar
        def _(s):
            for c in range(DC):
                s.wait_ge(sV.h, cp_done[c])
                if c >= 2:
                    s.wait_ge(sT.h, ln_mm[c - 2])
                sS.inc(s.activation(xsq_sb[:, c % 2, :], xb_sb[:, c % 2, :], AF.Square))
            s.wait_ge(sV.h, V0 + DC + 1)
            if xgate is not None and qq == 0:
                s.wait_ge(sIN.h, 16)
            sS.inc(s.activation(rowC, rowB, AF.Sqrt, bias=eps_sb[0:1, :]))

        @blk.tensor
        def _(t):
            for c in range(DC):
                t.wait_ge(sS.h, sq_done[c])
                if c == 0:
                    t.wait_ge(sV.h, V0)
                t.matmul(pR1[0:1, :], ones_col[:], xb_sb[:, c % 2, :], start=(c == 0), stop=(c == DC - 1))
                sT.inc(t.matmul(pR2[0:1, :], ones_col[:], xsq_sb[:, c % 2, :],
                                start=(c == 0), stop=(c == DC - 1)))
            t.wait_ge(sV.h, V0 + DC + 2)
            t.matmul(pR3[:, :], ones_row[0:1, 0:128], ab_bf[0:1, 0:512], start=True, stop=True)
            sT.inc(t.matmul(pR4[:, :], ones_row[0:1, 0:128], ab_bf[0:1, 512:1024],
                            start=True, stop=True))

        @blk.vector
        def _(v):
            v.wait_ge(sT.h, T0 + DC)
            v.tensor_scalar_mul(rowA, pR1[0:1, :], 1.0 / D)
            v.tensor_tensor(rowB, rowA, rowA, ALU.mult)
            sV.inc(v.scalar_tensor_tensor(rowB, pR2[0:1, :], 1.0 / D, rowB, ALU.mult, ALU.subtract))
            v.wait_ge(sS.h, S0 + DC + 1)
            v.reciprocal(rowB, rowC)
            v.scalar_tensor_tensor(rowC, rowA, -1.0, rowB, ALU.mult, ALU.mult)
            v.tensor_copy(ab_bf[0:1, 0:512], rowB)
            sV.inc(v.tensor_copy(ab_bf[0:1, 512:1024], rowC))
            v.wait_ge(sT.h, T0 + DC + 1)
            last = None
            for c in range(DC):
                v.tensor_tensor(tmp_sb[:], x_sb[:, c, q0:q0 + 512], pR3[:, :], ALU.mult)
                last = v.tensor_tensor(ln_sb[:, c, q0:q0 + 512], tmp_sb[:], pR4[:, :], ALU.add)
            sV.inc(last)

        assert sT.n == T0 + DC + 1
        assert sV.n == V0 + DC + 3
        assert sS.n == S0 + DC + 1

    def emit_ffn_era(tag):
        fes = ExitStack()
        wfc_ring = sb(fes, "wfc_ring" + tag, [128, 2, 2, DC, 128], BF16)
        h_sb = sb(fes, "h_sb" + tag, [128, FC, 1024], BF16)
        wp_ring = sb(fes, "wp_ring" + tag, [128, 2, 16, 128], BF16)
        r_sb = sb(fes, "r_sb" + tag, [128, 1024], F32)
        s_scan = sb(fes, "s_scan" + tag, [128, 512], F32)
        mem_scan = sb(fes, "mem_scan" + tag, [128, 512], F32)
        acc_scan = sb(fes, "acc_scan" + tag, [128, 512], F32)

        era_t0 = sT.n
        era_v0 = sV.n
        half_scan = []

        def yvb(dc, sub):
            r = dc % 4
            if r == 0:
                return pAB[:, sub * 512:sub * 512 + 512]
            if r == 1:
                return pCD[:, sub * 512:sub * 512 + 512]
            if r == 2:
                return (pR1 if sub == 0 else pR2)[:, :]
            return (pR3 if sub == 0 else pR4)[:, :]

        apply_base = len(ln_apply)
        for qq in range(NQS):
            emit_ln(qq, xgate=(32 + 16 * qq) if tag == "1" else None)
        era_apply = ln_apply[apply_base:apply_base + NQS]
        prev_scan = None

        for hh in range(2):
            q0 = hh * 1024
            T0, V0, S0, W0 = sT.n, sV.n, sS.n, sW.n
            u_cnt = {f: T0 + f + 1 for f in range(FC)}
            wp_mm = {si: T0 + FC + si + 1 for si in range(16)}
            scan_cnt = {dc: V0 + dc + 1 for dc in range(DC)}
            relu_cnt = {f: S0 + 2 * f + 1 for f in range(FC)}
            sq_cnt = {f: S0 + 2 * f + 2 for f in range(FC)}
            wfc_ld = {p: W0 + 16 * (p + 1) for p in range(FC // 2)}
            wp_ld = {si: W0 + 16 * (FC // 2) + 16 * (si + 1) for si in range(16)}

            @blk.gpsimd
            def _(sy, hh=hh, T0=T0, u_cnt=u_cnt, wp_mm=wp_mm):
                for p in range(FC // 2):
                    if p >= 2:
                        sy.wait_ge(sT.h, u_cnt[2 * p - 3])
                    elif hh == 1:
                        sy.wait_ge(sT.h, T0)
                    else:
                        if era_t0 > 0:
                            sy.wait_ge(sT.h, era_t0)
                        if era_v0 > 0:
                            sy.wait_ge(sV.h, era_v0)
                    sW.inc(sy.dma_start(wfc_ring[:, p % 2, :, :, :], wfcb[p]), 16)
                for si in range(16):
                    dc, fh = si // 2, si % 2
                    if si >= 2:
                        sy.wait_ge(sT.h, wp_mm[si - 2])
                    elif hh == 1:
                        sy.wait_ge(sT.h, T0)
                    elif era_v0 > 0:
                        sy.wait_ge(sV.h, era_v0)
                    sW.inc(sy.dma_start(wp_ring[:, si % 2, :, :], wpb[dc, fh]), 16)

            @blk.tensor
            def _(t, q0=q0, V0=V0, u_cnt=u_cnt, wp_mm=wp_mm, sq_cnt=sq_cnt,
                  scan_cnt=scan_cnt, relu_cnt=relu_cnt, wfc_ld=wfc_ld, wp_ld=wp_ld,
                  hh=hh, prev_scan=prev_scan):
                for f in range(FC):
                    psu = pAB if f % 2 == 0 else pCD
                    t.wait_ge(sW.h, wfc_ld[f // 2])
                    if f < 2:
                        if hh == 0:
                            if era_v0 > 0:
                                t.wait_ge(sV.h, era_v0)
                        else:
                            t.wait_ge(sV.h, prev_scan[4 + f])
                    else:
                        t.wait_ge(sS.h, relu_cnt[f - 2])
                    mm = None
                    for sub in range(2):
                        if f == 0:
                            t.wait_ge(sV.h, era_apply[2 * hh + sub])
                        for k in range(DC):
                            mm = t.matmul(psu[:, sub * 512:sub * 512 + 512],
                                          wfc_ring[:, (f // 2) % 2, f % 2, k, :],
                                          ln_sb[:, k, q0 + sub * 512:q0 + sub * 512 + 512],
                                          start=(k == 0), stop=(k == DC - 1))
                    sT.inc(mm)
                for si in range(16):
                    dc, fh = si // 2, si % 2
                    t.wait_ge(sW.h, wp_ld[si])
                    if si == 0:
                        t.wait_ge(sS.h, sq_cnt[15])
                    elif si == 1:
                        t.wait_ge(sS.h, sq_cnt[FC - 1])
                    if fh == 0 and dc >= 4:
                        t.wait_ge(sV.h, scan_cnt[dc - 4])
                    mm = None
                    for sub in range(2):
                        ysl = yvb(dc, sub)
                        for fi in range(16):
                            f = 16 * fh + fi
                            mm = t.matmul(ysl, wp_ring[:, si % 2, fi, :],
                                          h_sb[:, f, sub * 512:sub * 512 + 512],
                                          start=(f == 0), stop=(f == FC - 1))
                    sT.inc(mm)

            @blk.scalar
            def _(s, u_cnt=u_cnt):
                for f in range(FC):
                    psu = pAB if f % 2 == 0 else pCD
                    s.wait_ge(sT.h, u_cnt[f])
                    sS.inc(s.activation(r_sb[:], psu[:], AF.Relu))
                    sS.inc(s.activation(h_sb[:, f, :], r_sb[:], AF.Square))

            def scan_ops(eng, y, xsl, ss, mem, acc, fused):
                eng.tensor_single_scalar(acc, y, 1.0, ALU.is_gt)
                if fused:
                    eng.scalar_tensor_tensor(mem, y, 1.9, acc, ALU.mult, ALU.subtract)
                else:
                    eng.tensor_scalar_mul(mem, y, 1.9)
                    eng.tensor_tensor(mem, mem, acc, ALU.subtract)
                for _step in range(3):
                    eng.tensor_single_scalar(ss, mem, 1.0, ALU.is_gt)
                    eng.tensor_tensor(acc, acc, ss, ALU.add)
                    if fused:
                        eng.scalar_tensor_tensor(mem, mem, 0.9, y, ALU.mult, ALU.add)
                    else:
                        eng.tensor_scalar_mul(mem, mem, 0.9)
                        eng.tensor_tensor(mem, mem, y, ALU.add)
                    eng.tensor_tensor(mem, mem, ss, ALU.subtract)
                eng.tensor_single_scalar(ss, mem, 1.0, ALU.is_gt)
                eng.tensor_tensor(acc, acc, ss, ALU.add)
                if fused:
                    return eng.scalar_tensor_tensor(xsl, acc, 0.1, xsl, ALU.mult, ALU.add)
                eng.tensor_scalar_mul(ss, acc, 0.1)
                return eng.tensor_tensor(xsl, xsl, ss, ALU.add)

            @blk.vector
            def _(v, q0=q0, wp_mm=wp_mm):
                for dc in range(DC):
                    v.wait_ge(sT.h, wp_mm[2 * dc + 1])
                    last = None
                    for sub in range(2):
                        y = yvb(dc, sub)
                        xsl = x_sb[:, dc, q0 + sub * 512:q0 + sub * 512 + 512]
                        last = scan_ops(v, y, xsl, s_scan[:], mem_scan[:], acc_scan[:], True)
                    sV.inc(last)

            assert sT.n == T0 + FC + 16
            assert sV.n == V0 + DC
            assert sS.n == S0 + 2 * FC
            assert sW.n == W0 + 16 * (FC // 2 + 16)
            prev_scan = scan_cnt
            half_scan.append(scan_cnt[DC - 1])

        fes.close()
        return half_scan

    emit_ffn_era("1")

    aes = ExitStack()
    q_sb = sb(aes, "q_sb", [128, DC, T], BF16)
    wq_sb = sb(aes, "wq_sb", [128, DC, D], BF16)
    bes = ExitStack()
    kstage = sb(bes, "kstage", [128, 4, 512], BF16)
    v_sb = sb(bes, "v_sb", [128, NTC, D], BF16)

    ln2_base = len(ln_apply)
    for qq in range(NQS):
        emit_ln(qq)
    ln2_apply = ln_apply[ln2_base:ln2_base + NQS]

    def load_w(dram):
        @blk.gpsimd
        def _(g):
            sT.wait(g)
            sW.inc(g.dma_start(wq_sb[:], dram), 16)

    def emit_proj_q(dma_gate):
        T0, S0 = sT.n, sS.n
        evac = {i: S0 + i + 1 for i in range(DC * NQS)}
        mmc = {i: T0 + i + 1 for i in range(DC * NQS)}

        @blk.tensor
        def _(t):
            t.wait_ge(sW.h, dma_gate)
            for i in range(DC * NQS):
                m, qn = divmod(i, NQS)
                psu = pAB if i % 2 == 0 else pCD
                if i == 0:
                    sS.wait(t)
                if i >= 2:
                    t.wait_ge(sS.h, evac[i - 2])
                for k in range(DC):
                    t.matmul(psu[:, 0:512], wq_sb[:, k, 128 * m:128 * m + 128],
                             ln_sb[:, k, qn * 512:qn * 512 + 512], start=(k == 0), stop=False)
                sT.inc(t.matmul(psu[:, 0:512], b_sb[0:1, 0, 128 * m:128 * m + 128],
                                ones_row[0:1, :], start=False, stop=True))

        @blk.scalar
        def _(s):
            for i in range(DC * NQS):
                m, qn = divmod(i, NQS)
                psu = pAB if i % 2 == 0 else pCD
                s.wait_ge(sT.h, mmc[i])
                sS.inc(s.copy(q_sb[:, m, qn * 512:qn * 512 + 512], psu[:, 0:512]))

    t_e1 = sT.n
    wk_g = load_w2(wkT, t_e1)
    bk_g = load_bias2(1, 0, t_e1)
    wv_g = load_w2(wvT, t_e1, dst=q_sb[:, :, 0:D])
    bv_g = load_bias2(2, 1, t_e1)

    def emit_proj_k(dma_gate):
        T0, S0, V0, SP0 = sT.n, sS.n, sV.n, sSP.n
        kred = {qn: V0 + qn + 1 for qn in range(NQS)}
        kevac = {i: S0 + 2 * i + 1 for i in range(DC * NQS)}
        ksq = {i: S0 + 2 * i + 2 for i in range(DC * NQS)}
        proj_mm = {i: T0 + 2 * i + 1 for i in range(DC * NQS)}
        knorm_mm = {i: T0 + 2 * i + 2 for i in range(DC * NQS)}
        kspill = {i: SP0 + 16 * (i + 1) for i in range(DC * NQS)}

        @blk.tensor
        def _(t):
            t.wait_ge(sW.h, dma_gate)
            for qn in range(NQS):
                for m in range(DC):
                    i = qn * DC + m
                    psu = pAB if i % 2 == 0 else pCD
                    if m == 0 and qn < NQS:
                        t.wait_ge(sV.h, ln2_apply[qn])
                    if i >= 2:
                        t.wait_ge(sS.h, kevac[i - 2])
                    for k in range(DC):
                        t.matmul(psu[:, 0:512], wq_sb[:, k, 128 * m:128 * m + 128],
                                 ln_sb[:, k, qn * 512:qn * 512 + 512], start=(k == 0), stop=False)
                    sT.inc(t.matmul(psu[:, 0:512], b_sb[0:1, 0, 128 * m:128 * m + 128],
                                    ones_row[0:1, :], start=False, stop=True))
                    t.wait_ge(sS.h, ksq[i])
                    if i == 0:
                        t.wait_ge(sV.h, ln2_apply[NQS - 1])
                    elif m == 0:
                        t.wait_ge(sV.h, kred[qn - 1])
                    sT.inc(t.matmul(pR1[0:16, :], masks[:, m, :], xsq_sb[:, i % 2, :],
                                    start=(m == 0), stop=(m == DC - 1)))

        @blk.scalar
        def _(s):
            for i in range(DC * NQS):
                psu = pAB if i % 2 == 0 else pCD
                s.wait_ge(sT.h, proj_mm[i])
                if i >= 4:
                    s.wait_ge(sSP.h, kspill[i - 4])
                if i >= 2:
                    s.wait_ge(sT.h, knorm_mm[i - 2])
                sS.inc(s.copy(kstage[:, i % 4, :], psu[:, 0:512]))
                sS.inc(s.activation(xsq_sb[:, i % 2, :], kstage[:, i % 4, :], AF.Square))

        @blk.sync
        def _(sy):
            for qn in range(NQS):
                for m in range(DC):
                    i = qn * DC + m
                    sy.wait_ge(sS.h, kevac[i])
                    sSP.inc(sy.dma_start(cc_inK[128 * m:128 * m + 128, qn * 512:qn * 512 + 512],
                                         kstage[:, i % 4, :]), 16)

        @blk.vector
        def _(v):
            for qn in range(NQS):
                v.wait_ge(sT.h, knorm_mm[qn * DC + DC - 1])
                if qn == 0:
                    sV.inc(v.tensor_reduce(nmx2[:], pR1[0:16, :], AX.X, ALU.max))
                else:
                    v.tensor_reduce(nmx3[:], pR1[0:16, :], AX.X, ALU.max)
                    sV.inc(v.tensor_tensor(nmx2[:], nmx2[:], nmx3[:], ALU.max))

    emit_proj_k(max(wk_g, bk_g))
    k_end = sT.n
    wq_g = load_w2(wqT, k_end)
    bq_g = load_bias2(0, 0, k_end)

    @blk.gpsimd
    def _(g):
        sSP.wait(g)
        sCC.inc(g.collective_compute(
            "AllGather", ALU.bypass, ins=[cc_inK], outs=[cc_outK],
            replica_groups=[[0, 1], [2, 3], [4, 5], [6, 7]]))

    def emit_proj_v(dma_gate):
        T0, S0 = sT.n, sS.n
        vevac = {tc: S0 + tc + 1 for tc in range(NTC)}
        vmm = {tc: T0 + tc + 1 for tc in range(NTC)}

        @blk.tensor
        def _(t):
            t.wait_ge(sW.h, dma_gate)
            for tc in range(NTC):
                psu = pAB if tc % 2 == 0 else pCD
                if tc < 2:
                    sV.wait(t)
                else:
                    t.wait_ge(sS.h, vevac[tc - 2])
                mm = None
                for half in range(2):
                    for k in range(DC):
                        t.matmul(psu[:, half * 512:half * 512 + 512],
                                 ln_sb[:, k, tc * 128:tc * 128 + 128],
                                 q_sb[:, k, half * 512:half * 512 + 512],
                                 start=(k == 0), stop=False)
                    mm = t.matmul(psu[:, half * 512:half * 512 + 512],
                                  ones_row[0:1, 0:128],
                                  b_sb[0:1, 1, half * 512:half * 512 + 512],
                                  start=False, stop=True)
                sT.inc(mm)

        @blk.scalar
        def _(s):
            for tc in range(NTC):
                psu = pAB if tc % 2 == 0 else pCD
                s.wait_ge(sT.h, vmm[tc])
                s.copy(v_sb[:, tc, 0:512], psu[:, 0:512])
                sS.inc(s.copy(v_sb[:, tc, 512:1024], psu[:, 512:1024]))

    emit_proj_v(max(wv_g, bv_g))

    @blk.sync
    def _(sy):
        sS.wait(sy)
        vdst = cc_inV.rearrange("r (two d) -> (r two) d", two=2) \
                     .rearrange("(tc p) d -> p tc d", p=128)
        sSP.inc(sy.dma_start(vdst, v_sb[:]), 16)

    @blk.gpsimd
    def _(g):
        sSP.wait(g)
        sCC.inc(g.collective_compute(
            "AllGather", ALU.bypass, ins=[cc_inV], outs=[cc_outV],
            replica_groups=[[0, 1], [2, 3], [4, 5], [6, 7]]))

    emit_proj_q(max(wq_g, bq_g))

    def emit_norms_q():
        T0, S0 = sT.n, sS.n
        V0n = sV.n
        qred = {qs: V0n + qs + 1 for qs in range(NQS)}
        sqq = {}
        nm_mm = {}
        i = 0
        for qs in range(NQS):
            for c in range(DC):
                sqq[(qs, c)] = S0 + i + 1
                nm_mm[(qs, c)] = T0 + i + 1
                i += 1

        @blk.scalar
        def _(s):
            sT.wait(s)
            first = True
            for qs in range(NQS):
                for c in range(DC):
                    if not first and not (qs == 0 and c == 1):
                        prev = (qs, c - 2) if c >= 2 else (qs - 1, DC - 2 + c)
                        s.wait_ge(sT.h, nm_mm[prev])
                    first = False
                    sS.inc(s.activation(xsq_sb[:, c % 2, :],
                                        q_sb[:, c, qs * 512:qs * 512 + 512], AF.Square))

        @blk.tensor
        def _(t):
            for qs in range(NQS):
                for c in range(DC):
                    t.wait_ge(sS.h, sqq[(qs, c)])
                    if qs == 0 and c == 0:
                        sV.wait(t)
                    elif c == 0:
                        t.wait_ge(sV.h, qred[qs - 1])
                    sT.inc(t.matmul(pR1[0:16, :], masks[:, c, :], xsq_sb[:, c % 2, :],
                                    start=(c == 0), stop=(c == DC - 1)))

        @blk.vector
        def _(v):
            for qs in range(NQS):
                v.wait_ge(sT.h, nm_mm[(qs, DC - 1)])
                if qs == 0:
                    sV.inc(v.tensor_reduce(nmx[:], pR1[0:16, :], AX.X, ALU.max))
                else:
                    v.tensor_reduce(nmx3[:], pR1[0:16, :], AX.X, ALU.max)
                    sV.inc(v.tensor_tensor(nmx[:], nmx[:], nmx3[:], ALU.max))

    emit_norms_q()

    @blk.vector
    def _(v):
        sV.wait(v)
        sV.inc(v.tensor_tensor(nmx[:], nmx[:], nmx2[:], ALU.mult))

    @blk.scalar
    def _(s):
        sV.wait(s)
        sS.inc(s.activation(nmx3[:], nmx[:], AF.Sqrt))

    @blk.vector
    def _(v):
        sS.wait(v)
        v.memset(t32a[:], 0.0)
        sV.inc(v.tensor_copy(t32a[0:16, 0:1], nmx3[:]))
        sV.wait(v)
        sV.inc(v.transpose(t32b[:], t32a[:]))
        sV.wait(v)
        sV.inc(v.tensor_copy(brow_f[:], t32b[0:1, 0:16]))

    @blk.scalar
    def _(s):
        sV.wait(s)
        sS.inc(s.copy(brow_b[:], t32b[0:1, 0:16]))

    @blk.tensor
    def _(t):
        sS.wait(t)
        t.matmul(pR2[:, 0:16], s16_row[:], brow_b[:], start=True, stop=False)
        sT.inc(t.matmul(pR2[:, 0:16], m44_row[:], ones_row[0:1, 0:16], start=False, stop=True))

    @blk.vector
    def _(v):
        sT.wait(v)
        sV.inc(v.tensor_reduce(bneg_sb[:], pR2[:, 0:16], AX.X, ALU.max, negate=True))

    bneg_ready = sV.n
    bes.close()

    load_w(woT)
    load_bias(3, 1)

    des = ExitStack()
    kp_sb = sb(des, "kp_sb", [128, 2, 2, T], BF16)
    vp_sb = sb(des, "vp_sb", [128, 2, 2, NTC, 128], BF16)
    expS = sb(des, "expS", [128, 2, 1024], BF16)
    rec_bf = sb(des, "rec_bf", [33, 512], BF16)
    binv_sb = sb(des, "binv_sb", [128, 512], BF16)
    avstage = sb(des, "avstage", [128, 2, 512], BF16)

    kv_cnt = {}
    pair_t_done = {}
    last_norm_v = [bneg_ready]
    av_spill = {}
    prev_pair_av = {}

    def emit_kv_load(p, prev_t):
        buf = p % 2

        @blk.sync
        def _(sy):
            sy.wait_ge(sCC.h, 1)
            if prev_t:
                sy.wait_ge(sT.h, prev_t)
            for s_ in range(2):
                sKV.inc(sy.dma_start(kp_sb[:, buf, s_, :], cc_outK[s_, 128 * p:128 * p + 128, :]), 16)
            sy.wait_ge(sCC.h, 2)
            for s_ in range(2):
                vsrc = cc_outV[s_].rearrange("r (two d) -> (r two) d", two=2)
                vsrc = vsrc[:, 128 * p:128 * p + 128].rearrange("(tc p2) d -> p2 tc d", p2=128)
                sKV.inc(sy.dma_start(vp_sb[:, buf, s_, :, :], vsrc), 16)
            kv_cnt[p] = sKV.n

    emit_kv_load(0, 0)
    emit_kv_load(1, 0)

    units_norm = {}
    prev_exp = {}

    for p in range(NP):
        buf = p % 2
        T0, S0, V0, SP0 = sT.n, sS.n, sV.n, sSP.n
        qk_cnt = {}
        av_cnt = {}
        binv_mm = {}
        n = T0
        for qs in range(NQS):
            for kc in range(32):
                n += 1
                qk_cnt[(qs, kc)] = n
                if kc >= 1:
                    n += 1
                    av_cnt[(qs, kc - 1)] = n
                if kc == 8 and qs > 0:
                    n += 1
                    binv_mm[qs - 1] = n
            n += 1
            av_cnt[(qs, 31)] = n
        n += 1
        binv_mm[NQS - 1] = n
        exp_cnt = {(qs, kc): S0 + qs * 32 + kc + 1 for qs in range(NQS) for kc in range(32)}
        rec_v = {qs: V0 + qs * 2 + 1 for qs in range(NQS)}
        norm_v = {qs: V0 + qs * 2 + 2 for qs in range(NQS)}
        spill_n = {qs: SP0 + 16 * (qs + 1) for qs in range(NQS)}
        for qs in range(NQS):
            units_norm[p * NQS + qs] = norm_v[qs]

        @blk.tensor
        def _(t, p=p, buf=buf, qk_cnt=qk_cnt, av_cnt=av_cnt, binv_mm=binv_mm,
              exp_cnt=exp_cnt, rec_v=rec_v, norm_v=norm_v, prevx=prev_exp):
            def emit_av(qs, kc):
                u = p * NQS + qs
                avp = (pR2, pR3)[u % 2]
                smp = (pR1, pR4)[u % 2]
                t.wait_ge(sS.h, exp_cnt[(qs, kc)])
                if kc == 0 and u >= 2:
                    t.wait_ge(sV.h, units_norm[u - 2])
                slot, j = kc // 16, kc % 16
                t.matmul(avp[0:64, :], vp_sb[:, buf, slot, j, 0:64],
                         expS[:, kc % 2, 0:512], start=(kc == 0), stop=(kc == 31))
                t.matmul(avp[64:128, :], vp_sb[:, buf, slot, j, 64:128],
                         expS[:, kc % 2, 512:1024], start=(kc == 0), stop=(kc == 31),
                         tile_position=(0, 64), skip_group_check=True)
                t.matmul(smp[0:1, :], ones_col[:], expS[:, kc % 2, 0:512],
                         start=(kc == 0), stop=(kc == 31))
                sT.inc(t.matmul(smp[32:33, :], ones_col[:], expS[:, kc % 2, 512:1024],
                                start=(kc == 0), stop=(kc == 31), tile_position=(0, 32),
                                skip_group_check=True))

            def emit_binv(qs):
                u = p * NQS + qs
                smp = (pR1, pR4)[u % 2]
                t.wait_ge(sV.h, rec_v[qs])
                t.matmul(smp[0:64, 0:512], ones33[0:1, :], rec_bf[0:1, :], start=True, stop=False)
                sT.inc(t.matmul(smp[64:128, 0:512], ones33[32:33, :], rec_bf[32:33, :],
                                start=True, stop=True, tile_position=(32, 64),
                                skip_group_check=True))

            for qs in range(NQS):
                q0 = qs * 512
                for kc in range(32):
                    sc = pAB if kc % 2 == 0 else pCD
                    slot, j = kc // 16, kc % 16
                    if kc == 0 and qs == 0:
                        t.wait_ge(sKV.h, kv_cnt[p])
                        if p == 0:
                            t.wait_ge(sV.h, last_norm_v[0])
                    if kc >= 2:
                        t.wait_ge(sS.h, exp_cnt[(qs, kc - 2)])
                    elif qs > 0:
                        t.wait_ge(sS.h, exp_cnt[(qs - 1, 30 + kc)])
                    elif p > 0:
                        s_prev = prevx.get(30 + kc)
                        if s_prev:
                            t.wait_ge(sS.h, s_prev)
                    t.matmul(sc[:, 0:512], kp_sb[0:64, buf, slot, 128 * j:128 * j + 128],
                             q_sb[0:64, p, q0:q0 + 512], start=True, stop=True)
                    sT.inc(t.matmul(sc[:, 512:1024], kp_sb[64:128, buf, slot, 128 * j:128 * j + 128],
                                    q_sb[64:128, p, q0:q0 + 512], start=True, stop=True,
                                    tile_position=(64, 0)))
                    if kc >= 1:
                        emit_av(qs, kc - 1)
                    if kc == 8 and qs > 0:
                        emit_binv(qs - 1)
                emit_av(qs, 31)
            emit_binv(NQS - 1)

        @blk.scalar
        def _(s, p=p, qk_cnt=qk_cnt, av_cnt=av_cnt, prev=prev_pair_av):
            for qs in range(NQS):
                for kc in range(32):
                    sc = pAB if kc % 2 == 0 else pCD
                    s.wait_ge(sT.h, qk_cnt[(qs, kc)])
                    if kc >= 2:
                        s.wait_ge(sT.h, av_cnt[(qs, kc - 2)])
                    elif qs > 0:
                        s.wait_ge(sT.h, av_cnt[(qs - 1, 30 + kc)])
                    elif p > 0:
                        s.wait_ge(sT.h, prev[30 + kc])
                    sS.inc(s.activation(expS[:, kc % 2, :], sc[:], AF.Exp,
                                        bias=bneg_sb[:], scale=0.125))

        @blk.vector
        def _(v, p=p, av_cnt=av_cnt, binv_mm=binv_mm):
            for qs in range(NQS):
                u = p * NQS + qs
                avp = (pR2, pR3)[u % 2]
                smp = (pR1, pR4)[u % 2]
                v.wait_ge(sT.h, av_cnt[(qs, 31)])
                with nc.allow_low_precision(reason="softmax normalizer feeds bf16 matmul"):
                    sV.inc(v.reciprocal(rec_bf[:], smp[0:33, :]))
                v.wait_ge(sT.h, binv_mm[qs])
                if u >= 2:
                    v.wait_ge(sSP.h, av_spill[u - 2] if qs < 2 else spill_n[qs - 2])
                v.tensor_copy(binv_sb[:], smp[:, 0:512])
                sV.inc(v.tensor_tensor(avstage[:, qs % 2, :], avp[:], binv_sb[:], ALU.mult))

        @blk.sync
        def _(sy, p=p, norm_v=norm_v):
            for qs in range(NQS):
                sy.wait_ge(sV.h, norm_v[qs])
                av_spill[p * NQS + qs] = sSP.inc(sy.dma_start(
                    attnv_dram[128 * p:128 * p + 128, qs * 512:qs * 512 + 512],
                    avstage[:, qs % 2, :]), 16)

        prev_pair_av = {30: av_cnt[(NQS - 1, 30)], 31: av_cnt[(NQS - 1, 31)]}
        prev_exp = {30: exp_cnt[(NQS - 1, 30)], 31: exp_cnt[(NQS - 1, 31)]}
        last_norm_v[0] = norm_v[NQS - 1]
        assert sT.n == T0 + NQS * 65
        assert sS.n == S0 + NQS * 32
        assert sV.n == V0 + NQS * 2
        pair_t_done[p] = sT.n
        if p + 2 < NP:
            emit_kv_load(p + 2, pair_t_done[p])

    des.close()
    wes = ExitStack()
    av_ring = sb(wes, "av_ring", [128, 2, DC, 512], BF16)

    def emit_wo():
        T0, V0, KV0 = sT.n, sV.n, sKV.n
        wo_mm = {i: T0 + i + 1 for i in range(NQS * DC)}
        wo_ev = {i: V0 + i + 1 for i in range(NQS * DC)}
        av_ld = {qn: KV0 + 16 * (qn + 1) for qn in range(NQS)}

        @blk.sync
        def _(sy):
            for qn in range(NQS):
                if qn == 0:
                    sy.wait_ge(sSP.h, av_spill[NP * NQS - 1])
                if qn >= 2:
                    sy.wait_ge(sT.h, wo_mm[(qn - 2) * DC + DC - 1])
                src_ap = attnv_dram[:, qn * 512:qn * 512 + 512].rearrange("(k p) t -> p k t", p=128)
                sKV.inc(sy.dma_start(av_ring[:, qn % 2, :, :], src_ap), 16)

        @blk.tensor
        def _(t):
            sW.wait(t)
            for qn in range(NQS):
                t.wait_ge(sKV.h, av_ld[qn])
                for m in range(DC):
                    i = qn * DC + m
                    psu = pAB if i % 2 == 0 else pCD
                    if i < 2:
                        t.wait_ge(sV.h, last_norm_v[0])
                    else:
                        t.wait_ge(sV.h, wo_ev[i - 2])
                    for k in range(DC):
                        t.matmul(psu[:, 0:512], wq_sb[:, k, 128 * m:128 * m + 128],
                                 av_ring[:, qn % 2, k, :], start=(k == 0), stop=False)
                    sT.inc(t.matmul(psu[:, 0:512], b_sb[0:1, 1, 128 * m:128 * m + 128],
                                    ones_row[0:1, :], start=False, stop=True))

        @blk.vector
        def _(v):
            for qn in range(NQS):
                for m in range(DC):
                    i = qn * DC + m
                    psu = pAB if i % 2 == 0 else pCD
                    v.wait_ge(sT.h, wo_mm[i])
                    xsl = x_sb[:, m, qn * 512:qn * 512 + 512]
                    sV.inc(v.tensor_tensor(xsl, xsl, psu[:, 0:512], ALU.add))

    emit_wo()
    wes.close()
    aes.close()

    e2_half = emit_ffn_era("2")

    @blk.sync
    def _(sy):
        dst = out_t.rearrange("(c p) t -> p c t", p=128)
        for hh in range(2):
            sy.wait_ge(sV.h, e2_half[hh])
            sOUT.inc(sy.dma_start(dst[:, :, hh * 1024:hh * 1024 + 1024],
                                  x_sb[:, :, hh * 1024:hh * 1024 + 1024]), 16)
        sOUT.wait(sy)

    es.close()
    return nc


_NC_CACHE = None


def kernel(L0_attn_x, L0_attn_Wq, L0_attn_bq, L0_attn_Wk, L0_attn_bk,
           L0_attn_Wv, L0_attn_bv, L0_attn_Wo, L0_attn_bo,
           L0_ffn_W_fc, L0_ffn_W_proj):
    global _NC_CACHE
    x = np.asarray(L0_attn_x, dtype=np.float32)
    B, S, _ = x.shape
    assert (B, S) == (4, 4096)

    wqT = np.ascontiguousarray(np.asarray(L0_attn_Wq, np.float32).T.reshape(DC, 128, D).transpose(1, 0, 2))
    wkT = np.ascontiguousarray(np.asarray(L0_attn_Wk, np.float32).T.reshape(DC, 128, D).transpose(1, 0, 2))
    wvT = np.ascontiguousarray(np.asarray(L0_attn_Wv, np.float32).T.reshape(DC, 128, D).transpose(1, 0, 2))
    woT = np.ascontiguousarray(np.asarray(L0_attn_Wo, np.float32).T.reshape(DC, 128, D).transpose(1, 0, 2))
    wfcT = np.asarray(L0_ffn_W_fc, np.float32).T
    wfcb = np.ascontiguousarray(wfcT.reshape(DC, 128, FC, 128).transpose(2, 1, 0, 3).reshape(FC // 2, 2, 128, DC, 128).transpose(0, 2, 1, 3, 4))
    wprojT = np.asarray(L0_ffn_W_proj, np.float32).T
    wpb = np.ascontiguousarray(wprojT.reshape(2, 16, 128, DC, 128).transpose(3, 0, 2, 1, 4))
    brow = np.ascontiguousarray(np.concatenate(
        [np.asarray(b, np.float32).reshape(-1)
         for b in (L0_attn_bq, L0_attn_bk, L0_attn_bv, L0_attn_bo)])[None, :])
    cstv = np.full((128, 1), EPS, np.float32)

    if _NC_CACHE is None:
        _NC_CACHE = build_program()
    nc = _NC_CACHE

    in_maps = []
    for c in range(8):
        b, h = c // 2, c % 2
        xs = np.ascontiguousarray(x[b, h * T:(h + 1) * T, :].T)
        in_maps.append(dict(x_t=xs, wqT=wqT, wkT=wkT, wvT=wvT, woT=woT,
                            wfcb=wfcb, wpb=wpb, brow=brow, cst=cstv))

    res = run_bass_kernel_spmd(nc, in_maps, core_ids=list(range(8)))
    out = np.empty((B, S, D), np.float32)
    for c in range(8):
        b, h = c // 2, c % 2
        out[b, h * T:(h + 1) * T, :] = res.results[c]["out_t"].T
    return out


# version 22 (final)
# speedup vs baseline: 1.1651x; 1.0048x over previous
import numpy as np
from contextlib import ExitStack

import concourse.bass as bass
import concourse.mybir as mybir
from concourse.bass_utils import run_bass_kernel_spmd

F32 = mybir.dt.float32
BF16 = mybir.dt.bfloat16
AF = mybir.ActivationFunctionType
ALU = mybir.AluOpType
AX = mybir.AxisListType

D = 1024
DC = 8
T = 2048
FF = 4096
FC = 32
NP = 8
QS = 512
NQS = T // QS
NTC = T // 128
EPS = 1e-5


class Sem:
    def __init__(self, es, nc, name):
        self.h = es.enter_context(nc.semaphore(name))
        self.n = 0

    def inc(self, instr, k=1):
        instr.then_inc(self.h, k)
        self.n += k
        return self.n

    def wait(self, eng, n=None):
        eng.wait_ge(self.h, self.n if n is None else n)


def build_program(debug=False):
    nc = bass.Bass("TRN2", target_bir_lowering=False)

    x_t = nc.dram_tensor("x_t", [D, T], F32, kind="ExternalInput").ap()
    wqT = nc.dram_tensor("wqT", [128, DC, D], F32, kind="ExternalInput").ap()
    wkT = nc.dram_tensor("wkT", [128, DC, D], F32, kind="ExternalInput").ap()
    wvT = nc.dram_tensor("wvT", [128, DC, D], F32, kind="ExternalInput").ap()
    woT = nc.dram_tensor("woT", [128, DC, D], F32, kind="ExternalInput").ap()
    wfcb = nc.dram_tensor("wfcb", [FC // 2, 128, 2, DC, 128], F32, kind="ExternalInput").ap()
    wpb = nc.dram_tensor("wpb", [DC, 2, 128, 16, 128], F32, kind="ExternalInput").ap()
    brow = nc.dram_tensor("brow", [1, 4 * D], F32, kind="ExternalInput").ap()
    cst = nc.dram_tensor("cst", [128, 1], F32, kind="ExternalInput").ap()
    out_t = nc.dram_tensor("out_t", [D, T], F32, kind="ExternalOutput").ap()
    cc_inK = nc.dram_tensor("cc_inK", [1024, 2048], BF16, kind="Internal").ap()
    cc_inV = nc.dram_tensor("cc_inV", [1024, 2048], BF16, kind="Internal").ap()
    cc_outK = nc.dram_tensor("cc_outK", [2, 1024, 2048], BF16, kind="Internal").ap()
    cc_outV = nc.dram_tensor("cc_outV", [2, 1024, 2048], BF16, kind="Internal").ap()
    attnv_dram = nc.dram_tensor("attnv_dram", [D, T], BF16, kind="Internal").ap()

    es = ExitStack()

    def sb(stack, name, shape, dt):
        return stack.enter_context(nc.sbuf_tensor(name, shape, dt))

    x_sb = sb(es, "x_sb", [128, DC, T], F32)
    ln_sb = sb(es, "ln_sb", [128, DC, T], BF16)
    ones_col = sb(es, "ones_col", [128, 1], BF16)
    ones_row = sb(es, "ones_row", [1, 512], BF16)
    masks = sb(es, "masks", [128, DC, 16], BF16)
    b_sb = sb(es, "b_sb", [1, 2, D], BF16)
    eps_sb = sb(es, "eps_sb", [128, 1], F32)
    bneg_sb = sb(es, "bneg_sb", [128, 1], F32)
    rows_sb = sb(es, "rows_sb", [1, 2, 512], F32)
    ab_bf = sb(es, "ab_bf", [1, 1024], BF16)
    ones33 = sb(es, "ones33", [33, 64], BF16)
    s16_row = sb(es, "s16_row", [1, 128], BF16)
    m44_row = sb(es, "m44_row", [1, 128], BF16)
    nmx = sb(es, "nmx", [16, 1], F32)
    nmx2 = sb(es, "nmx2", [16, 1], F32)
    nmx3 = sb(es, "nmx3", [16, 1], F32)
    xb_sb = sb(es, "xb_sb", [128, 2, 512], BF16)
    xsq_sb = sb(es, "xsq_sb", [128, 2, 512], BF16)
    tmp_sb = sb(es, "tmp_sb", [128, 512], F32)
    brow_f = sb(es, "brow_f", [1, 16], F32)
    t32a = sb(es, "t32a", [32, 32], F32)
    t32b = sb(es, "t32b", [32, 32], F32)
    brow_b = sb(es, "brow_b", [1, 16], BF16)

    pAB = es.enter_context(nc.psum_tensor("pAB", [128, 1024], F32))
    pCD = es.enter_context(nc.psum_tensor("pCD", [128, 1024], F32))
    pR1 = es.enter_context(nc.psum_tensor("pR1", [128, 512], F32))
    pR2 = es.enter_context(nc.psum_tensor("pR2", [128, 512], F32))
    pR3 = es.enter_context(nc.psum_tensor("pR3", [128, 512], F32))
    pR4 = es.enter_context(nc.psum_tensor("pR4", [128, 512], F32))

    sIN = Sem(es, nc, "sIN")
    sW = Sem(es, nc, "sW")
    sT = Sem(es, nc, "sT")
    sV = Sem(es, nc, "sV")
    sS = Sem(es, nc, "sS")
    sSP = Sem(es, nc, "sSP")
    sCC = Sem(es, nc, "sCC")
    sKV = Sem(es, nc, "sKV")
    sOUT = Sem(es, nc, "sOUT")

    blk = es.enter_context(nc.Block())

    @blk.sync
    def _(sy):
        sIN.inc(sy.dma_start(eps_sb[:], cst), 16)
        for qq in range(NQS):
            sIN.inc(sy.dma_start(x_sb[:, :, qq * 512:qq * 512 + 512],
                                 x_t[:, qq * 512:qq * 512 + 512]
                                 .rearrange("(c p) t -> p c t", p=128)), 16)

    @blk.vector
    def _(v):
        v.memset(ones_col[:], 1.0)
        v.memset(ones_row[:], 1.0)
        v.memset(ones33[0:1, :], 1.0)
        v.memset(ones33[32:33, :], 1.0)
        v.memset(s16_row[:], 0.125)
        v.memset(m44_row[:], -44.0)
        v.memset(masks[:], 0.0)
        for c in range(DC):
            v.memset(masks[0:64, c, 2 * c:2 * c + 1], 1.0)
        last = None
        for c in range(DC):
            last = v.memset(masks[64:128, c, 2 * c + 1:2 * c + 2], 1.0)
        sV.inc(last)

    def load_bias(r, slot):
        @blk.gpsimd
        def _(g):
            sT.wait(g)
            sW.inc(g.dma_start(b_sb[0:1, slot, :], brow[0:1, r * D:(r + 1) * D]), 16)

    def load_bias2(r, slot, gate):
        @blk.gpsimd
        def _(g):
            g.wait_ge(sT.h, gate)
            sW.inc(g.dma_start(b_sb[0:1, slot, :], brow[0:1, r * D:(r + 1) * D]), 16)
        return sW.n

    def load_w2(dram, gate, dst=None):
        @blk.gpsimd
        def _(g):
            g.wait_ge(sT.h, gate)
            sW.inc(g.dma_start(dst if dst is not None else wq_sb[:], dram), 16)
        return sW.n

    rowA = rows_sb[0:1, 0, :]
    rowB = rows_sb[0:1, 1, :]
    rowC = tmp_sb[0:1, :]

    ln_apply = []
    ln_prev_mm = [None]

    def emit_ln(qq, xgate=None, xver=None):
        q0 = qq * 512
        T0, V0, S0 = sT.n, sV.n, sS.n
        ln_apply.append(V0 + 3)
        ln_mm = {c: T0 + c + 1 for c in range(DC)}
        sq_done = {c: S0 + c + 1 for c in range(DC)}
        prev_mm = ln_prev_mm[0]
        ln_prev_mm[0] = ln_mm

        @blk.scalar
        def _(s):
            if xgate is not None:
                s.wait_ge(sIN.h, xgate)
            if xver is not None:
                s.wait_ge(sV.h, xver)
            for c in range(DC):
                if c >= 2:
                    s.wait_ge(sT.h, ln_mm[c - 2])
                elif prev_mm is not None:
                    s.wait_ge(sT.h, prev_mm[DC - 2 + c])
                s.copy(xb_sb[:, c % 2, :], x_sb[:, c, q0:q0 + 512])
                sS.inc(s.activation(xsq_sb[:, c % 2, :], xb_sb[:, c % 2, :], AF.Square))
            s.wait_ge(sV.h, V0 + 1)
            if xgate is not None and qq == 0:
                s.wait_ge(sIN.h, 16)
            sS.inc(s.activation(rowC, rowB, AF.Sqrt, bias=eps_sb[0:1, :]))

        @blk.tensor
        def _(t):
            for c in range(DC):
                t.wait_ge(sS.h, sq_done[c])
                if c == 0:
                    t.wait_ge(sV.h, V0)
                t.matmul(pR1[0:1, :], ones_col[:], xb_sb[:, c % 2, :], start=(c == 0), stop=(c == DC - 1))
                sT.inc(t.matmul(pR2[0:1, :], ones_col[:], xsq_sb[:, c % 2, :],
                                start=(c == 0), stop=(c == DC - 1)))
            t.wait_ge(sV.h, V0 + 2)
            t.matmul(pR3[:, :], ones_row[0:1, 0:128], ab_bf[0:1, 0:512], start=True, stop=True)
            sT.inc(t.matmul(pR4[:, :], ones_row[0:1, 0:128], ab_bf[0:1, 512:1024],
                            start=True, stop=True))

        @blk.vector
        def _(v):
            if sOUT.n > 0:
                sOUT.wait(v)
            v.wait_ge(sT.h, T0 + DC)
            v.tensor_scalar_mul(rowA, pR1[0:1, :], 1.0 / D)
            v.tensor_tensor(rowB, rowA, rowA, ALU.mult)
            sV.inc(v.scalar_tensor_tensor(rowB, pR2[0:1, :], 1.0 / D, rowB, ALU.mult, ALU.subtract))
            v.wait_ge(sS.h, S0 + DC + 1)
            v.reciprocal(rowB, rowC)
            v.scalar_tensor_tensor(rowC, rowA, -1.0, rowB, ALU.mult, ALU.mult)
            v.tensor_copy(ab_bf[0:1, 0:512], rowB)
            sV.inc(v.tensor_copy(ab_bf[0:1, 512:1024], rowC))
            v.wait_ge(sT.h, T0 + DC + 1)
            last = None
            for c in range(DC):
                v.tensor_tensor(tmp_sb[:], x_sb[:, c, q0:q0 + 512], pR3[:, :], ALU.mult)
                last = v.tensor_tensor(ln_sb[:, c, q0:q0 + 512], tmp_sb[:], pR4[:, :], ALU.add)
            sV.inc(last)

        assert sT.n == T0 + DC + 1
        assert sV.n == V0 + 3
        assert sS.n == S0 + DC + 1

    def emit_ffn_era(tag, xver=None):
        fes = ExitStack()
        wfc_ring = sb(fes, "wfc_ring" + tag, [128, 2, 2, DC, 128], BF16)
        h_sb = sb(fes, "h_sb" + tag, [128, FC, 1024], BF16)
        wp_ring = sb(fes, "wp_ring" + tag, [128, 2, 16, 128], BF16)
        r_sb = sb(fes, "r_sb" + tag, [128, 1024], F32)
        s_scan = sb(fes, "s_scan" + tag, [128, 512], F32)
        mem_scan = sb(fes, "mem_scan" + tag, [128, 512], F32)
        acc_scan = sb(fes, "acc_scan" + tag, [128, 512], F32)

        era_t0 = sT.n
        era_v0 = sV.n
        half_scan = []

        def yvb(dc, sub):
            r = dc % 4
            if r == 0:
                return pAB[:, sub * 512:sub * 512 + 512]
            if r == 1:
                return pCD[:, sub * 512:sub * 512 + 512]
            if r == 2:
                return (pR1 if sub == 0 else pR2)[:, :]
            return (pR3 if sub == 0 else pR4)[:, :]

        apply_base = len(ln_apply)
        for qq in range(NQS):
            emit_ln(qq, xgate=(32 + 16 * qq) if tag == "1" else None,
                    xver=None if xver is None else xver[qq])
        era_apply = ln_apply[apply_base:apply_base + NQS]
        prev_scan = None

        for hh in range(2):
            q0 = hh * 1024
            T0, V0, S0, W0 = sT.n, sV.n, sS.n, sW.n
            u_cnt = {f: T0 + f + 1 for f in range(FC)}
            wp_mm = {si: T0 + FC + si + 1 for si in range(16)}
            scan_cnt = {dc: V0 + dc + 1 for dc in range(DC)}
            relu_cnt = {f: S0 + 2 * f + 1 for f in range(FC)}
            sq_cnt = {f: S0 + 2 * f + 2 for f in range(FC)}
            wfc_ld = {p: W0 + 16 * (p + 1) for p in range(FC // 2)}
            wp_ld = {si: W0 + 16 * (FC // 2) + 16 * (si + 1) for si in range(16)}

            @blk.gpsimd
            def _(sy, hh=hh, T0=T0, u_cnt=u_cnt, wp_mm=wp_mm):
                for p in range(FC // 2):
                    if p >= 2:
                        sy.wait_ge(sT.h, u_cnt[2 * p - 3])
                    elif hh == 1:
                        sy.wait_ge(sT.h, T0)
                    else:
                        if era_t0 > 0:
                            sy.wait_ge(sT.h, era_t0)
                        if era_v0 > 0:
                            sy.wait_ge(sV.h, era_v0)
                    sW.inc(sy.dma_start(wfc_ring[:, p % 2, :, :, :], wfcb[p]), 16)
                for si in range(16):
                    dc, fh = si // 2, si % 2
                    if si >= 2:
                        sy.wait_ge(sT.h, wp_mm[si - 2])
                    elif hh == 1:
                        sy.wait_ge(sT.h, T0)
                    elif era_v0 > 0:
                        sy.wait_ge(sV.h, era_v0)
                    sW.inc(sy.dma_start(wp_ring[:, si % 2, :, :], wpb[dc, fh]), 16)

            @blk.tensor
            def _(t, q0=q0, V0=V0, u_cnt=u_cnt, wp_mm=wp_mm, sq_cnt=sq_cnt,
                  scan_cnt=scan_cnt, relu_cnt=relu_cnt, wfc_ld=wfc_ld, wp_ld=wp_ld,
                  hh=hh, prev_scan=prev_scan):
                for f in range(FC):
                    psu = pAB if f % 2 == 0 else pCD
                    t.wait_ge(sW.h, wfc_ld[f // 2])
                    if f < 2:
                        if hh == 0:
                            if era_v0 > 0:
                                t.wait_ge(sV.h, era_v0)
                        else:
                            t.wait_ge(sV.h, prev_scan[4 + f])
                    else:
                        t.wait_ge(sS.h, relu_cnt[f - 2])
                    mm = None
                    for sub in range(2):
                        if f == 0:
                            t.wait_ge(sV.h, era_apply[2 * hh + sub])
                        for k in range(DC):
                            mm = t.matmul(psu[:, sub * 512:sub * 512 + 512],
                                          wfc_ring[:, (f // 2) % 2, f % 2, k, :],
                                          ln_sb[:, k, q0 + sub * 512:q0 + sub * 512 + 512],
                                          start=(k == 0), stop=(k == DC - 1))
                    sT.inc(mm)
                for si in range(16):
                    dc, fh = si // 2, si % 2
                    t.wait_ge(sW.h, wp_ld[si])
                    if si == 0:
                        t.wait_ge(sS.h, sq_cnt[15])
                    elif si == 1:
                        t.wait_ge(sS.h, sq_cnt[FC - 1])
                    if fh == 0 and dc >= 4:
                        t.wait_ge(sV.h, scan_cnt[dc - 4])
                    mm = None
                    for sub in range(2):
                        ysl = yvb(dc, sub)
                        for fi in range(16):
                            f = 16 * fh + fi
                            mm = t.matmul(ysl, wp_ring[:, si % 2, fi, :],
                                          h_sb[:, f, sub * 512:sub * 512 + 512],
                                          start=(f == 0), stop=(f == FC - 1))
                    sT.inc(mm)

            @blk.scalar
            def _(s, u_cnt=u_cnt):
                for f in range(FC):
                    psu = pAB if f % 2 == 0 else pCD
                    s.wait_ge(sT.h, u_cnt[f])
                    sS.inc(s.activation(r_sb[:], psu[:], AF.Relu))
                    sS.inc(s.activation(h_sb[:, f, :], r_sb[:], AF.Square))

            def scan_ops(eng, y, xsl, ss, mem, acc, fused):
                eng.tensor_single_scalar(acc, y, 1.0, ALU.is_gt)
                if fused:
                    eng.scalar_tensor_tensor(mem, y, 1.9, acc, ALU.mult, ALU.subtract)
                else:
                    eng.tensor_scalar_mul(mem, y, 1.9)
                    eng.tensor_tensor(mem, mem, acc, ALU.subtract)
                for _step in range(3):
                    eng.tensor_single_scalar(ss, mem, 1.0, ALU.is_gt)
                    eng.tensor_tensor(acc, acc, ss, ALU.add)
                    if fused:
                        eng.scalar_tensor_tensor(mem, mem, 0.9, y, ALU.mult, ALU.add)
                    else:
                        eng.tensor_scalar_mul(mem, mem, 0.9)
                        eng.tensor_tensor(mem, mem, y, ALU.add)
                    eng.tensor_tensor(mem, mem, ss, ALU.subtract)
                eng.tensor_single_scalar(ss, mem, 1.0, ALU.is_gt)
                eng.tensor_tensor(acc, acc, ss, ALU.add)
                if fused:
                    return eng.scalar_tensor_tensor(xsl, acc, 0.1, xsl, ALU.mult, ALU.add)
                eng.tensor_scalar_mul(ss, acc, 0.1)
                return eng.tensor_tensor(xsl, xsl, ss, ALU.add)

            @blk.vector
            def _(v, q0=q0, wp_mm=wp_mm):
                for dc in range(DC):
                    v.wait_ge(sT.h, wp_mm[2 * dc + 1])
                    last = None
                    for sub in range(2):
                        y = yvb(dc, sub)
                        xsl = x_sb[:, dc, q0 + sub * 512:q0 + sub * 512 + 512]
                        last = scan_ops(v, y, xsl, s_scan[:], mem_scan[:], acc_scan[:], True)
                    sV.inc(last)

            assert sT.n == T0 + FC + 16
            assert sV.n == V0 + DC
            assert sS.n == S0 + 2 * FC
            assert sW.n == W0 + 16 * (FC // 2 + 16)
            prev_scan = scan_cnt
            half_scan.append(scan_cnt[DC - 1])

        fes.close()
        return half_scan

    e1_half = emit_ffn_era("1")

    aes = ExitStack()
    q_sb = sb(aes, "q_sb", [128, DC, T], BF16)
    wq_sb = sb(aes, "wq_sb", [128, DC, D], BF16)
    bes = ExitStack()
    kstage = sb(bes, "kstage", [128, 4, 512], BF16)
    v_sb = sb(bes, "v_sb", [128, NTC, D], BF16)

    ln2_base = len(ln_apply)
    for qq in range(NQS):
        emit_ln(qq, xver=e1_half[qq // 2])
    ln2_apply = ln_apply[ln2_base:ln2_base + NQS]

    def load_w(dram):
        @blk.gpsimd
        def _(g):
            sT.wait(g)
            sW.inc(g.dma_start(wq_sb[:], dram), 16)

    def emit_proj_q(dma_gate):
        T0, S0 = sT.n, sS.n
        evac = {i: S0 + i + 1 for i in range(DC * NQS)}
        mmc = {i: T0 + i + 1 for i in range(DC * NQS)}

        @blk.tensor
        def _(t):
            t.wait_ge(sW.h, dma_gate)
            for i in range(DC * NQS):
                m, qn = divmod(i, NQS)
                psu = pAB if i % 2 == 0 else pCD
                if i == 0:
                    sS.wait(t)
                if i >= 2:
                    t.wait_ge(sS.h, evac[i - 2])
                for k in range(DC):
                    t.matmul(psu[:, 0:512], wq_sb[:, k, 128 * m:128 * m + 128],
                             ln_sb[:, k, qn * 512:qn * 512 + 512], start=(k == 0), stop=False)
                sT.inc(t.matmul(psu[:, 0:512], b_sb[0:1, 0, 128 * m:128 * m + 128],
                                ones_row[0:1, :], start=False, stop=True))

        @blk.scalar
        def _(s):
            for i in range(DC * NQS):
                m, qn = divmod(i, NQS)
                psu = pAB if i % 2 == 0 else pCD
                s.wait_ge(sT.h, mmc[i])
                sS.inc(s.copy(q_sb[:, m, qn * 512:qn * 512 + 512], psu[:, 0:512]))

    t_e1 = sT.n
    wk_g = load_w2(wkT, t_e1)
    bk_g = load_bias2(1, 0, t_e1)
    wv_g = load_w2(wvT, t_e1, dst=q_sb[:, :, 0:D])
    bv_g = load_bias2(2, 1, t_e1)

    def emit_proj_k(dma_gate):
        T0, S0, V0, SP0 = sT.n, sS.n, sV.n, sSP.n
        kred = {qn: V0 + qn + 1 for qn in range(NQS)}
        kevac = {i: S0 + 2 * i + 1 for i in range(DC * NQS)}
        ksq = {i: S0 + 2 * i + 2 for i in range(DC * NQS)}
        proj_mm = {i: T0 + 2 * i + 1 for i in range(DC * NQS)}
        knorm_mm = {i: T0 + 2 * i + 2 for i in range(DC * NQS)}
        kspill = {i: SP0 + 16 * (i + 1) for i in range(DC * NQS)}

        @blk.tensor
        def _(t):
            t.wait_ge(sW.h, dma_gate)
            for qn in range(NQS):
                for m in range(DC):
                    i = qn * DC + m
                    psu = pAB if i % 2 == 0 else pCD
                    if m == 0 and qn < NQS:
                        t.wait_ge(sV.h, ln2_apply[qn])
                    if i >= 2:
                        t.wait_ge(sS.h, kevac[i - 2])
                    for k in range(DC):
                        t.matmul(psu[:, 0:512], wq_sb[:, k, 128 * m:128 * m + 128],
                                 ln_sb[:, k, qn * 512:qn * 512 + 512], start=(k == 0), stop=False)
                    sT.inc(t.matmul(psu[:, 0:512], b_sb[0:1, 0, 128 * m:128 * m + 128],
                                    ones_row[0:1, :], start=False, stop=True))
                    t.wait_ge(sS.h, ksq[i])
                    if i == 0:
                        t.wait_ge(sV.h, ln2_apply[NQS - 1])
                    elif m == 0:
                        t.wait_ge(sV.h, kred[qn - 1])
                    sT.inc(t.matmul(pR1[0:16, :], masks[:, m, :], xsq_sb[:, i % 2, :],
                                    start=(m == 0), stop=(m == DC - 1)))

        @blk.scalar
        def _(s):
            for i in range(DC * NQS):
                psu = pAB if i % 2 == 0 else pCD
                s.wait_ge(sT.h, proj_mm[i])
                if i >= 4:
                    s.wait_ge(sSP.h, kspill[i - 4])
                if i >= 2:
                    s.wait_ge(sT.h, knorm_mm[i - 2])
                sS.inc(s.copy(kstage[:, i % 4, :], psu[:, 0:512]))
                sS.inc(s.activation(xsq_sb[:, i % 2, :], kstage[:, i % 4, :], AF.Square))

        @blk.sync
        def _(sy):
            for qn in range(NQS):
                for m in range(DC):
                    i = qn * DC + m
                    sy.wait_ge(sS.h, kevac[i])
                    sSP.inc(sy.dma_start(cc_inK[128 * m:128 * m + 128, qn * 512:qn * 512 + 512],
                                         kstage[:, i % 4, :]), 16)

        @blk.vector
        def _(v):
            for qn in range(NQS):
                v.wait_ge(sT.h, knorm_mm[qn * DC + DC - 1])
                if qn == 0:
                    sV.inc(v.tensor_reduce(nmx2[:], pR1[0:16, :], AX.X, ALU.max))
                else:
                    v.tensor_reduce(nmx3[:], pR1[0:16, :], AX.X, ALU.max)
                    sV.inc(v.tensor_tensor(nmx2[:], nmx2[:], nmx3[:], ALU.max))

    emit_proj_k(max(wk_g, bk_g))
    k_end = sT.n
    wq_g = load_w2(wqT, k_end)
    bq_g = load_bias2(0, 0, k_end)

    @blk.gpsimd
    def _(g):
        sSP.wait(g)
        sCC.inc(g.collective_compute(
            "AllGather", ALU.bypass, ins=[cc_inK], outs=[cc_outK],
            replica_groups=[[0, 1], [2, 3], [4, 5], [6, 7]]))

    def emit_proj_v(dma_gate):
        T0, S0 = sT.n, sS.n
        vevac = {tc: S0 + tc + 1 for tc in range(NTC)}
        vmm = {tc: T0 + tc + 1 for tc in range(NTC)}

        @blk.tensor
        def _(t):
            t.wait_ge(sW.h, dma_gate)
            for tc in range(NTC):
                psu = pAB if tc % 2 == 0 else pCD
                if tc < 2:
                    sV.wait(t)
                else:
                    t.wait_ge(sS.h, vevac[tc - 2])
                mm = None
                for half in range(2):
                    for k in range(DC):
                        t.matmul(psu[:, half * 512:half * 512 + 512],
                                 ln_sb[:, k, tc * 128:tc * 128 + 128],
                                 q_sb[:, k, half * 512:half * 512 + 512],
                                 start=(k == 0), stop=False)
                    mm = t.matmul(psu[:, half * 512:half * 512 + 512],
                                  ones_row[0:1, 0:128],
                                  b_sb[0:1, 1, half * 512:half * 512 + 512],
                                  start=False, stop=True)
                sT.inc(mm)

        @blk.scalar
        def _(s):
            for tc in range(NTC):
                psu = pAB if tc % 2 == 0 else pCD
                s.wait_ge(sT.h, vmm[tc])
                s.copy(v_sb[:, tc, 0:512], psu[:, 0:512])
                sS.inc(s.copy(v_sb[:, tc, 512:1024], psu[:, 512:1024]))

    emit_proj_v(max(wv_g, bv_g))

    @blk.sync
    def _(sy):
        sS.wait(sy)
        vdst = cc_inV.rearrange("r (two d) -> (r two) d", two=2) \
                     .rearrange("(tc p) d -> p tc d", p=128)
        sSP.inc(sy.dma_start(vdst, v_sb[:]), 16)

    @blk.gpsimd
    def _(g):
        sSP.wait(g)
        sCC.inc(g.collective_compute(
            "AllGather", ALU.bypass, ins=[cc_inV], outs=[cc_outV],
            replica_groups=[[0, 1], [2, 3], [4, 5], [6, 7]]))

    emit_proj_q(max(wq_g, bq_g))

    def emit_norms_q():
        T0, S0 = sT.n, sS.n
        V0n = sV.n
        qred = {qs: V0n + qs + 1 for qs in range(NQS)}
        sqq = {}
        nm_mm = {}
        i = 0
        for qs in range(NQS):
            for c in range(DC):
                sqq[(qs, c)] = S0 + i + 1
                nm_mm[(qs, c)] = T0 + i + 1
                i += 1

        @blk.scalar
        def _(s):
            sT.wait(s)
            first = True
            for qs in range(NQS):
                for c in range(DC):
                    if not first and not (qs == 0 and c == 1):
                        prev = (qs, c - 2) if c >= 2 else (qs - 1, DC - 2 + c)
                        s.wait_ge(sT.h, nm_mm[prev])
                    first = False
                    sS.inc(s.activation(xsq_sb[:, c % 2, :],
                                        q_sb[:, c, qs * 512:qs * 512 + 512], AF.Square))

        @blk.tensor
        def _(t):
            for qs in range(NQS):
                for c in range(DC):
                    t.wait_ge(sS.h, sqq[(qs, c)])
                    if qs == 0 and c == 0:
                        sV.wait(t)
                    elif c == 0:
                        t.wait_ge(sV.h, qred[qs - 1])
                    sT.inc(t.matmul(pR1[0:16, :], masks[:, c, :], xsq_sb[:, c % 2, :],
                                    start=(c == 0), stop=(c == DC - 1)))

        @blk.vector
        def _(v):
            for qs in range(NQS):
                v.wait_ge(sT.h, nm_mm[(qs, DC - 1)])
                if qs == 0:
                    sV.inc(v.tensor_reduce(nmx[:], pR1[0:16, :], AX.X, ALU.max))
                else:
                    v.tensor_reduce(nmx3[:], pR1[0:16, :], AX.X, ALU.max)
                    sV.inc(v.tensor_tensor(nmx[:], nmx[:], nmx3[:], ALU.max))

    emit_norms_q()

    @blk.vector
    def _(v):
        sV.wait(v)
        sV.inc(v.tensor_tensor(nmx[:], nmx[:], nmx2[:], ALU.mult))

    @blk.scalar
    def _(s):
        sV.wait(s)
        sS.inc(s.activation(nmx3[:], nmx[:], AF.Sqrt))

    @blk.vector
    def _(v):
        sS.wait(v)
        v.memset(t32a[:], 0.0)
        sV.inc(v.tensor_copy(t32a[0:16, 0:1], nmx3[:]))
        sV.wait(v)
        sV.inc(v.transpose(t32b[:], t32a[:]))
        sV.wait(v)
        sV.inc(v.tensor_copy(brow_f[:], t32b[0:1, 0:16]))

    @blk.scalar
    def _(s):
        sV.wait(s)
        sS.inc(s.copy(brow_b[:], t32b[0:1, 0:16]))

    @blk.tensor
    def _(t):
        sS.wait(t)
        t.matmul(pR2[:, 0:16], s16_row[:], brow_b[:], start=True, stop=False)
        sT.inc(t.matmul(pR2[:, 0:16], m44_row[:], ones_row[0:1, 0:16], start=False, stop=True))

    @blk.vector
    def _(v):
        sT.wait(v)
        sV.inc(v.tensor_reduce(bneg_sb[:], pR2[:, 0:16], AX.X, ALU.max, negate=True))

    bneg_ready = sV.n
    bes.close()

    load_w(woT)
    load_bias(3, 1)

    des = ExitStack()
    kp_sb = sb(des, "kp_sb", [128, 2, 2, T], BF16)
    vp_sb = sb(des, "vp_sb", [128, 2, 2, NTC, 128], BF16)
    expS = sb(des, "expS", [128, 2, 1024], BF16)
    rec_bf = sb(des, "rec_bf", [33, 512], BF16)
    binv_sb = sb(des, "binv_sb", [128, 512], BF16)
    avstage = sb(des, "avstage", [128, 2, 512], BF16)

    kv_cnt = {}
    pair_t_done = {}
    last_norm_v = [bneg_ready]
    av_spill = {}
    prev_pair_av = {}

    def emit_kv_load(p, prev_t):
        buf = p % 2

        @blk.sync
        def _(sy):
            sy.wait_ge(sCC.h, 1)
            if prev_t:
                sy.wait_ge(sT.h, prev_t)
            for s_ in range(2):
                sKV.inc(sy.dma_start(kp_sb[:, buf, s_, :], cc_outK[s_, 128 * p:128 * p + 128, :]), 16)
            sy.wait_ge(sCC.h, 2)
            for s_ in range(2):
                vsrc = cc_outV[s_].rearrange("r (two d) -> (r two) d", two=2)
                vsrc = vsrc[:, 128 * p:128 * p + 128].rearrange("(tc p2) d -> p2 tc d", p2=128)
                sKV.inc(sy.dma_start(vp_sb[:, buf, s_, :, :], vsrc), 16)
            kv_cnt[p] = sKV.n

    emit_kv_load(0, 0)
    emit_kv_load(1, 0)

    units_norm = {}
    prev_exp = {}

    for p in range(NP):
        buf = p % 2
        T0, S0, V0, SP0 = sT.n, sS.n, sV.n, sSP.n
        qk_cnt = {}
        av_cnt = {}
        binv_mm = {}
        n = T0
        for qs in range(NQS):
            for kc in range(32):
                n += 1
                qk_cnt[(qs, kc)] = n
                if kc >= 1:
                    n += 1
                    av_cnt[(qs, kc - 1)] = n
                if kc == 8 and qs > 0:
                    n += 1
                    binv_mm[qs - 1] = n
            n += 1
            av_cnt[(qs, 31)] = n
        n += 1
        binv_mm[NQS - 1] = n
        exp_cnt = {(qs, kc): S0 + qs * 32 + kc + 1 for qs in range(NQS) for kc in range(32)}
        rec_v = {qs: V0 + qs * 2 + 1 for qs in range(NQS)}
        norm_v = {qs: V0 + qs * 2 + 2 for qs in range(NQS)}
        spill_n = {qs: SP0 + 16 * (qs + 1) for qs in range(NQS)}
        for qs in range(NQS):
            units_norm[p * NQS + qs] = norm_v[qs]

        @blk.tensor
        def _(t, p=p, buf=buf, qk_cnt=qk_cnt, av_cnt=av_cnt, binv_mm=binv_mm,
              exp_cnt=exp_cnt, rec_v=rec_v, norm_v=norm_v, prevx=prev_exp):
            def emit_av(qs, kc):
                u = p * NQS + qs
                avp = (pR2, pR3)[u % 2]
                smp = (pR1, pR4)[u % 2]
                t.wait_ge(sS.h, exp_cnt[(qs, kc)])
                if kc == 0 and u >= 2:
                    t.wait_ge(sV.h, units_norm[u - 2])
                slot, j = kc // 16, kc % 16
                t.matmul(avp[0:64, :], vp_sb[:, buf, slot, j, 0:64],
                         expS[:, kc % 2, 0:512], start=(kc == 0), stop=(kc == 31))
                t.matmul(avp[64:128, :], vp_sb[:, buf, slot, j, 64:128],
                         expS[:, kc % 2, 512:1024], start=(kc == 0), stop=(kc == 31),
                         tile_position=(0, 64), skip_group_check=True)
                t.matmul(smp[0:1, :], ones_col[:], expS[:, kc % 2, 0:512],
                         start=(kc == 0), stop=(kc == 31))
                sT.inc(t.matmul(smp[32:33, :], ones_col[:], expS[:, kc % 2, 512:1024],
                                start=(kc == 0), stop=(kc == 31), tile_position=(0, 32),
                                skip_group_check=True))

            def emit_binv(qs):
                u = p * NQS + qs
                smp = (pR1, pR4)[u % 2]
                t.wait_ge(sV.h, rec_v[qs])
                t.matmul(smp[0:64, 0:512], ones33[0:1, :], rec_bf[0:1, :], start=True, stop=False)
                sT.inc(t.matmul(smp[64:128, 0:512], ones33[32:33, :], rec_bf[32:33, :],
                                start=True, stop=True, tile_position=(32, 64),
                                skip_group_check=True))

            for qs in range(NQS):
                q0 = qs * 512
                for kc in range(32):
                    sc = pAB if kc % 2 == 0 else pCD
                    slot, j = kc // 16, kc % 16
                    if kc == 0 and qs == 0:
                        t.wait_ge(sKV.h, kv_cnt[p])
                        if p == 0:
                            t.wait_ge(sV.h, last_norm_v[0])
                    if kc >= 2:
                        t.wait_ge(sS.h, exp_cnt[(qs, kc - 2)])
                    elif qs > 0:
                        t.wait_ge(sS.h, exp_cnt[(qs - 1, 30 + kc)])
                    elif p > 0:
                        s_prev = prevx.get(30 + kc)
                        if s_prev:
                            t.wait_ge(sS.h, s_prev)
                    t.matmul(sc[:, 0:512], kp_sb[0:64, buf, slot, 128 * j:128 * j + 128],
                             q_sb[0:64, p, q0:q0 + 512], start=True, stop=True)
                    sT.inc(t.matmul(sc[:, 512:1024], kp_sb[64:128, buf, slot, 128 * j:128 * j + 128],
                                    q_sb[64:128, p, q0:q0 + 512], start=True, stop=True,
                                    tile_position=(64, 0)))
                    if kc >= 1:
                        emit_av(qs, kc - 1)
                    if kc == 8 and qs > 0:
                        emit_binv(qs - 1)
                emit_av(qs, 31)
            emit_binv(NQS - 1)

        @blk.scalar
        def _(s, p=p, qk_cnt=qk_cnt, av_cnt=av_cnt, prev=prev_pair_av):
            for qs in range(NQS):
                for kc in range(32):
                    sc = pAB if kc % 2 == 0 else pCD
                    s.wait_ge(sT.h, qk_cnt[(qs, kc)])
                    if kc >= 2:
                        s.wait_ge(sT.h, av_cnt[(qs, kc - 2)])
                    elif qs > 0:
                        s.wait_ge(sT.h, av_cnt[(qs - 1, 30 + kc)])
                    elif p > 0:
                        s.wait_ge(sT.h, prev[30 + kc])
                    sS.inc(s.activation(expS[:, kc % 2, :], sc[:], AF.Exp,
                                        bias=bneg_sb[:], scale=0.125))

        @blk.vector
        def _(v, p=p, av_cnt=av_cnt, binv_mm=binv_mm):
            for qs in range(NQS):
                u = p * NQS + qs
                avp = (pR2, pR3)[u % 2]
                smp = (pR1, pR4)[u % 2]
                v.wait_ge(sT.h, av_cnt[(qs, 31)])
                with nc.allow_low_precision(reason="softmax normalizer feeds bf16 matmul"):
                    sV.inc(v.reciprocal(rec_bf[:], smp[0:33, :]))
                v.wait_ge(sT.h, binv_mm[qs])
                if u >= 2:
                    v.wait_ge(sSP.h, av_spill[u - 2] if qs < 2 else spill_n[qs - 2])
                v.tensor_copy(binv_sb[:], smp[:, 0:512])
                sV.inc(v.tensor_tensor(avstage[:, qs % 2, :], avp[:], binv_sb[:], ALU.mult))

        @blk.sync
        def _(sy, p=p, norm_v=norm_v):
            for qs in range(NQS):
                sy.wait_ge(sV.h, norm_v[qs])
                av_spill[p * NQS + qs] = sSP.inc(sy.dma_start(
                    attnv_dram[128 * p:128 * p + 128, qs * 512:qs * 512 + 512],
                    avstage[:, qs % 2, :]), 16)

        prev_pair_av = {30: av_cnt[(NQS - 1, 30)], 31: av_cnt[(NQS - 1, 31)]}
        prev_exp = {30: exp_cnt[(NQS - 1, 30)], 31: exp_cnt[(NQS - 1, 31)]}
        last_norm_v[0] = norm_v[NQS - 1]
        assert sT.n == T0 + NQS * 65
        assert sS.n == S0 + NQS * 32
        assert sV.n == V0 + NQS * 2
        pair_t_done[p] = sT.n
        if p + 2 < NP:
            emit_kv_load(p + 2, pair_t_done[p])

    des.close()
    wes = ExitStack()
    av_ring = sb(wes, "av_ring", [128, 2, DC, 512], BF16)

    def emit_wo():
        T0, V0, KV0 = sT.n, sV.n, sKV.n
        wo_mm = {i: T0 + i + 1 for i in range(NQS * DC)}
        wo_ev = {i: V0 + i + 1 for i in range(NQS * DC)}
        av_ld = {qn: KV0 + 16 * (qn + 1) for qn in range(NQS)}

        @blk.sync
        def _(sy):
            for qn in range(NQS):
                if qn == 0:
                    sy.wait_ge(sSP.h, av_spill[NP * NQS - 1])
                if qn >= 2:
                    sy.wait_ge(sT.h, wo_mm[(qn - 2) * DC + DC - 1])
                src_ap = attnv_dram[:, qn * 512:qn * 512 + 512].rearrange("(k p) t -> p k t", p=128)
                sKV.inc(sy.dma_start(av_ring[:, qn % 2, :, :], src_ap), 16)

        @blk.tensor
        def _(t):
            sW.wait(t)
            for qn in range(NQS):
                t.wait_ge(sKV.h, av_ld[qn])
                for m in range(DC):
                    i = qn * DC + m
                    psu = pAB if i % 2 == 0 else pCD
                    if i < 2:
                        t.wait_ge(sV.h, last_norm_v[0])
                    else:
                        t.wait_ge(sV.h, wo_ev[i - 2])
                    for k in range(DC):
                        t.matmul(psu[:, 0:512], wq_sb[:, k, 128 * m:128 * m + 128],
                                 av_ring[:, qn % 2, k, :], start=(k == 0), stop=False)
                    sT.inc(t.matmul(psu[:, 0:512], b_sb[0:1, 1, 128 * m:128 * m + 128],
                                    ones_row[0:1, :], start=False, stop=True))

        @blk.vector
        def _(v):
            for qn in range(NQS):
                for m in range(DC):
                    i = qn * DC + m
                    psu = pAB if i % 2 == 0 else pCD
                    v.wait_ge(sT.h, wo_mm[i])
                    xsl = x_sb[:, m, qn * 512:qn * 512 + 512]
                    sV.inc(v.tensor_tensor(xsl, xsl, psu[:, 0:512], ALU.add))

        return [wo_ev[qq * DC + DC - 1] for qq in range(NQS)]

    wo_xv = emit_wo()
    wes.close()
    aes.close()

    e2_half = emit_ffn_era("2", xver=wo_xv)

    @blk.sync
    def _(sy):
        dst = out_t.rearrange("(c p) t -> p c t", p=128)
        for hh in range(2):
            sy.wait_ge(sV.h, e2_half[hh])
            sOUT.inc(sy.dma_start(dst[:, :, hh * 1024:hh * 1024 + 1024],
                                  x_sb[:, :, hh * 1024:hh * 1024 + 1024]), 16)
        sOUT.wait(sy)

    es.close()
    return nc


_NC_CACHE = None


def kernel(L0_attn_x, L0_attn_Wq, L0_attn_bq, L0_attn_Wk, L0_attn_bk,
           L0_attn_Wv, L0_attn_bv, L0_attn_Wo, L0_attn_bo,
           L0_ffn_W_fc, L0_ffn_W_proj):
    global _NC_CACHE
    x = np.asarray(L0_attn_x, dtype=np.float32)
    B, S, _ = x.shape
    assert (B, S) == (4, 4096)

    wqT = np.ascontiguousarray(np.asarray(L0_attn_Wq, np.float32).T.reshape(DC, 128, D).transpose(1, 0, 2))
    wkT = np.ascontiguousarray(np.asarray(L0_attn_Wk, np.float32).T.reshape(DC, 128, D).transpose(1, 0, 2))
    wvT = np.ascontiguousarray(np.asarray(L0_attn_Wv, np.float32).T.reshape(DC, 128, D).transpose(1, 0, 2))
    woT = np.ascontiguousarray(np.asarray(L0_attn_Wo, np.float32).T.reshape(DC, 128, D).transpose(1, 0, 2))
    wfcT = np.asarray(L0_ffn_W_fc, np.float32).T
    wfcb = np.ascontiguousarray(wfcT.reshape(DC, 128, FC, 128).transpose(2, 1, 0, 3).reshape(FC // 2, 2, 128, DC, 128).transpose(0, 2, 1, 3, 4))
    wprojT = np.asarray(L0_ffn_W_proj, np.float32).T
    wpb = np.ascontiguousarray(wprojT.reshape(2, 16, 128, DC, 128).transpose(3, 0, 2, 1, 4))
    brow = np.ascontiguousarray(np.concatenate(
        [np.asarray(b, np.float32).reshape(-1)
         for b in (L0_attn_bq, L0_attn_bk, L0_attn_bv, L0_attn_bo)])[None, :])
    cstv = np.full((128, 1), EPS, np.float32)

    if _NC_CACHE is None:
        _NC_CACHE = build_program()
    nc = _NC_CACHE

    in_maps = []
    for c in range(8):
        b, h = c // 2, c % 2
        xs = np.ascontiguousarray(x[b, h * T:(h + 1) * T, :].T)
        in_maps.append(dict(x_t=xs, wqT=wqT, wkT=wkT, wvT=wvT, woT=woT,
                            wfcb=wfcb, wpb=wpb, brow=brow, cst=cstv))

    res = run_bass_kernel_spmd(nc, in_maps, core_ids=list(range(8)))
    out = np.empty((B, S, D), np.float32)
    for c in range(8):
        b, h = c // 2, c % 2
        out[b, h * T:(h + 1) * T, :] = res.results[c]["out_t"].T
    return out
